# version 15
# baseline (speedup 1.0000x reference)
import contextlib
import sys
import numpy as np
import concourse.bass as bass
import concourse.mybir as mybir
from concourse.bass_utils import run_bass_kernel_spmd

F32 = mybir.dt.float32
BF16 = mybir.dt.bfloat16
I32 = mybir.dt.int32
AF = mybir.ActivationFunctionType
ALU = mybir.AluOpType
AX = mybir.AxisListType

B, L, D = 4, 4096, 1024
NH, DQH, DVH = 4, 128, 256
C = 128
NCH = L // C
OWN0 = NCH // 2
IF = 2816
NLEV = 7
USE_COLLECTIVE = False
SERIALIZE_TE = True
SERIALIZE_ALL = True
SERIALIZE_DMA = True
DBG = False

QROWS = (IF * D * 2 + D * IF) // 1024
QSH = QROWS // 8 if USE_COLLECTIVE else QROWS

ENGS = ("te", "dv", "ac", "gp", "sy")
ENG_SEM = {"te": "TE", "dv": "DV", "ac": "AC", "gp": "GP", "sy": "SY"}
DMA_Q = {"sy": "DS", "ac": "DA", "gp": "DG"}
NROT = 8
DMA_SEMS = {f"{q}{i}" for q in DMA_Q.values() for i in range(NROT)}


class Buf:
    __slots__ = ("w", "r", "name")

    def __init__(self, name=""):
        self.w = []
        self.r = []
        self.name = name


class Op:
    __slots__ = ("eng", "fn", "waits", "sem", "val", "tag", "need_inc")

    def __init__(self, eng, fn, waits, sem, val, tag):
        self.eng, self.fn, self.waits, self.sem, self.val, self.tag = \
            eng, fn, waits, sem, val, tag
        self.need_inc = False


class Sched:

    def __init__(self):
        self.count = {s: 0 for s in list(ENG_SEM.values()) + list(DMA_SEMS)}
        self.eng_pos = {e: 0 for e in ENGS}
        self.dma_uses = {q: 0 for q in DMA_Q.values()}
        self.ops = []
        self.by_sem_val = {}
        self.fence_map = {}

    def emit(self, eng, fn, reads=(), writes=(), dma=False, tag=""):
        deps = []
        for b in reads:
            deps.extend(b.w)
        for b in writes:
            deps.extend(b.w)
            deps.extend(b.r)
        dmap = dict(self.fence_map)
        for s, v, pos in deps:
            if v > dmap.get(s, -1):
                dmap[s] = v
        if dma and SERIALIZE_DMA:
            for s, v, pos in getattr(self, "dmatok", []):
                if v > dmap.get(s, -1):
                    dmap[s] = v
        if dma:
            q = DMA_Q[eng]
            k = self.dma_uses[q]
            self.dma_uses[q] = k + 1
            sem = f"{q}{k % NROT}"
            prev = self.count[sem]
            if prev > 0 and dmap.get(sem, -1) < prev:
                dmap[sem] = prev
            self.count[sem] += 16
        else:
            sem = ENG_SEM[eng]
            self.count[sem] += 1
        val = self.count[sem]
        if dma and SERIALIZE_DMA:
            self.dmatok = [(sem, val, 0)]
        waits = sorted(dmap.items())
        op = Op(eng, fn, waits, sem, val, tag)
        self.ops.append(op)
        self.by_sem_val[(sem, val)] = op
        self.eng_pos[eng] += 1
        dep = (sem, val, self.eng_pos[eng])
        for b in reads:
            b.r.append(dep)
        for b in writes:
            b.w = [dep]
            b.r = []
        return dep

    def barrier_dep(self):
        return [(s, v, 10 ** 9) for s, v in self.count.items() if v > 0]

    def fence(self):
        self.fence_map = {s: v for s, v in self.count.items() if v > 0}

    def finalize(self):
        engs = ENGS if SERIALIZE_ALL else (("te",) if SERIALIZE_TE else ())
        for eng in engs:
            sem = ENG_SEM[eng]
            prev = None
            for op in self.ops:
                if op.eng == eng and op.sem == sem:
                    if prev is not None:
                        if dict(op.waits).get(sem, -1) < prev:
                            w = dict(op.waits)
                            w[sem] = prev
                            op.waits = sorted(w.items())
                    prev = op.val
        waited = {}
        for op in self.ops:
            for s, v in op.waits:
                waited.setdefault(s, set()).add(v)
        for op in self.ops:
            ws = waited.get(op.sem)
            if ws and any(w >= op.val for w in ws):
                op.need_inc = True
        per = {e: [] for e in ENGS}
        for op in self.ops:
            per[op.eng].append(op)
        idx = {e: 0 for e in ENGS}
        cur = {s: 0 for s in self.count}
        done, total = 0, len(self.ops)
        while done < total:
            prog = False
            for e in ENGS:
                while idx[e] < len(per[e]):
                    op = per[e][idx[e]]
                    if all(cur[s] >= v for s, v in op.waits):
                        cur[op.sem] = op.val
                        idx[e] += 1
                        done += 1
                        prog = True
                    else:
                        break
            if not prog:
                stuck = []
                for e in ENGS:
                    if idx[e] < len(per[e]):
                        op = per[e][idx[e]]
                        bad = [(s, v, cur[s]) for s, v in op.waits
                               if cur[s] < v]
                        stuck.append((e, op.tag, bad[:3]))
                raise RuntimeError(f"deadlock: {stuck[:4]}")

    def flush(self, block, sems):
        self.finalize()
        per = {e: [] for e in ENGS}
        for op in self.ops:
            per[op.eng].append(op)

        def mk(lst, ename):
            sat = {}

            def f(e):
                for op in lst:
                    for s, v in op.waits:
                        if sat.get(s, -1) >= v:
                            continue
                        e.wait_ge(sems[s], v)
                        sat[s] = v
                    ins = op.fn(e)
                    if op.need_inc:
                        ins.then_inc(sems[op.sem],
                                     16 if op.sem in DMA_SEMS else 1)
            return f
        if per["te"]:
            block.tensor(mk(per["te"], "te"))
        if per["dv"]:
            block.vector(mk(per["dv"], "dv"))
        if per["ac"]:
            block.scalar(mk(per["ac"], "ac"))
        if per["gp"]:
            block.gpsimd(mk(per["gp"], "gp"))
        if per["sy"]:
            block.sync(mk(per["sy"], "sy"))


def build_program():
    nc = bass.Bass("TRN2", target_bir_lowering=False, debug=False)
    S = Sched()

    xseq = nc.dram_tensor("xseq", [L, D], F32, kind="ExternalInput").ap()
    w_in = {}
    for nm, shp in [("conv_w", [D, 4]), ("Wq", [D, 512]), ("Wk", [D, 512]),
                    ("Wv", [D, D]), ("Wb", [D, 4]), ("Wg", [D, D]),
                    ("norm_w", [DVH]), ("Wo", [D, D])]:
        w_in[nm] = nc.dram_tensor(nm, shp, F32, kind="ExternalInput").ap()
    qw_in = nc.dram_tensor("qw_in", [QSH, 1024], F32,
                           kind="ExternalInput").ap()
    consts = nc.dram_tensor("consts", [128, 4, 512], F32,
                            kind="ExternalInput").ap()
    out_h = nc.dram_tensor("out", [2048, D], F32, kind="ExternalOutput").ap()
    if DBG:
        dbg_h = nc.dram_tensor("dbg", [16, 128, 1536], BF16,
                               kind="ExternalOutput").ap()
        b_dbg = Buf("dbg")
        dbg_stk = contextlib.ExitStack()
        dbg_stage = dbg_stk.enter_context(
            nc.sbuf_tensor("dbg_st", [128, 1536], BF16))
        b_dbgst = Buf("dbgst")

        def dbg_dump(slot, ap, rbufs, eng="sy"):
            w = ap.shape[-1]
            S.emit("dv", lambda e: e.tensor_copy(
                out=dbg_stage[:, 0:w], in_=ap),
                reads=list(rbufs), writes=[b_dbgst], tag=f"dbgc{slot}")
            S.emit("sy", lambda e: e.dma_start(
                out=dbg_h[slot, :, 0:w], in_=dbg_stage[:, 0:w]),
                reads=[b_dbgst], writes=[b_dbg], dma=True,
                tag=f"dbg{slot}")
    else:
        def dbg_dump(slot, ap, rbufs, eng="sy"):
            pass

    v_h = nc.dram_tensor("v_sp", [NCH, 128, D], BF16).ap()
    g_h = nc.dram_tensor("g_sp", [16, 128, D], BF16).ap()
    if USE_COLLECTIVE:
        qw_my = nc.dram_tensor("qw_my", [QSH, 1024], BF16).ap()
        qw_all = nc.dram_tensor("qw_all", [QROWS, 1024], BF16,
                                addr_space="Local").ap()
    else:
        qw_my = nc.dram_tensor("qw_my", [QROWS, 1024], BF16).ap()
        qw_all = qw_my

    outer = contextlib.ExitStack()
    sems = {}
    for s in list(ENG_SEM.values()) + list(DMA_SEMS):
        sems[s] = outer.enter_context(nc.semaphore(s))

    def sbt(stk, name, shape, dt):
        return stk.enter_context(nc.sbuf_tensor(name, shape, dt))

    def pst(stk, name, shape, dt):
        return stk.enter_context(nc.psum_tensor(name, shape, dt))

    maskSL = sbt(outer, "maskSL", [128, 512], BF16)
    maskSU = sbt(outer, "maskSU", [128, 512], BF16)
    maskLE = sbt(outer, "maskLE", [128, 512], BF16)
    i_sb = sbt(outer, "i_sb", [128, 128], BF16)
    convw = sbt(outer, "convw", [128, 8, 4], F32)
    nwc = sbt(outer, "nwc", [128, 2], F32)
    stats = sbt(outer, "stats", [128, NCH, 2], F32)
    stQ = contextlib.ExitStack()
    QW = 512
    q_m = sbt(stQ, "q_m", [128, 32], F32)
    q_s = sbt(stQ, "q_s", [128, 32], F32)
    q_a = sbt(stQ, "q_a", [128, 32], F32)
    q_h = sbt(stQ, "q_h", [128, 32], F32)
    q_u = sbt(stQ, "q_u", [128, QW], F32)
    q_au = sbt(stQ, "q_au", [128, QW], BF16)
    q_f1 = sbt(stQ, "q_f1", [128, QW], BF16)
    q_f = sbt(stQ, "q_f", [128, QW], BF16)
    q_d = sbt(stQ, "q_d", [128, QW], BF16)
    q_w = sbt(stQ, "q_w", [128, QW], F32)
    q_ri = sbt(stQ, "q_ri", [128, QW], I32)
    q_rb = sbt(stQ, "q_rb", [128, QW], BF16)
    (b_qm, b_qs, b_qa, b_qh, b_qu, b_qau, b_qf1, b_qf, b_qd, b_qw,
     b_qri, b_qrb) = (Buf(n) for n in
                      ("qm", "qs", "qa", "qh", "qu", "qau", "qf1", "qf",
                       "qd", "qw", "qri", "qrb"))
    st12 = contextlib.ExitStack()
    beta_a = sbt(st12, "beta_a", [128, NCH, 4], F32)
    s2k_a = sbt(st12, "s2k_a", [128, NCH, 4], F32)
    s2q_a = sbt(st12, "s2q_a", [128, 16, 4], F32)
    k_raw = sbt(st12, "k_raw", [128, NCH, 512], BF16)
    q_raw = sbt(st12, "q_raw", [128, 16, 512], BF16)
    g_s = sbt(st12, "g_s", [128, 16, D], BF16)
    S_bf = sbt(st12, "S_bf", [128, NH, 256], BF16)

    b_maskSL, b_maskSU, b_maskLE, b_isb, b_convw, b_nwc = \
        (Buf(n) for n in ("mSL", "mSU", "mLE", "isb", "cw", "nwc"))
    b_stats = [Buf(f"st{t}") for t in range(NCH)]
    b_beta = [Buf(f"be{c}") for c in range(NCH)]
    b_s2k = [Buf(f"sk{c}") for c in range(NCH)]
    b_s2q = [Buf(f"sq{c}") for c in range(16)]
    b_kraw = [Buf(f"kr{c}") for c in range(NCH)]
    b_qraw = [Buf(f"qr{c}") for c in range(16)]
    b_gs = [Buf(f"gs{c}") for c in range(16)]
    b_Sbf = [Buf(f"Sb{h}") for h in range(NH)]
    b_vh = [Buf(f"vh{c}") for c in range(NCH)]
    b_gh = [Buf(f"gh{c}") for c in range(16)]
    b_qwmy = Buf("qwmy")
    b_qwall = Buf("qwall")

    with nc.Block() as block:
        S.emit("gp", lambda e: e.dma_start(out=maskSL[:], in_=consts[:, 0, :]),
               writes=[b_maskSL], dma=True, tag="m0")
        S.emit("gp", lambda e: e.dma_start(out=maskSU[:], in_=consts[:, 1, :]),
               writes=[b_maskSU], dma=True, tag="m1")
        S.emit("gp", lambda e: e.dma_start(out=maskLE[:], in_=consts[:, 2, :]),
               writes=[b_maskLE], dma=True, tag="m2")
        S.emit("gp", lambda e: e.dma_start(out=i_sb[:],
                                           in_=consts[:, 3, 0:128]),
               writes=[b_isb], dma=True, tag="m3")
        S.emit("gp", lambda e: e.memset(S_bf[:], 0.0),
               writes=list(b_Sbf), tag="S0")
        S.emit("sy", lambda e: e.dma_start(
            out=convw[:], in_=w_in["conv_w"].rearrange("(a p) k -> p a k",
                                                       p=128)),
            writes=[b_convw], dma=True, tag="ldcw")
        S.emit("sy", lambda e: e.dma_start(
            out=nwc[:], in_=w_in["norm_w"].rearrange("(a p) -> p a", p=128),
            allow_slow_non_contiguous=True),
            writes=[b_nwc], dma=True, tag="ldnw")


        stW = contextlib.ExitStack()
        Wcat = sbt(stW, "Wcat", [128, 8, 3076], BF16)
        wq_x = [sbt(stW, f"wq_x{i}", [128, 1024], F32) for i in range(2)]
        wq_o = sbt(stW, "wq_o", [128, 1024], BF16)
        b_wqx = [Buf(f"wqx{i}") for i in range(2)]
        b_wqo = Buf("wqo")
        b_wcat = Buf("wcat")
        for nm, c0, c1 in [("Wq", 0, 512), ("Wk", 512, 1024),
                           ("Wv", 1024, 2048), ("Wg", 2048, 3072),
                           ("Wb", 3072, 3076)]:
            S.emit("gp", lambda e, nm=nm, c0=c0, c1=c1: e.dma_start(
                out=Wcat[:, :, c0:c1],
                in_=w_in[nm].rearrange("(a p) n -> p a n", p=128)),
                writes=[b_wcat], dma=True, tag=f"ld{nm}")

        stP = contextlib.ExitStack()
        xp = [sbt(stP, f"xp{i}", [128, D], F32) for i in range(2)]
        b_xp = [Buf(f"xp{i}") for i in range(2)]
        bns = sbt(stP, "bns", [128, 12], F32)
        bna = sbt(stP, "bna", [128, NCH, 2], F32)
        b_bns = Buf("bns")
        b_bna = [Buf(f"bna{t}") for t in range(NCH)]
        for t in range(NCH):
            pi = t % 2
            S.emit("sy", lambda e, t=t, pi=pi: e.dma_start(
                out=xp[pi][:], in_=xseq[t * 128:(t + 1) * 128, :]),
                writes=[b_xp[pi]], dma=True, tag=f"px{t}")
            S.emit("dv", lambda e, pi=pi: e.bn_stats(
                out=bns[:, 0:6], in_=xp[pi][:, 0:512]),
                reads=[b_xp[pi]], writes=[b_bns], tag=f"pb{t}")
            S.emit("dv", lambda e, pi=pi: e.bn_stats(
                out=bns[:, 6:12], in_=xp[pi][:, 512:1024]),
                reads=[b_xp[pi]], writes=[b_bns], tag=f"pb2{t}")
            S.emit("dv", lambda e, t=t: e.bn_aggr(out=bna[:, t, :],
                                                  in_=bns[:]),
                   reads=[b_bns], writes=[b_bna[t]], tag=f"pa{t}")
        b_all_bna = list(b_bna)
        S.emit("dv", lambda e: e.tensor_scalar(
            out=stats[:, :, 1], in0=bna[:, :, 1], scalar1=1e-5, scalar2=None,
            op0=ALU.add), reads=b_all_bna, writes=[b_stats[0]], tag="pe")
        S.emit("dv", lambda e: e.reciprocal(
            out=stats[:, :, 1], in_=stats[:, :, 1]),
            reads=[b_stats[0]], writes=[b_stats[0]], tag="pr")
        S.emit("ac", lambda e: e.activation(
            out=stats[:, :, 1], in_=stats[:, :, 1], func=AF.Sqrt),
            reads=[b_stats[0]], writes=[b_stats[0]], tag="ps")
        S.emit("dv", lambda e: e.scalar_tensor_tensor(
            out=stats[:, :, 0], in0=bna[:, :, 0], scalar=-1.0,
            in1=stats[:, :, 1], op0=ALU.mult, op1=ALU.mult),
            reads=b_all_bna + [b_stats[0]], writes=[b_stats[1]], tag="pm")
        stP.close()
        S.fence()


        def quant(xap, bx, n, outap, bout, P=128):
            for p0 in range(0, n, QW):
                quant_piece(xap[:, p0:min(n, p0 + QW)], bx,
                            min(n, p0 + QW) - p0,
                            outap[:, p0:min(n, p0 + QW)], bout, P=P)

        def quant_piece(xap, bx, n, outap, bout, P=128):
            nb = n // 16
            x3 = xap.rearrange("p (b k) -> p b k", k=16)
            S.emit("dv", lambda e: e.reduce_max(
                out=q_m[:P, 0:nb], in_=x3, axis=AX.X,
                apply_absolute_value=True),
                reads=[bx], writes=[b_qm], tag="q_red")
            S.emit("dv", lambda e: e.tensor_scalar(
                out=q_s[:P, 0:nb], in0=q_m[:P, 0:nb], scalar1=1.0 / 6.0,
                scalar2=1e-12, op0=ALU.mult, op1=ALU.max),
                reads=[b_qm], writes=[b_qs], tag="q_s")
            S.emit("dv", lambda e: e.reciprocal(
                out=q_a[:P, 0:nb], in_=q_s[:P, 0:nb]),
                reads=[b_qs], writes=[b_qa], tag="q_a")
            S.emit("dv", lambda e: e.tensor_scalar(
                out=q_a[:P, 0:nb], in0=q_a[:P, 0:nb], scalar1=2.0,
                scalar2=None, op0=ALU.mult),
                reads=[b_qa], writes=[b_qa], tag="q_a2")
            S.emit("dv", lambda e: e.tensor_scalar(
                out=q_h[:P, 0:nb], in0=q_s[:P, 0:nb], scalar1=0.5,
                scalar2=None, op0=ALU.mult),
                reads=[b_qs], writes=[b_qh], tag="q_h")
            abc = q_a[:P, 0:nb].unsqueeze(2).broadcast_to([P, nb, 16])
            S.emit("dv", lambda e: e.tensor_tensor(
                out=q_u[:P, 0:n].rearrange("p (b k) -> p b k", k=16),
                in0=x3, in1=abc, op=ALU.mult),
                reads=[bx, b_qa], writes=[b_qu], tag="q_u")
            S.emit("ac", lambda e: e.activation(
                out=q_au[:P, 0:n], in_=q_u[:P, 0:n], func=AF.Abs),
                reads=[b_qu], writes=[b_qau], tag="q_au")
            S.emit("dv", lambda e: e.tensor_scalar(
                out=q_f1[:P, 0:n], in0=q_au[:P, 0:n], scalar1=4.0,
                scalar2=1.0, op0=ALU.is_gt, op1=ALU.add),
                reads=[b_qau], writes=[b_qf1], tag="q_f1")
            S.emit("dv", lambda e: e.tensor_scalar(
                out=q_f[:P, 0:n], in0=q_au[:P, 0:n], scalar1=8.0,
                scalar2=1.0, op0=ALU.is_gt, op1=ALU.add),
                reads=[b_qau], writes=[b_qf], tag="q_f2")
            S.emit("dv", lambda e: e.tensor_tensor(
                out=q_f[:P, 0:n], in0=q_f[:P, 0:n], in1=q_f1[:P, 0:n],
                op=ALU.mult),
                reads=[b_qf, b_qf1], writes=[b_qf], tag="q_f")
            S.emit("dv", lambda e: e.tensor_scalar(
                out=q_d[:P, 0:n], in0=q_f1[:P, 0:n], scalar1=-0.5,
                scalar2=1.5, op0=ALU.mult, op1=ALU.add),
                reads=[b_qf1], writes=[b_qd], tag="q_d1")
            S.emit("dv", lambda e: e.tensor_tensor(
                out=q_w[:P, 0:n], in0=q_u[:P, 0:n], in1=q_d[:P, 0:n],
                op=ALU.mult),
                reads=[b_qu, b_qd], writes=[b_qw], tag="q_w1")
            S.emit("dv", lambda e: e.tensor_scalar(
                out=q_d[:P, 0:n], in0=q_au[:P, 0:n], scalar1=8.0,
                scalar2=-0.5, op0=ALU.is_gt, op1=ALU.mult),
                reads=[b_qau], writes=[b_qd], tag="q_d2a")
            S.emit("dv", lambda e: e.tensor_scalar(
                out=q_d[:P, 0:n], in0=q_d[:P, 0:n], scalar1=1.0,
                scalar2=None, op0=ALU.add),
                reads=[b_qd], writes=[b_qd], tag="q_d2b")
            S.emit("dv", lambda e: e.tensor_tensor(
                out=q_w[:P, 0:n], in0=q_w[:P, 0:n], in1=q_d[:P, 0:n],
                op=ALU.mult),
                reads=[b_qw, b_qd], writes=[b_qw], tag="q_w2")
            S.emit("dv", lambda e: e.tensor_scalar(
                out=q_u[:P, 0:n], in0=q_w[:P, 0:n], scalar1=64.5,
                scalar2=None, op0=ALU.add),
                reads=[b_qw], writes=[b_qu], tag="q_yv")
            S.emit("dv", lambda e: e.tensor_copy(
                out=q_ri[:P, 0:n], in_=q_u[:P, 0:n]),
                reads=[b_qu], writes=[b_qri], tag="q_ri")
            S.emit("ac", lambda e: e.copy(
                out=q_w[:P, 0:n], in_=q_ri[:P, 0:n]),
                reads=[b_qri], writes=[b_qw], tag="q_rf")
            S.emit("dv", lambda e: e.tensor_tensor(
                out=q_d[:P, 0:n], in0=q_w[:P, 0:n], in1=q_u[:P, 0:n],
                op=ALU.is_gt),
                reads=[b_qw, b_qu], writes=[b_qd], tag="q_c")
            S.emit("dv", lambda e: e.scalar_tensor_tensor(
                out=q_rb[:P, 0:n], in0=q_w[:P, 0:n], scalar=-64.0,
                in1=q_d[:P, 0:n], op0=ALU.add, op1=ALU.subtract),
                reads=[b_qw, b_qd], writes=[b_qrb], tag="q_fix")
            S.emit("dv", lambda e: e.tensor_tensor(
                out=q_rb[:P, 0:n], in0=q_rb[:P, 0:n], in1=q_f[:P, 0:n],
                op=ALU.mult),
                reads=[b_qrb, b_qf], writes=[b_qrb], tag="q_z")
            hbc = q_h[:P, 0:nb].unsqueeze(2).broadcast_to([P, nb, 16])
            S.emit("dv", lambda e: e.tensor_tensor(
                out=outap.rearrange("p (b k) -> p b k", k=16),
                in0=q_rb[:P, 0:n].rearrange("p (b k) -> p b k", k=16),
                in1=hbc, op=ALU.mult),
                reads=[b_qrb, b_qh], writes=[bout], tag="q_y")

        NJOB = (QSH + 127) // 128

        def wq_job(j):
            pi = j % 2
            r0 = j * 128
            P = min(128, QSH - r0)
            S.emit("sy", lambda e: e.dma_start(
                out=wq_x[pi][:P, :], in_=qw_in[r0:r0 + P, :]),
                writes=[b_wqx[pi]], dma=True, tag=f"wl{j}")
            quant(wq_x[pi][:P, :], b_wqx[pi], 1024, wq_o[:P, :], b_wqo, P=P)
            S.emit("sy", lambda e: e.dma_start(
                out=qw_my[r0:r0 + P, :], in_=wq_o[:P, :]),
                reads=[b_wqo], writes=[b_qwmy], dma=True, tag=f"ws{j}")

        st1 = contextlib.ExitStack()
        x_t = [sbt(st1, f"x_t{i}", [128, D], F32) for i in range(2)]
        xn_t = sbt(st1, "xn_t", [128, D], BF16)
        xn_f = [sbt(st1, f"xn_f{i}", [128, 8, 144], BF16) for i in range(2)]
        h_f = sbt(st1, "h_f", [128, 8, 128], BF16)
        h_acc = sbt(st1, "h_acc", [128, 8, 128], BF16)
        vst = sbt(st1, "vst", [128, D], BF16)
        ksq = sbt(st1, "ksq", [128, 512], BF16)
        bsc = sbt(st1, "bsc", [128, 4], F32)
        b_xt = [Buf(f"x{i}") for i in range(2)]
        b_xnt = Buf("xnt")
        b_xnf = [[Buf(f"xnf{i}_{d}") for d in range(9)] for i in range(2)]
        b_hf = Buf("hf")
        b_hacc = Buf("hacc")
        b_vst = [Buf("vst0"), Buf("vst1")]
        b_ksq = Buf("ksq")
        b_bsc = Buf("bsc")

        ps1 = [pst(st1, f"ps1_{i}", [128, 512], F32) for i in range(7)]
        b_ps1 = [Buf(f"p1{i}") for i in range(7)]

        S.emit("gp", lambda e: e.memset(xn_f[0][:, :, 13:16], 0.0),
               writes=[b_xnf[0][8]], tag="halo0")

        def p1_tile(t):
            pi = t % 2
            own = t >= OWN0
            S.emit("sy", lambda e: e.dma_start(
                out=x_t[pi][:], in_=xseq[t * 128:(t + 1) * 128, :]),
                writes=[b_xt[pi]], dma=True, tag=f"lx{t}")
            S.emit("ac", lambda e: e.activation(
                out=xn_t[:], in_=x_t[pi][:], func=AF.Identity,
                scale=stats[:, t, 1:2], bias=stats[:, t, 0:1]),
                reads=[b_xt[pi], b_stats[0], b_stats[1]], writes=[b_xnt],
                tag=f"ln{t}")
            for dc in range(8):
                eng = "ac" if dc % 2 == 0 else "sy"
                S.emit(eng, lambda e, dc=dc: e.dma_start(
                    out=xn_f[pi][:, dc, 16:144],
                    in_=xn_t[:, dc * 128:(dc + 1) * 128], transpose=True),
                    reads=[b_xnt], writes=[b_xnf[pi][dc]], dma=True,
                    tag=f"tx{t}_{dc}")
            for i in range(4):
                for dc in range(8):
                    if i == 0:
                        S.emit("dv", lambda e, dc=dc: e.tensor_scalar(
                            out=h_acc[:, dc, :], in0=xn_f[pi][:, dc, 13:141],
                            scalar1=convw[:, dc, 0:1], scalar2=None,
                            op0=ALU.mult),
                            reads=[b_xnf[pi][dc], b_xnf[pi][8], b_convw],
                            writes=[b_hacc],
                            tag=f"cv{t}_{dc}_0")
                    else:
                        S.emit("dv", lambda e, dc=dc, i=i:
                               e.scalar_tensor_tensor(
                                   out=h_acc[:, dc, :],
                                   in0=xn_f[pi][:, dc, 13 + i:141 + i],
                                   scalar=convw[:, dc, i:i + 1],
                                   in1=h_acc[:, dc, :],
                                   op0=ALU.mult, op1=ALU.add),
                               reads=[b_xnf[pi][dc], b_xnf[pi][8], b_convw,
                                      b_hacc],
                               writes=[b_hacc], tag=f"cv{t}_{dc}_{i}")
            S.emit("ac", lambda e: e.activation(
                out=h_f[:].rearrange("p a n -> p (a n)"),
                in_=h_acc[:].rearrange("p a n -> p (a n)"), func=AF.Silu),
                reads=[b_hacc], writes=[b_hf], tag=f"si{t}")

            if t + 1 < NCH:
                S.emit("dv", lambda e: e.tensor_copy(
                    out=xn_f[1 - pi][:, :, 13:16],
                    in_=xn_f[pi][:, :, 141:144]),
                    reads=list(b_xnf[pi][0:8]), writes=[b_xnf[1 - pi][8]],
                    tag=f"hl{t}")
            pieces = [(512, 1024, 1), (1024, 1536, 2), (1536, 2048, 3),
                      (3072, 3076, 6)]
            if own:
                pieces = [(0, 512, 0)] + pieces + \
                         [(2048, 2560, 4), (2560, 3072, 5)]
            for dc in range(8):
                for (c0, c1, pb) in pieces:
                    S.emit("te", lambda e, dc=dc, c0=c0, c1=c1, pb=pb:
                           e.matmul(ps1[pb][:, 0:c1 - c0], h_f[:, dc, :],
                                    Wcat[:, dc, c0:c1],
                                    start=(dc == 0), stop=(dc == 7)),
                           reads=[b_hf, b_wcat], writes=[b_ps1[pb]],
                           tag=f"mm{t}_{dc}_{pb}")
            S.emit("ac", lambda e, t=t: e.copy(
                out=k_raw[:, t, :], in_=ps1[1][:]),
                reads=[b_ps1[1]], writes=[b_kraw[t]], tag=f"ek{t}")
            S.emit("ac", lambda e: e.activation(
                out=ksq[:], in_=ps1[1][:], func=AF.Square),
                reads=[b_ps1[1]], writes=[b_ksq], tag=f"k2{t}")
            S.emit("dv", lambda e, t=t: e.reduce_sum(
                out=s2k_a[:, t, :],
                in_=ksq[:].rearrange("p (h n) -> p h n", n=128), axis=AX.X),
                reads=[b_ksq], writes=[b_s2k[t]], tag=f"k3{t}")
            S.emit("dv", lambda e: e.tensor_copy(
                out=vst[:, 0:512], in_=ps1[2][:]),
                reads=[b_ps1[2]], writes=[b_vst[0]], tag=f"ev1{t}")
            S.emit("ac", lambda e: e.copy(
                out=vst[:, 512:1024], in_=ps1[3][:]),
                reads=[b_ps1[3]], writes=[b_vst[1]], tag=f"ev2{t}")
            S.emit("sy", lambda e, t=t: e.dma_start(
                out=v_h[t], in_=vst[:]),
                reads=[b_vst[0], b_vst[1]], writes=[b_vh[t]], dma=True,
                tag=f"sv{t}")
            S.emit("ac", lambda e: e.activation(
                out=bsc[:], in_=ps1[6][:, 0:4], func=AF.Tanh, scale=0.5),
                reads=[b_ps1[6]], writes=[b_bsc], tag=f"bt{t}")
            S.emit("dv", lambda e, t=t: e.tensor_scalar(
                out=beta_a[:, t, :], in0=bsc[:], scalar1=0.5, scalar2=0.5,
                op0=ALU.mult, op1=ALU.add),
                reads=[b_bsc], writes=[b_beta[t]], tag=f"bs{t}")

            if own:
                tt = t - OWN0
                S.emit("ac", lambda e, t=t: e.copy(
                    out=q_raw[:, t - OWN0, :], in_=ps1[0][:]),
                    reads=[b_ps1[0]], writes=[b_qraw[tt]], tag=f"eq{t}")
                S.emit("ac", lambda e: e.activation(
                    out=ksq[:], in_=ps1[0][:], func=AF.Square),
                    reads=[b_ps1[0]], writes=[b_ksq], tag=f"q2{t}")
                S.emit("dv", lambda e, tt=tt: e.reduce_sum(
                    out=s2q_a[:, tt, :],
                    in_=ksq[:].rearrange("p (h n) -> p h n", n=128),
                    axis=AX.X),
                    reads=[b_ksq], writes=[b_s2q[tt]], tag=f"q3{t}")
                S.emit("ac", lambda e, tt=tt: e.activation(
                    out=g_s[:, tt, 0:512], in_=ps1[4][:], func=AF.Silu),
                    reads=[b_ps1[4]], writes=[b_gs[tt]], tag=f"eg1{t}")
                S.emit("ac", lambda e, tt=tt: e.activation(
                    out=g_s[:, tt, 512:1024], in_=ps1[5][:], func=AF.Silu),
                    reads=[b_ps1[5]], writes=[b_gs[tt]], tag=f"eg2{t}")

        wq_pos = [0]
        for t in range(NCH):
            p1_tile(t)
            tgt = (t + 1) * NJOB // NCH
            while wq_pos[0] < tgt:
                wq_job(wq_pos[0])
                wq_pos[0] += 1

        st1.close()
        stW.close()
        S.fence()

        if USE_COLLECTIVE:
            S.emit("gp", lambda e: e.collective_compute(
                "AllGather", ALU.bypass,
                ins=[qw_my[:]], outs=[qw_all[:]],
                replica_groups=[list(range(8))]),
                reads=[b_qwmy], writes=[b_qwall], tag="ag")
        else:
            b_qwall.w = list(b_qwmy.w)

        st2 = contextlib.ExitStack()
        def hpos(h):
            return (h % 2) * 2 + (h // 2)
        kn_t = sbt(st2, "kn_t", [128, NH, 128], BF16)
        kn_f = sbt(st2, "kn_f", [128, NH, 128], BF16)
        kbn_f = sbt(st2, "kbn_f", [128, NH, 128], BF16)
        kbn_t = sbt(st2, "kbn_t", [128, NH, 128], BF16)
        qn_t = sbt(st2, "qn_t", [128, NH, 128], BF16)
        qn_f = sbt(st2, "qn_f", [128, NH, 128], BF16)
        v_t = [sbt(st2, f"v_t{i}", [128, D], BF16) for i in range(2)]
        vb_t = sbt(st2, "vb_t", [128, NH, 256], BF16)
        Xp = [sbt(st2, f"Xp{i}", [128, NH, 128], BF16) for i in range(2)]
        XpT = [sbt(st2, f"XpT{i}", [128, NH, 128], BF16) for i in range(2)]
        Rsn = sbt(st2, "Rsn", [128, NH, 256], BF16)
        U_t = sbt(st2, "U_t", [128, NH, 256], BF16)
        Pt_s = sbt(st2, "Pt_s", [128, NH, 128], BF16)
        settle = sbt(st2, "settle", [128, 64], BF16)
        osb = sbt(st2, "osb", [128, 512], BF16)
        b_osb = Buf("osb")
        b_settle = Buf("settle")
        S_ma = sbt(st2, "S_ma", [128, NH, 256], F32)
        b_Sma = [Buf(f"Sma{i}") for i in range(2)]
        inv_k = sbt(st2, "inv_k", [128, 4], F32)
        inv_q = sbt(st2, "inv_q", [128, 4], F32)
        nbk = sbt(st2, "nbk", [128, 4], F32)
        invr = sbt(st2, "invr", [128, 4], F32)
        osq = sbt(st2, "osq", [128, 512], BF16)
        r2 = sbt(st2, "r2", [128, 4], F32)
        gat = sbt(st2, "gat", [128, D], BF16)
        b_knt, b_kbnt, b_qnt = (Buf(n) for n in ("knt", "kbnt", "qnt"))
        b_knf = [Buf(f"knf{h}") for h in range(NH)]
        b_kbnf = [Buf(f"kbnf{h}") for h in range(NH)]
        b_qnf = [Buf(f"qnf{h}") for h in range(NH)]
        b_vt = [Buf(f"vt{i}") for i in range(2)]
        b_vbt = Buf("vbt")
        b_Xp = [Buf(f"Xp{i}") for i in range(2)]
        b_XpT = [Buf(f"XpT{i}") for i in range(2)]
        b_Rsn = [Buf("Rsn0"), Buf("Rsn1")]
        b_Ut = [Buf("Ut0"), Buf("Ut1")]
        b_Pts = Buf("Pts")
        b_invk, b_invq, b_nbk, b_invr = \
            (Buf(n) for n in ("invk", "invq", "nbk", "invr"))
        b_osq = Buf("osq")
        b_r2 = Buf("r2")
        b_gat = [Buf(f"gat{h}") for h in range(NH)]

        S.emit("gp", lambda e: e.memset(S_ma[:], 0.0),
               writes=list(b_Sma), tag="Sma0")
        S_ps = [pst(st2, f"S_ps{i}", [128, 512], F32) for i in range(2)]
        R_ps = [pst(st2, f"R_ps{i}", [128, 512], F32) for i in range(2)]
        P_ps = [pst(st2, f"P_ps{i}", [128, 512], F32) for i in range(2)]
        o_ps = [pst(st2, f"o_ps{i}", [128, 512], F32) for i in range(2)]
        b_Sps = [Buf(f"Sps{i}") for i in range(2)]
        b_Rps = [Buf(f"Rps{i}") for i in range(2)]
        b_Pps = [Buf(f"Pps{i}") for i in range(2)]
        b_ops = [Buf(f"ops{i}") for i in range(2)]

        def s_slot(h):
            return S_ps[h % 2][:, (h // 2) * 256:(h // 2) * 256 + 256]

        def r_slot(h):
            return R_ps[h % 2][:, (h // 2) * 256:(h // 2) * 256 + 256]

        def o_slot(h):
            return o_ps[h % 2][:, (h // 2) * 256:(h // 2) * 256 + 256]

        def prod_mm(pbank, h, lhsT, rhs, reads, tag):
            S.emit("te", lambda e: e.matmul(
                P_ps[pbank][:, h * 128:(h + 1) * 128], lhsT, rhs,
                start=True, stop=True),
                reads=reads, writes=[b_Pps[pbank]], tag=tag)

        def sc_chunk(c):
            own = c >= OWN0
            tt = c - OWN0
            pi = c % 2
            S.emit("sy", lambda e, c=c: e.dma_start(
                out=v_t[pi][:], in_=v_h[c]),
                reads=[b_vh[c]], writes=[b_vt[pi]], dma=True, tag=f"lv{c}")
            S.emit("dv", lambda e, c=c: e.tensor_scalar(
                out=inv_k[:], in0=s2k_a[:, c, :], scalar1=1e-24, scalar2=None,
                op0=ALU.max), reads=[b_s2k[c]], writes=[b_invk], tag=f"ik{c}")
            S.emit("dv", lambda e: e.reciprocal(out=inv_k[:], in_=inv_k[:]),
                   reads=[b_invk], writes=[b_invk], tag=f"ik2{c}")
            S.emit("ac", lambda e: e.activation(
                out=inv_k[:], in_=inv_k[:], func=AF.Sqrt),
                reads=[b_invk], writes=[b_invk], tag=f"ik3{c}")
            S.emit("dv", lambda e, c=c: e.tensor_tensor(
                out=nbk[:], in0=inv_k[:], in1=beta_a[:, c, :], op=ALU.mult),
                reads=[b_invk, b_beta[c]], writes=[b_nbk], tag=f"nb{c}")
            if own:
                S.emit("dv", lambda e, tt=tt: e.tensor_scalar(
                    out=inv_q[:], in0=s2q_a[:, tt, :], scalar1=1e-24,
                    scalar2=None, op0=ALU.max),
                    reads=[b_s2q[tt]], writes=[b_invq], tag=f"iq{c}")
                S.emit("dv", lambda e: e.reciprocal(out=inv_q[:],
                                                    in_=inv_q[:]),
                       reads=[b_invq], writes=[b_invq], tag=f"iq2{c}")
                S.emit("ac", lambda e: e.activation(
                    out=inv_q[:], in_=inv_q[:], func=AF.Sqrt),
                    reads=[b_invq], writes=[b_invq], tag=f"iq3{c}")
            for h in range(NH):
                S.emit("dv", lambda e, c=c, h=h: e.tensor_scalar(
                    out=kn_t[:, h, :], in0=k_raw[:, c, h * 128:(h + 1) * 128],
                    scalar1=inv_k[:, h:h + 1], scalar2=None, op0=ALU.mult),
                    reads=[b_kraw[c], b_invk], writes=[b_knt], tag=f"kn{c}{h}")
                S.emit("dv", lambda e, c=c, h=h: e.tensor_scalar(
                    out=kbn_t[:, h, :],
                    in0=k_raw[:, c, h * 128:(h + 1) * 128],
                    scalar1=nbk[:, h:h + 1], scalar2=-1.0,
                    op0=ALU.mult, op1=ALU.mult),
                    reads=[b_kraw[c], b_nbk], writes=[b_kbnt], tag=f"kb{c}{h}")
                S.emit("dv", lambda e, c=c, h=h: e.tensor_scalar(
                    out=vb_t[:, h, :],
                    in0=v_t[pi][:, h * 256:(h + 1) * 256],
                    scalar1=beta_a[:, c, h:h + 1], scalar2=None,
                    op0=ALU.mult),
                    reads=[b_vt[pi], b_beta[c]], writes=[b_vbt],
                    tag=f"vb{c}{h}")
                if own:
                    S.emit("dv", lambda e, tt=tt, h=h: e.tensor_scalar(
                        out=qn_t[:, h, :],
                        in0=q_raw[:, tt, h * 128:(h + 1) * 128],
                        scalar1=inv_q[:, h:h + 1], scalar2=None,
                        op0=ALU.mult),
                        reads=[b_qraw[tt], b_invq], writes=[b_qnt],
                        tag=f"qn{c}{h}")
            for h in range(NH):
                S.emit("ac", lambda e, h=h: e.dma_start(
                    out=kn_f[:, h, :], in_=kn_t[:, h, :], transpose=True),
                    reads=[b_knt], writes=[b_knf[h]], dma=True, tag=f"tk{c}{h}")
                S.emit("sy", lambda e, h=h: e.dma_start(
                    out=kbn_f[:, h, :], in_=kbn_t[:, h, :], transpose=True),
                    reads=[b_kbnt], writes=[b_kbnf[h]], dma=True, tag=f"tb{c}{h}")
                if own:
                    S.emit("ac", lambda e, h=h: e.dma_start(
                        out=qn_f[:, h, :], in_=qn_t[:, h, :], transpose=True),
                        reads=[b_qnt], writes=[b_qnf[h]], dma=True,
                        tag=f"tq{c}{h}")
            for h in range(NH):
                prod_mm(0, h, kbn_f[:, h, :], kn_f[:, h, :],
                        [b_kbnf[h], b_knf[h]], f"X{c}{h}")
            S.emit("dv", lambda e: e.tensor_tensor(
                out=Xp[0][:].rearrange("p a n -> p (a n)"),
                in0=P_ps[0][:, 0:512], in1=maskSL[:], op=ALU.mult),
                reads=[b_Pps[0], b_maskSL], writes=[b_Xp[0]], tag=f"eX{c}")
            for h in range(NH):
                prod_mm(1, h, kn_f[:, h, :], kbn_f[:, h, :],
                        [b_knf[h], b_kbnf[h]], f"XT{c}{h}")
            S.emit("dv", lambda e: e.tensor_tensor(
                out=XpT[0][:].rearrange("p a n -> p (a n)"),
                in0=P_ps[1][:, 0:512], in1=maskSU[:], op=ALU.mult),
                reads=[b_Pps[1], b_maskSU], writes=[b_XpT[0]], tag=f"eXT{c}")
            for h in range(NH):
                S.emit("te", lambda e, h=h: e.matmul(
                    r_slot(h), i_sb[:], vb_t[:, h, :],
                    start=(h < 2), stop=False, skip_group_check=True),
                    reads=[b_isb, b_vbt], writes=[b_Rps[h % 2]],
                    tag=f"Ri{c}{h}")
            for h in range(NH):
                S.emit("te", lambda e, h=h: e.matmul(
                    r_slot(h), kbn_f[:, h, :], S_bf[:, hpos(h), :],
                    start=False, stop=(h >= 2), skip_group_check=True),
                    reads=[b_kbnf[h], b_Sbf[hpos(h)]], writes=[b_Rps[h % 2]],
                    tag=f"Rs{c}{h}")
            for lev in range(NLEV):
                cur = lev % 2
                S.emit("dv", lambda e: e.tensor_copy(
                    out=Rsn[:, 0:2, :].rearrange("p a n -> p (a n)"),
                    in_=R_ps[0][:]),
                    reads=[b_Rps[0]], writes=[b_Rsn[0]], tag=f"sn{c}_{lev}a")
                S.emit("dv", lambda e: e.tensor_copy(
                    out=Rsn[:, 2:4, :].rearrange("p a n -> p (a n)"),
                    in_=R_ps[1][:]),
                    reads=[b_Rps[1]], writes=[b_Rsn[1]], tag=f"sn{c}_{lev}b")
                if lev < NLEV - 1:
                    nxt = 1 - cur
                    last = (lev == NLEV - 2)
                    if not last:
                        for h in range(NH):
                            prod_mm(0, h, XpT[cur][:, h, :], Xp[cur][:, h, :],
                                    [b_XpT[cur], b_Xp[cur]], f"sq{c}_{lev}{h}")
                        S.emit("dv", lambda e, nxt=nxt: e.tensor_copy(
                            out=Xp[nxt][:].rearrange("p a n -> p (a n)"),
                            in_=P_ps[0][:, 0:512]),
                            reads=[b_Pps[0]], writes=[b_Xp[nxt]],
                            tag=f"esq{c}_{lev}")
                    for h in range(NH):
                        prod_mm(1, h, Xp[cur][:, h, :], XpT[cur][:, h, :],
                                [b_Xp[cur], b_XpT[cur]], f"sqT{c}_{lev}{h}")
                    S.emit("dv", lambda e, nxt=nxt: e.tensor_copy(
                        out=XpT[nxt][:].rearrange("p a n -> p (a n)"),
                        in_=P_ps[1][:, 0:512]),
                        reads=[b_Pps[1]], writes=[b_XpT[nxt]],
                        tag=f"esqT{c}_{lev}")
                for h in range(NH):
                    S.emit("te", lambda e, h=h, cur=cur: e.matmul(
                        r_slot(h), XpT[cur][:, h, :], Rsn[:, hpos(h), :],
                        start=False, stop=(h >= 2),
                        skip_group_check=True),
                        reads=[b_XpT[cur], b_Rsn[h % 2]],
                        writes=[b_Rps[h % 2]],
                        tag=f"Ra{c}_{lev}{h}")
            S.emit("dv", lambda e: e.tensor_copy(
                out=U_t[:, 0:2, :].rearrange("p a n -> p (a n)"),
                in_=R_ps[0][:]),
                reads=[b_Rps[0]], writes=[b_Ut[0]], tag=f"eU{c}a")
            S.emit("dv", lambda e: e.tensor_copy(
                out=U_t[:, 2:4, :].rearrange("p a n -> p (a n)"),
                in_=R_ps[1][:]),
                reads=[b_Rps[1]], writes=[b_Ut[1]], tag=f"eU{c}b")
            S.emit("dv", lambda e: e.tensor_copy(
                out=settle[:], in_=U_t[:, 0, 0:64]),
                reads=[b_Ut[0], b_Ut[1]], writes=[b_settle], tag=f"stl{c}")
            if own:
                for h in range(NH):
                    prod_mm(0, h, kn_f[:, h, :], qn_f[:, h, :],
                            [b_knf[h], b_qnf[h]], f"Pt{c}{h}")
                S.emit("dv", lambda e: e.tensor_tensor(
                    out=Pt_s[:].rearrange("p a n -> p (a n)"),
                    in0=P_ps[0][:, 0:512], in1=maskLE[:], op=ALU.mult),
                    reads=[b_Pps[0], b_maskLE], writes=[b_Pts], tag=f"eP{c}")
                for h in range(NH):
                    S.emit("te", lambda e, h=h: e.matmul(
                        o_slot(h), qn_f[:, h, :], S_bf[:, hpos(h), :],
                        start=True, stop=True, skip_group_check=True),
                        reads=[b_qnf[h], b_Sbf[hpos(h)]],
                        writes=[b_ops[h % 2]],
                        tag=f"o1{c}{h}")
                for h in range(NH):
                    S.emit("te", lambda e, h=h: e.matmul(
                        s_slot(h), Pt_s[:, h, :], U_t[:, hpos(h), :],
                        start=True, stop=True, skip_group_check=True),
                        reads=[b_Pts, b_Ut[h % 2], b_settle],
                        writes=[b_Sps[h % 2]],
                        tag=f"o2{c}{h}")
                for half in range(2):
                    S.emit("dv", lambda e, half=half: e.tensor_copy(
                        out=osb[:], in_=o_ps[half][:]),
                        reads=[b_ops[half]], writes=[b_osb],
                        tag=f"oe{c}{half}")
                    S.emit("dv", lambda e, half=half: e.scalar_tensor_tensor(
                        out=osb[:], in0=S_ps[half][:], scalar=1.0,
                        in1=osb[:], op0=ALU.mult, op1=ALU.add),
                        reads=[b_Sps[half], b_osb], writes=[b_osb],
                        tag=f"oc{c}{half}")
                    S.emit("ac", lambda e, half=half: e.activation(
                        out=osq[:], in_=osb[:], func=AF.Square),
                        reads=[b_osb], writes=[b_osq], tag=f"os{c}{half}")
                    S.emit("dv", lambda e, half=half: e.reduce_sum(
                        out=r2[:, half * 2:half * 2 + 2],
                        in_=osq[:].rearrange("p (h n) -> p h n", n=256),
                        axis=AX.X),
                        reads=[b_osq], writes=[b_r2], tag=f"or{c}{half}")
                    S.emit("dv", lambda e, half=half: e.tensor_copy(
                        out=gat[:].rearrange("p (a n) -> p a n", n=256)[
                            :, half * 2:half * 2 + 2, :].rearrange(
                            "p a n -> p (a n)"), in_=osb[:]),
                        reads=[b_osb],
                        writes=[b_gat[half], b_gat[half + 2]],
                        tag=f"ob{c}{half}")
                S.emit("dv", lambda e: e.tensor_scalar(
                    out=invr[:], in0=r2[:], scalar1=1.0 / 256.0, scalar2=1e-5,
                    op0=ALU.mult, op1=ALU.add),
                    reads=[b_r2], writes=[b_invr], tag=f"om{c}")
                S.emit("dv", lambda e: e.reciprocal(out=invr[:], in_=invr[:]),
                       reads=[b_invr], writes=[b_invr], tag=f"oc{c}")
                S.emit("ac", lambda e: e.activation(
                    out=invr[:], in_=invr[:], func=AF.Sqrt),
                    reads=[b_invr], writes=[b_invr], tag=f"oq{c}")
                for h in range(NH):
                    S.emit("dv", lambda e, h=h: e.tensor_scalar(
                        out=gat[:, hpos(h) * 256:hpos(h) * 256 + 256],
                        in0=gat[:, hpos(h) * 256:hpos(h) * 256 + 256],
                        scalar1=invr[:, hpos(h):hpos(h) + 1], scalar2=None,
                        op0=ALU.mult),
                        reads=[b_gat[hpos(h)], b_invr],
                        writes=[b_gat[hpos(h)]],
                        tag=f"og{c}{h}")
                for h in range(NH):
                    S.emit("dv", lambda e, h=h, tt=tt: e.tensor_tensor(
                        out=gat[:, hpos(h) * 256:hpos(h) * 256 + 256],
                        in0=gat[:, hpos(h) * 256:hpos(h) * 256 + 256],
                        in1=g_s[:, tt, h * 256:(h + 1) * 256], op=ALU.mult),
                        reads=[b_gat[hpos(h)], b_gs[tt]],
                        writes=[b_gat[hpos(h)]],
                        tag=f"gg{c}{h}")
                S.emit("sy", lambda e, tt=tt: e.dma_start(
                    out=g_h[tt], in_=gat[:]),
                    reads=list(b_gat), writes=[b_gh[tt]], dma=True,
                    tag=f"sg{c}")
            for h in range(NH):
                S.emit("te", lambda e, h=h: e.matmul(
                    s_slot(h), kn_t[:, h, :], U_t[:, hpos(h), :],
                    start=True, stop=True, skip_group_check=True),
                    reads=[b_knt, b_Ut[h % 2], b_settle],
                    writes=[b_Sps[h % 2]],
                    tag=f"Su{c}{h}")
            for half in range(2):
                S.emit("dv", lambda e, half=half: e.scalar_tensor_tensor(
                    out=S_ma[:, half * 2:half * 2 + 2, :].rearrange(
                        "p a n -> p (a n)"),
                    in0=S_ps[half][:], scalar=1.0,
                    in1=S_ma[:, half * 2:half * 2 + 2, :].rearrange(
                        "p a n -> p (a n)"),
                    op0=ALU.mult, op1=ALU.add),
                    reads=[b_Sps[half], b_Sma[half]],
                    writes=[b_Sma[half]],
                    tag=f"aS{c}{half}")
                S.emit("dv", lambda e, half=half: e.tensor_copy(
                    out=S_bf[:, half * 2:half * 2 + 2, :].rearrange(
                        "p a n -> p (a n)"),
                    in_=S_ma[:, half * 2:half * 2 + 2, :].rearrange(
                        "p a n -> p (a n)")),
                    reads=[b_Sma[half]],
                    writes=[b_Sbf[half * 2], b_Sbf[half * 2 + 1]],
                    tag=f"eS{c}{half}")
            if c == 0:
                dbg_dump(0, S_bf[:].rearrange("p a n -> p (a n)"),
                         list(b_Sbf))
            if c == 8:
                dbg_dump(1, S_bf[:].rearrange("p a n -> p (a n)"),
                         list(b_Sbf))

        for c in range(NCH):
            sc_chunk(c)
        st2.close()
        st12.close()
        S.fence()

        st3 = contextlib.ExitStack()
        wgqT = sbt(st3, "wgqT", [128, 8, IF], BF16)
        wuqT = sbt(st3, "wuqT", [128, 8, IF], BF16)
        wdqT = sbt(st3, "wdqT", [128, 22, D], BF16)
        Wo_b = sbt(st3, "Wo_b", [128, 8, D], BF16)
        b_wgqT = [Buf(f"wgqT{d}") for d in range(8)]
        b_wuqT = [Buf(f"wuqT{d}") for d in range(8)]
        b_wdqT = [Buf(f"wdqT{i}") for i in range(22)]
        b_wob = Buf("wob")
        qw_flat = qw_all.rearrange("a b -> (a b)")

        x3_t = [sbt(st3, f"x3_t{i}", [128, D], F32) for i in range(2)]
        b_x3 = [Buf(f"x3{i}") for i in range(2)]
        for dc in range(8):
            S.emit("sy", lambda e, dc=dc: e.dma_start(
                out=x3_t[dc % 2][:],
                in_=w_in["Wo"][dc * 128:(dc + 1) * 128, :].rearrange(
                    "(a p) n -> p (a n)", p=128)),
                writes=[b_x3[dc % 2]], dma=True, tag=f"lwo{dc}")
            S.emit("dv", lambda e, dc=dc: e.tensor_scalar(
                out=Wo_b[:, dc, :], in0=x3_t[dc % 2][:],
                scalar1=nwc[:, dc % 2:dc % 2 + 1], scalar2=None,
                op0=ALU.mult),
                reads=[b_x3[dc % 2], b_nwc], writes=[b_wob], tag=f"wos{dc}")
        for dc in range(8):
            S.emit("sy", lambda e, dc=dc: e.dma_start(
                out=wgqT[:, dc, :],
                in_=qw_all[0:IF, dc * 128:(dc + 1) * 128], transpose=True),
                reads=[b_qwall], writes=[b_wgqT[dc]], dma=True,
                tag=f"lwg{dc}")
            S.emit("sy", lambda e, dc=dc: e.dma_start(
                out=wuqT[:, dc, :],
                in_=qw_all[IF:2 * IF, dc * 128:(dc + 1) * 128],
                transpose=True),
                reads=[b_qwall], writes=[b_wuqT[dc]], dma=True,
                tag=f"lwu{dc}")
        wd_2d = qw_flat[2 * IF * 1024:].rearrange("(a b) -> a b", b=IF)
        for ic in range(22):
            S.emit("ac", lambda e, ic=ic: e.dma_start(
                out=wdqT[:, ic, :],
                in_=wd_2d[:, ic * 128:(ic + 1) * 128], transpose=True),
                reads=[b_qwall], writes=[b_wdqT[ic]], dma=True,
                tag=f"lwd{ic}")

        xn3 = sbt(st3, "xn3", [128, D], F32)
        xq_t = sbt(st3, "xq_t", [128, D], BF16)
        xq_f = sbt(st3, "xq_f", [128, 8, 128], BF16)
        gg_t = [sbt(st3, f"gg_t{i}", [128, D], BF16) for i in range(2)]
        gg_f = sbt(st3, "gg_f", [128, 8, 128], BF16)
        sg_t = sbt(st3, "sg_t", [128, 512], F32)
        h2 = sbt(st3, "h2", [128, 512], F32)
        h2q = sbt(st3, "h2q", [128, IF], BF16)
        h2q_f = sbt(st3, "h2q_f", [128, 22, 128], BF16)
        ot = [sbt(st3, "ot0", [128, D], F32)] * 2
        b_xn3 = Buf("xn3")
        b_xqt = Buf("xqt")
        b_xqf = [Buf(f"xqf{d}") for d in range(8)]
        b_ggt = [Buf(f"ggt{i}") for i in range(2)]
        b_ggf = [Buf(f"ggf{d}") for d in range(8)]
        b_sgt = Buf("sgt")
        b_h2 = Buf("h2")
        b_h2q = Buf("h2q")
        b_h2qf = [Buf(f"h2qf{i}") for i in range(22)]
        _otb = [Buf("ot_0"), Buf("ot_1")]
        b_ot = [_otb, _otb]
        b_outh = [Buf(f"oh{t}") for t in range(16)]

        gu_ps = [pst(st3, f"gu_ps{i}", [128, 512], F32) for i in range(4)]
        dn_ps = [pst(st3, f"dn_ps{i}", [128, 512], F32) for i in range(2)]
        b_gups = [Buf(f"gu{i}") for i in range(4)]
        b_dnps = [Buf(f"dn{i}") for i in range(2)]

        def p3_tile(tt):
            pi = tt % 2
            t = tt + OWN0
            S.emit("sy", lambda e, t=t: e.dma_start(
                out=x3_t[pi][:], in_=xseq[t * 128:(t + 1) * 128, :]),
                writes=[b_x3[pi]], dma=True, tag=f"3x{tt}")
            S.emit("ac", lambda e, t=t: e.activation(
                out=xn3[:], in_=x3_t[pi][:], func=AF.Identity,
                scale=stats[:, t, 1:2], bias=stats[:, t, 0:1]),
                reads=[b_x3[pi], b_stats[0], b_stats[1]], writes=[b_xn3],
                tag=f"3n{tt}")
            quant(xn3[:], b_xn3, 1024, xq_t[:], b_xqt)
            if tt == 0:
                dbg_dump(8, xq_t[:], [b_xqt])
                dbg_dump(9, wgqT[:, 0, 0:1024], [b_wgqT[0]])
                dbg_dump(10, wdqT[:, 0, :], [b_wdqT[0]])
                dbg_dump(11, gg_t[pi][:], [b_ggt[pi]])
            for dc in range(8):
                S.emit("sy", lambda e, dc=dc: e.dma_start(
                    out=xq_f[:, dc, :], in_=xq_t[:, dc * 128:(dc + 1) * 128],
                    transpose=True),
                    reads=[b_xqt], writes=[b_xqf[dc]], dma=True,
                    tag=f"3t{tt}{dc}")
            S.emit("ac", lambda e, tt=tt: e.dma_start(
                out=gg_t[pi][:], in_=g_h[tt]),
                reads=[b_gh[tt]], writes=[b_ggt[pi]], dma=True, tag=f"3g{tt}")
            for dc in range(8):
                S.emit("ac", lambda e, dc=dc: e.dma_start(
                    out=gg_f[:, dc, :],
                    in_=gg_t[pi][:, dc * 128:(dc + 1) * 128], transpose=True),
                    reads=[b_ggt[pi]], writes=[b_ggf[dc]], dma=True,
                    tag=f"3gt{tt}{dc}")
            for dc in range(8):
                for half in range(2):
                    S.emit("te", lambda e, dc=dc, half=half: e.matmul(
                        dn_ps[half][:], gg_f[:, dc, :],
                        Wo_b[:, dc, half * 512:(half + 1) * 512],
                        start=(dc == 0), stop=False, skip_group_check=True),
                        reads=[b_ggf[dc], b_wob], writes=[b_dnps[half]],
                        tag=f"wo{tt}_{dc}_{half}")
            pieces = [(i * 512, min(IF, (i + 1) * 512)) for i in range(6)]
            for pidx, (i0, i1) in enumerate(pieces):
                gb = (pidx % 2) * 2
                ub = gb + 1
                for dc in range(8):
                    S.emit("te", lambda e, dc=dc, i0=i0, i1=i1, gb=gb:
                           e.matmul(gu_ps[gb][:, 0:i1 - i0], xq_f[:, dc, :],
                                    wgqT[:, dc, i0:i1],
                                    start=(dc == 0), stop=(dc == 7)),
                           reads=[b_xqf[dc], b_wgqT[dc]],
                           writes=[b_gups[gb]],
                           tag=f"gm{tt}_{pidx}_{dc}")
                    S.emit("te", lambda e, dc=dc, i0=i0, i1=i1, ub=ub:
                           e.matmul(gu_ps[ub][:, 0:i1 - i0], xq_f[:, dc, :],
                                    wuqT[:, dc, i0:i1],
                                    start=(dc == 0), stop=(dc == 7)),
                           reads=[b_xqf[dc], b_wuqT[dc]],
                           writes=[b_gups[ub]],
                           tag=f"um{tt}_{pidx}_{dc}")
                S.emit("ac", lambda e, i0=i0, i1=i1, gb=gb: e.activation(
                    out=sg_t[:, 0:i1 - i0], in_=gu_ps[gb][:, 0:i1 - i0],
                    func=AF.Silu),
                    reads=[b_gups[gb]], writes=[b_sgt], tag=f"sg{tt}_{pidx}")
                S.emit("dv", lambda e, i0=i0, i1=i1, ub=ub: e.tensor_tensor(
                    out=h2[:, 0:i1 - i0], in0=gu_ps[ub][:, 0:i1 - i0],
                    in1=sg_t[:, 0:i1 - i0], op=ALU.mult),
                    reads=[b_gups[ub], b_sgt], writes=[b_h2],
                    tag=f"h2{tt}_{pidx}")
                quant_piece(h2[:, 0:i1 - i0], b_h2, i1 - i0,
                            h2q[:, i0:i1], b_h2q)
            if tt == 0 and pidx == 0:
                dbg_dump(12, sg_t[:, 0:512], [b_sgt])
                dbg_dump(13, h2q[:, 0:512], [b_h2q])
            for ic in range(22):
                eng = "sy" if ic % 2 == 0 else "ac"
                S.emit(eng, lambda e, ic=ic: e.dma_start(
                    out=h2q_f[:, ic, :],
                    in_=h2q[:, ic * 128:(ic + 1) * 128], transpose=True),
                    reads=[b_h2q], writes=[b_h2qf[ic]], dma=True,
                    tag=f"ht{tt}{ic}")
            for ic in range(22):
                for half in range(2):
                    S.emit("te", lambda e, ic=ic, half=half: e.matmul(
                        dn_ps[half][:], h2q_f[:, ic, :],
                        wdqT[:, ic, half * 512:(half + 1) * 512],
                        start=False, stop=(ic == 21), skip_group_check=True),
                        reads=[b_h2qf[ic], b_wdqT[ic]],
                        writes=[b_dnps[half]],
                        tag=f"dm{tt}_{ic}_{half}")
            if tt == 0:
                dbg_dump(14, gg_f[:].rearrange("p a n -> p (a n)"),
                         list(b_ggf))
            for half in range(2):
                S.emit("dv", lambda e, half=half: e.scalar_tensor_tensor(
                    out=ot[pi][:, half * 512:(half + 1) * 512],
                    in0=dn_ps[half][:], scalar=1.0,
                    in1=x3_t[pi][:, half * 512:(half + 1) * 512],
                    op0=ALU.mult, op1=ALU.add),
                    reads=[b_dnps[half], b_x3[pi]], writes=[b_ot[pi][half]],
                    tag=f"rs{tt}{half}")
            S.emit("sy", lambda e, tt=tt: e.dma_start(
                out=out_h[tt * 128:(tt + 1) * 128, :], in_=ot[pi][:]),
                reads=list(b_ot[pi]), writes=[b_outh[tt]], dma=True,
                tag=f"so{tt}")

        for tt in range(16):
            p3_tile(tt)
        st3.close()
        stQ.close()
        stW.close()

        fin = S.barrier_dep()
        fbuf = Buf("fin")
        fbuf.w = fin
        S.emit("sy", lambda e: e.nop(), reads=[fbuf], tag="fin")

        S.flush(block, sems)
    outer.close()
    return nc


_NC_CACHE = {}


USE = None


def kernel(L0_attn_hidden_states, L0_attn_conv_w, L0_attn_Wq, L0_attn_Wk,
           L0_attn_Wv, L0_attn_Wb, L0_attn_Wg, L0_attn_norm_w, L0_attn_Wo,
           L0_ffn_w_gate, L0_ffn_w_up, L0_ffn_w_down):
    x = np.asarray(L0_attn_hidden_states, dtype=np.float32)
    w = {
        "conv_w": np.ascontiguousarray(L0_attn_conv_w, dtype=np.float32),
        "Wq": np.ascontiguousarray(L0_attn_Wq, dtype=np.float32),
        "Wk": np.ascontiguousarray(L0_attn_Wk, dtype=np.float32),
        "Wv": np.ascontiguousarray(L0_attn_Wv, dtype=np.float32),
        "Wb": np.ascontiguousarray(L0_attn_Wb, dtype=np.float32),
        "Wg": np.ascontiguousarray(L0_attn_Wg, dtype=np.float32),
        "norm_w": np.ascontiguousarray(L0_attn_norm_w, dtype=np.float32),
        "Wo": np.ascontiguousarray(np.concatenate(
            [np.asarray(L0_attn_Wo, np.float32)[0:256],
             np.asarray(L0_attn_Wo, np.float32)[512:768],
             np.asarray(L0_attn_Wo, np.float32)[256:512],
             np.asarray(L0_attn_Wo, np.float32)[768:1024]], axis=0)),
    }
    flat = np.concatenate([
        np.asarray(L0_ffn_w_gate, np.float32).reshape(-1, 1024),
        np.asarray(L0_ffn_w_up, np.float32).reshape(-1, 1024),
        np.asarray(L0_ffn_w_down, np.float32).reshape(-1, 1024)], axis=0)
    if "nc" not in _NC_CACHE:
        _NC_CACHE["nc"] = build_program()
    nc = _NC_CACHE["nc"]
    in_maps = []
    for core in range(8):
        b, hf = core // 2, core % 2
        if hf == 0:
            seq = np.concatenate(
                [np.zeros((2048, D), np.float32), x[b, 0:2048]], 0)
        else:
            seq = np.ascontiguousarray(x[b])
        tri = np.tril(np.ones((128, 128), np.float32), -1)
        cns = np.stack([np.tile(tri, (1, 4)),
                        np.tile(tri.T, (1, 4)),
                        np.tile(np.triu(np.ones((128, 128), np.float32)),
                                (1, 4)),
                        np.concatenate([np.eye(128, dtype=np.float32),
                                        np.zeros((128, 384), np.float32)], 1)],
                       axis=1)
        if USE_COLLECTIVE:
            qsh = np.ascontiguousarray(flat[core * QSH:(core + 1) * QSH])
        else:
            qsh = flat
        im = {"xseq": seq, "qw_in": qsh, "consts": cns}
        im.update(w)
        in_maps.append(im)
    res = run_bass_kernel_spmd(nc, in_maps, core_ids=list(range(8)))
    if DBG:
        sys.modules[__name__].dbg_results = [r.get("dbg") for r in res.results]
    out = np.empty((B, L, D), np.float32)
    for core in range(8):
        b, hf = core // 2, core % 2
        out[b, hf * 2048:(hf + 1) * 2048] = res.results[core]["out"]
    return out


# version 16
# speedup vs baseline: 1.1559x; 1.1559x over previous
import contextlib
import sys
import numpy as np
import concourse.bass as bass
import concourse.mybir as mybir
from concourse.bass_utils import run_bass_kernel_spmd

F32 = mybir.dt.float32
BF16 = mybir.dt.bfloat16
I32 = mybir.dt.int32
AF = mybir.ActivationFunctionType
ALU = mybir.AluOpType
AX = mybir.AxisListType

B, L, D = 4, 4096, 1024
NH, DQH, DVH = 4, 128, 256
C = 128
NCH = L // C
OWN0 = NCH // 2
IF = 2816
NLEV = 7
USE_COLLECTIVE = False
SERIALIZE_TE = False
SERIALIZE_ALL = False
SERIALIZE_DMA = True
DBG = False

QROWS = (IF * D * 2 + D * IF) // 1024
QSH = QROWS // 8 if USE_COLLECTIVE else QROWS

ENGS = ("te", "dv", "ac", "gp", "sy")
ENG_SEM = {"te": "TE", "dv": "DV", "ac": "AC", "gp": "GP", "sy": "SY"}
DMA_Q = {"sy": "DS", "ac": "DA", "gp": "DG"}
NROT = 8
DMA_SEMS = {f"{q}{i}" for q in DMA_Q.values() for i in range(NROT)}


class Buf:
    __slots__ = ("w", "r", "name")

    def __init__(self, name=""):
        self.w = []
        self.r = []
        self.name = name


class Op:
    __slots__ = ("eng", "fn", "waits", "sem", "val", "tag", "need_inc")

    def __init__(self, eng, fn, waits, sem, val, tag):
        self.eng, self.fn, self.waits, self.sem, self.val, self.tag = \
            eng, fn, waits, sem, val, tag
        self.need_inc = False


class Sched:

    def __init__(self):
        self.count = {s: 0 for s in list(ENG_SEM.values()) + list(DMA_SEMS)}
        self.eng_pos = {e: 0 for e in ENGS}
        self.dma_uses = {q: 0 for q in DMA_Q.values()}
        self.ops = []
        self.by_sem_val = {}
        self.fence_map = {}

    def emit(self, eng, fn, reads=(), writes=(), dma=False, tag=""):
        deps = []
        for b in reads:
            deps.extend(b.w)
        for b in writes:
            deps.extend(b.w)
            deps.extend(b.r)
        dmap = dict(self.fence_map)
        for s, v, pos in deps:
            if v > dmap.get(s, -1):
                dmap[s] = v
        if dma and SERIALIZE_DMA:
            for s, v, pos in getattr(self, "dmatok", []):
                if v > dmap.get(s, -1):
                    dmap[s] = v
        if dma:
            q = DMA_Q[eng]
            k = self.dma_uses[q]
            self.dma_uses[q] = k + 1
            sem = f"{q}{k % NROT}"
            prev = self.count[sem]
            if prev > 0 and dmap.get(sem, -1) < prev:
                dmap[sem] = prev
            self.count[sem] += 16
        else:
            sem = ENG_SEM[eng]
            self.count[sem] += 1
        val = self.count[sem]
        if dma and SERIALIZE_DMA:
            self.dmatok = [(sem, val, 0)]
        waits = sorted(dmap.items())
        op = Op(eng, fn, waits, sem, val, tag)
        self.ops.append(op)
        self.by_sem_val[(sem, val)] = op
        self.eng_pos[eng] += 1
        dep = (sem, val, self.eng_pos[eng])
        for b in reads:
            b.r.append(dep)
        for b in writes:
            b.w = [dep]
            b.r = []
        return dep

    def barrier_dep(self):
        return [(s, v, 10 ** 9) for s, v in self.count.items() if v > 0]

    def fence(self):
        self.fence_map = {s: v for s, v in self.count.items() if v > 0}

    def finalize(self):
        engs = ENGS if SERIALIZE_ALL else (("te",) if SERIALIZE_TE else ())
        for eng in engs:
            sem = ENG_SEM[eng]
            prev = None
            for op in self.ops:
                if op.eng == eng and op.sem == sem:
                    if prev is not None:
                        if dict(op.waits).get(sem, -1) < prev:
                            w = dict(op.waits)
                            w[sem] = prev
                            op.waits = sorted(w.items())
                    prev = op.val
        waited = {}
        for op in self.ops:
            for s, v in op.waits:
                waited.setdefault(s, set()).add(v)
        for op in self.ops:
            ws = waited.get(op.sem)
            if ws and any(w >= op.val for w in ws):
                op.need_inc = True
        per = {e: [] for e in ENGS}
        for op in self.ops:
            per[op.eng].append(op)
        idx = {e: 0 for e in ENGS}
        cur = {s: 0 for s in self.count}
        done, total = 0, len(self.ops)
        while done < total:
            prog = False
            for e in ENGS:
                while idx[e] < len(per[e]):
                    op = per[e][idx[e]]
                    if all(cur[s] >= v for s, v in op.waits):
                        cur[op.sem] = op.val
                        idx[e] += 1
                        done += 1
                        prog = True
                    else:
                        break
            if not prog:
                stuck = []
                for e in ENGS:
                    if idx[e] < len(per[e]):
                        op = per[e][idx[e]]
                        bad = [(s, v, cur[s]) for s, v in op.waits
                               if cur[s] < v]
                        stuck.append((e, op.tag, bad[:3]))
                raise RuntimeError(f"deadlock: {stuck[:4]}")

    def flush(self, block, sems):
        self.finalize()
        per = {e: [] for e in ENGS}
        for op in self.ops:
            per[op.eng].append(op)

        def mk(lst, ename):
            sat = {}

            def f(e):
                for op in lst:
                    for s, v in op.waits:
                        if sat.get(s, -1) >= v:
                            continue
                        e.wait_ge(sems[s], v)
                        sat[s] = v
                    ins = op.fn(e)
                    if op.need_inc:
                        ins.then_inc(sems[op.sem],
                                     16 if op.sem in DMA_SEMS else 1)
            return f
        if per["te"]:
            block.tensor(mk(per["te"], "te"))
        if per["dv"]:
            block.vector(mk(per["dv"], "dv"))
        if per["ac"]:
            block.scalar(mk(per["ac"], "ac"))
        if per["gp"]:
            block.gpsimd(mk(per["gp"], "gp"))
        if per["sy"]:
            block.sync(mk(per["sy"], "sy"))


def build_program():
    nc = bass.Bass("TRN2", target_bir_lowering=False, debug=False)
    S = Sched()

    xseq = nc.dram_tensor("xseq", [L, D], F32, kind="ExternalInput").ap()
    w_in = {}
    for nm, shp in [("conv_w", [D, 4]), ("Wq", [D, 512]), ("Wk", [D, 512]),
                    ("Wv", [D, D]), ("Wb", [D, 4]), ("Wg", [D, D]),
                    ("norm_w", [DVH]), ("Wo", [D, D])]:
        w_in[nm] = nc.dram_tensor(nm, shp, F32, kind="ExternalInput").ap()
    qw_in = nc.dram_tensor("qw_in", [QSH, 1024], F32,
                           kind="ExternalInput").ap()
    consts = nc.dram_tensor("consts", [128, 4, 512], F32,
                            kind="ExternalInput").ap()
    out_h = nc.dram_tensor("out", [2048, D], F32, kind="ExternalOutput").ap()
    if DBG:
        dbg_h = nc.dram_tensor("dbg", [16, 128, 1536], BF16,
                               kind="ExternalOutput").ap()
        b_dbg = Buf("dbg")
        dbg_stk = contextlib.ExitStack()
        dbg_stage = dbg_stk.enter_context(
            nc.sbuf_tensor("dbg_st", [128, 1536], BF16))
        b_dbgst = Buf("dbgst")

        def dbg_dump(slot, ap, rbufs, eng="sy"):
            w = ap.shape[-1]
            S.emit("dv", lambda e: e.tensor_copy(
                out=dbg_stage[:, 0:w], in_=ap),
                reads=list(rbufs), writes=[b_dbgst], tag=f"dbgc{slot}")
            S.emit("sy", lambda e: e.dma_start(
                out=dbg_h[slot, :, 0:w], in_=dbg_stage[:, 0:w]),
                reads=[b_dbgst], writes=[b_dbg], dma=True,
                tag=f"dbg{slot}")
    else:
        def dbg_dump(slot, ap, rbufs, eng="sy"):
            pass

    v_h = nc.dram_tensor("v_sp", [NCH, 128, D], BF16).ap()
    g_h = nc.dram_tensor("g_sp", [16, 128, D], BF16).ap()
    if USE_COLLECTIVE:
        qw_my = nc.dram_tensor("qw_my", [QSH, 1024], BF16).ap()
        qw_all = nc.dram_tensor("qw_all", [QROWS, 1024], BF16,
                                addr_space="Local").ap()
    else:
        qw_my = nc.dram_tensor("qw_my", [QROWS, 1024], BF16).ap()
        qw_all = qw_my

    outer = contextlib.ExitStack()
    sems = {}
    for s in list(ENG_SEM.values()) + list(DMA_SEMS):
        sems[s] = outer.enter_context(nc.semaphore(s))

    def sbt(stk, name, shape, dt):
        return stk.enter_context(nc.sbuf_tensor(name, shape, dt))

    def pst(stk, name, shape, dt):
        return stk.enter_context(nc.psum_tensor(name, shape, dt))

    maskSL = sbt(outer, "maskSL", [128, 512], BF16)
    maskSU = sbt(outer, "maskSU", [128, 512], BF16)
    maskLE = sbt(outer, "maskLE", [128, 512], BF16)
    i_sb = sbt(outer, "i_sb", [128, 128], BF16)
    convw = sbt(outer, "convw", [128, 8, 4], F32)
    nwc = sbt(outer, "nwc", [128, 2], F32)
    stats = sbt(outer, "stats", [128, NCH, 2], F32)
    stQ = contextlib.ExitStack()
    QW = 512
    q_m = sbt(stQ, "q_m", [128, 32], F32)
    q_s = sbt(stQ, "q_s", [128, 32], F32)
    q_a = sbt(stQ, "q_a", [128, 32], F32)
    q_h = sbt(stQ, "q_h", [128, 32], F32)
    q_u = sbt(stQ, "q_u", [128, QW], F32)
    q_au = sbt(stQ, "q_au", [128, QW], BF16)
    q_f1 = sbt(stQ, "q_f1", [128, QW], BF16)
    q_f = sbt(stQ, "q_f", [128, QW], BF16)
    q_d = sbt(stQ, "q_d", [128, QW], BF16)
    q_w = sbt(stQ, "q_w", [128, QW], F32)
    q_ri = sbt(stQ, "q_ri", [128, QW], I32)
    q_rb = sbt(stQ, "q_rb", [128, QW], BF16)
    (b_qm, b_qs, b_qa, b_qh, b_qu, b_qau, b_qf1, b_qf, b_qd, b_qw,
     b_qri, b_qrb) = (Buf(n) for n in
                      ("qm", "qs", "qa", "qh", "qu", "qau", "qf1", "qf",
                       "qd", "qw", "qri", "qrb"))
    st12 = contextlib.ExitStack()
    beta_a = sbt(st12, "beta_a", [128, NCH, 4], F32)
    s2k_a = sbt(st12, "s2k_a", [128, NCH, 4], F32)
    s2q_a = sbt(st12, "s2q_a", [128, 16, 4], F32)
    k_raw = sbt(st12, "k_raw", [128, NCH, 512], BF16)
    q_raw = sbt(st12, "q_raw", [128, 16, 512], BF16)
    g_s = sbt(st12, "g_s", [128, 16, D], BF16)
    S_bf = sbt(st12, "S_bf", [128, NH, 256], BF16)

    b_maskSL, b_maskSU, b_maskLE, b_isb, b_convw, b_nwc = \
        (Buf(n) for n in ("mSL", "mSU", "mLE", "isb", "cw", "nwc"))
    b_stats = [Buf(f"st{t}") for t in range(NCH)]
    b_beta = [Buf(f"be{c}") for c in range(NCH)]
    b_s2k = [Buf(f"sk{c}") for c in range(NCH)]
    b_s2q = [Buf(f"sq{c}") for c in range(16)]
    b_kraw = [Buf(f"kr{c}") for c in range(NCH)]
    b_qraw = [Buf(f"qr{c}") for c in range(16)]
    b_gs = [Buf(f"gs{c}") for c in range(16)]
    b_Sbf = [Buf(f"Sb{h}") for h in range(NH)]
    b_vh = [Buf(f"vh{c}") for c in range(NCH)]
    b_gh = [Buf(f"gh{c}") for c in range(16)]
    b_qwmy = Buf("qwmy")
    b_qwall = Buf("qwall")

    with nc.Block() as block:
        S.emit("gp", lambda e: e.dma_start(out=maskSL[:], in_=consts[:, 0, :]),
               writes=[b_maskSL], dma=True, tag="m0")
        S.emit("gp", lambda e: e.dma_start(out=maskSU[:], in_=consts[:, 1, :]),
               writes=[b_maskSU], dma=True, tag="m1")
        S.emit("gp", lambda e: e.dma_start(out=maskLE[:], in_=consts[:, 2, :]),
               writes=[b_maskLE], dma=True, tag="m2")
        S.emit("gp", lambda e: e.dma_start(out=i_sb[:],
                                           in_=consts[:, 3, 0:128]),
               writes=[b_isb], dma=True, tag="m3")
        S.emit("gp", lambda e: e.memset(S_bf[:], 0.0),
               writes=list(b_Sbf), tag="S0")
        S.emit("sy", lambda e: e.dma_start(
            out=convw[:], in_=w_in["conv_w"].rearrange("(a p) k -> p a k",
                                                       p=128)),
            writes=[b_convw], dma=True, tag="ldcw")
        S.emit("sy", lambda e: e.dma_start(
            out=nwc[:], in_=w_in["norm_w"].rearrange("(a p) -> p a", p=128),
            allow_slow_non_contiguous=True),
            writes=[b_nwc], dma=True, tag="ldnw")


        stW = contextlib.ExitStack()
        Wcat = sbt(stW, "Wcat", [128, 8, 3076], BF16)
        wq_x = [sbt(stW, f"wq_x{i}", [128, 1024], F32) for i in range(2)]
        wq_o = sbt(stW, "wq_o", [128, 1024], BF16)
        b_wqx = [Buf(f"wqx{i}") for i in range(2)]
        b_wqo = Buf("wqo")
        b_wcat = Buf("wcat")
        for nm, c0, c1 in [("Wq", 0, 512), ("Wk", 512, 1024),
                           ("Wv", 1024, 2048), ("Wg", 2048, 3072),
                           ("Wb", 3072, 3076)]:
            S.emit("gp", lambda e, nm=nm, c0=c0, c1=c1: e.dma_start(
                out=Wcat[:, :, c0:c1],
                in_=w_in[nm].rearrange("(a p) n -> p a n", p=128)),
                writes=[b_wcat], dma=True, tag=f"ld{nm}")

        stP = contextlib.ExitStack()
        xp = [sbt(stP, f"xp{i}", [128, D], F32) for i in range(2)]
        b_xp = [Buf(f"xp{i}") for i in range(2)]
        bns = sbt(stP, "bns", [128, 12], F32)
        bna = sbt(stP, "bna", [128, NCH, 2], F32)
        b_bns = Buf("bns")
        b_bna = [Buf(f"bna{t}") for t in range(NCH)]
        for t in range(NCH):
            pi = t % 2
            S.emit("sy", lambda e, t=t, pi=pi: e.dma_start(
                out=xp[pi][:], in_=xseq[t * 128:(t + 1) * 128, :]),
                writes=[b_xp[pi]], dma=True, tag=f"px{t}")
            S.emit("dv", lambda e, pi=pi: e.bn_stats(
                out=bns[:, 0:6], in_=xp[pi][:, 0:512]),
                reads=[b_xp[pi]], writes=[b_bns], tag=f"pb{t}")
            S.emit("dv", lambda e, pi=pi: e.bn_stats(
                out=bns[:, 6:12], in_=xp[pi][:, 512:1024]),
                reads=[b_xp[pi]], writes=[b_bns], tag=f"pb2{t}")
            S.emit("dv", lambda e, t=t: e.bn_aggr(out=bna[:, t, :],
                                                  in_=bns[:]),
                   reads=[b_bns], writes=[b_bna[t]], tag=f"pa{t}")
        b_all_bna = list(b_bna)
        S.emit("dv", lambda e: e.tensor_scalar(
            out=stats[:, :, 1], in0=bna[:, :, 1], scalar1=1e-5, scalar2=None,
            op0=ALU.add), reads=b_all_bna, writes=[b_stats[0]], tag="pe")
        S.emit("dv", lambda e: e.reciprocal(
            out=stats[:, :, 1], in_=stats[:, :, 1]),
            reads=[b_stats[0]], writes=[b_stats[0]], tag="pr")
        S.emit("ac", lambda e: e.activation(
            out=stats[:, :, 1], in_=stats[:, :, 1], func=AF.Sqrt),
            reads=[b_stats[0]], writes=[b_stats[0]], tag="ps")
        S.emit("dv", lambda e: e.scalar_tensor_tensor(
            out=stats[:, :, 0], in0=bna[:, :, 0], scalar=-1.0,
            in1=stats[:, :, 1], op0=ALU.mult, op1=ALU.mult),
            reads=b_all_bna + [b_stats[0]], writes=[b_stats[1]], tag="pm")
        stP.close()
        S.fence()


        def quant(xap, bx, n, outap, bout, P=128):
            for p0 in range(0, n, QW):
                quant_piece(xap[:, p0:min(n, p0 + QW)], bx,
                            min(n, p0 + QW) - p0,
                            outap[:, p0:min(n, p0 + QW)], bout, P=P)

        def quant_piece(xap, bx, n, outap, bout, P=128):
            nb = n // 16
            x3 = xap.rearrange("p (b k) -> p b k", k=16)
            S.emit("dv", lambda e: e.reduce_max(
                out=q_m[:P, 0:nb], in_=x3, axis=AX.X,
                apply_absolute_value=True),
                reads=[bx], writes=[b_qm], tag="q_red")
            S.emit("dv", lambda e: e.tensor_scalar(
                out=q_s[:P, 0:nb], in0=q_m[:P, 0:nb], scalar1=1.0 / 6.0,
                scalar2=1e-12, op0=ALU.mult, op1=ALU.max),
                reads=[b_qm], writes=[b_qs], tag="q_s")
            S.emit("dv", lambda e: e.reciprocal(
                out=q_a[:P, 0:nb], in_=q_s[:P, 0:nb]),
                reads=[b_qs], writes=[b_qa], tag="q_a")
            S.emit("dv", lambda e: e.tensor_scalar(
                out=q_a[:P, 0:nb], in0=q_a[:P, 0:nb], scalar1=2.0,
                scalar2=None, op0=ALU.mult),
                reads=[b_qa], writes=[b_qa], tag="q_a2")
            S.emit("dv", lambda e: e.tensor_scalar(
                out=q_h[:P, 0:nb], in0=q_s[:P, 0:nb], scalar1=0.5,
                scalar2=None, op0=ALU.mult),
                reads=[b_qs], writes=[b_qh], tag="q_h")
            abc = q_a[:P, 0:nb].unsqueeze(2).broadcast_to([P, nb, 16])
            S.emit("dv", lambda e: e.tensor_tensor(
                out=q_u[:P, 0:n].rearrange("p (b k) -> p b k", k=16),
                in0=x3, in1=abc, op=ALU.mult),
                reads=[bx, b_qa], writes=[b_qu], tag="q_u")
            S.emit("ac", lambda e: e.activation(
                out=q_au[:P, 0:n], in_=q_u[:P, 0:n], func=AF.Abs),
                reads=[b_qu], writes=[b_qau], tag="q_au")
            S.emit("dv", lambda e: e.tensor_scalar(
                out=q_f1[:P, 0:n], in0=q_au[:P, 0:n], scalar1=4.0,
                scalar2=1.0, op0=ALU.is_gt, op1=ALU.add),
                reads=[b_qau], writes=[b_qf1], tag="q_f1")
            S.emit("dv", lambda e: e.tensor_scalar(
                out=q_f[:P, 0:n], in0=q_au[:P, 0:n], scalar1=8.0,
                scalar2=1.0, op0=ALU.is_gt, op1=ALU.add),
                reads=[b_qau], writes=[b_qf], tag="q_f2")
            S.emit("dv", lambda e: e.tensor_tensor(
                out=q_f[:P, 0:n], in0=q_f[:P, 0:n], in1=q_f1[:P, 0:n],
                op=ALU.mult),
                reads=[b_qf, b_qf1], writes=[b_qf], tag="q_f")
            S.emit("dv", lambda e: e.tensor_scalar(
                out=q_d[:P, 0:n], in0=q_f1[:P, 0:n], scalar1=-0.5,
                scalar2=1.5, op0=ALU.mult, op1=ALU.add),
                reads=[b_qf1], writes=[b_qd], tag="q_d1")
            S.emit("dv", lambda e: e.tensor_tensor(
                out=q_w[:P, 0:n], in0=q_u[:P, 0:n], in1=q_d[:P, 0:n],
                op=ALU.mult),
                reads=[b_qu, b_qd], writes=[b_qw], tag="q_w1")
            S.emit("dv", lambda e: e.tensor_scalar(
                out=q_d[:P, 0:n], in0=q_au[:P, 0:n], scalar1=8.0,
                scalar2=-0.5, op0=ALU.is_gt, op1=ALU.mult),
                reads=[b_qau], writes=[b_qd], tag="q_d2a")
            S.emit("dv", lambda e: e.tensor_scalar(
                out=q_d[:P, 0:n], in0=q_d[:P, 0:n], scalar1=1.0,
                scalar2=None, op0=ALU.add),
                reads=[b_qd], writes=[b_qd], tag="q_d2b")
            S.emit("dv", lambda e: e.tensor_tensor(
                out=q_w[:P, 0:n], in0=q_w[:P, 0:n], in1=q_d[:P, 0:n],
                op=ALU.mult),
                reads=[b_qw, b_qd], writes=[b_qw], tag="q_w2")
            S.emit("dv", lambda e: e.tensor_scalar(
                out=q_u[:P, 0:n], in0=q_w[:P, 0:n], scalar1=64.5,
                scalar2=None, op0=ALU.add),
                reads=[b_qw], writes=[b_qu], tag="q_yv")
            S.emit("dv", lambda e: e.tensor_copy(
                out=q_ri[:P, 0:n], in_=q_u[:P, 0:n]),
                reads=[b_qu], writes=[b_qri], tag="q_ri")
            S.emit("ac", lambda e: e.copy(
                out=q_w[:P, 0:n], in_=q_ri[:P, 0:n]),
                reads=[b_qri], writes=[b_qw], tag="q_rf")
            S.emit("dv", lambda e: e.tensor_tensor(
                out=q_d[:P, 0:n], in0=q_w[:P, 0:n], in1=q_u[:P, 0:n],
                op=ALU.is_gt),
                reads=[b_qw, b_qu], writes=[b_qd], tag="q_c")
            S.emit("dv", lambda e: e.scalar_tensor_tensor(
                out=q_rb[:P, 0:n], in0=q_w[:P, 0:n], scalar=-64.0,
                in1=q_d[:P, 0:n], op0=ALU.add, op1=ALU.subtract),
                reads=[b_qw, b_qd], writes=[b_qrb], tag="q_fix")
            S.emit("dv", lambda e: e.tensor_tensor(
                out=q_rb[:P, 0:n], in0=q_rb[:P, 0:n], in1=q_f[:P, 0:n],
                op=ALU.mult),
                reads=[b_qrb, b_qf], writes=[b_qrb], tag="q_z")
            hbc = q_h[:P, 0:nb].unsqueeze(2).broadcast_to([P, nb, 16])
            S.emit("dv", lambda e: e.tensor_tensor(
                out=outap.rearrange("p (b k) -> p b k", k=16),
                in0=q_rb[:P, 0:n].rearrange("p (b k) -> p b k", k=16),
                in1=hbc, op=ALU.mult),
                reads=[b_qrb, b_qh], writes=[bout], tag="q_y")

        NJOB = (QSH + 127) // 128

        def wq_job(j):
            pi = j % 2
            r0 = j * 128
            P = min(128, QSH - r0)
            S.emit("sy", lambda e: e.dma_start(
                out=wq_x[pi][:P, :], in_=qw_in[r0:r0 + P, :]),
                writes=[b_wqx[pi]], dma=True, tag=f"wl{j}")
            quant(wq_x[pi][:P, :], b_wqx[pi], 1024, wq_o[:P, :], b_wqo, P=P)
            S.emit("sy", lambda e: e.dma_start(
                out=qw_my[r0:r0 + P, :], in_=wq_o[:P, :]),
                reads=[b_wqo], writes=[b_qwmy], dma=True, tag=f"ws{j}")

        st1 = contextlib.ExitStack()
        x_t = [sbt(st1, f"x_t{i}", [128, D], F32) for i in range(2)]
        xn_t = sbt(st1, "xn_t", [128, D], BF16)
        xn_f = [sbt(st1, f"xn_f{i}", [128, 8, 144], BF16) for i in range(2)]
        h_f = sbt(st1, "h_f", [128, 8, 128], BF16)
        h_acc = sbt(st1, "h_acc", [128, 8, 128], BF16)
        vst = sbt(st1, "vst", [128, D], BF16)
        ksq = sbt(st1, "ksq", [128, 512], BF16)
        bsc = sbt(st1, "bsc", [128, 4], F32)
        b_xt = [Buf(f"x{i}") for i in range(2)]
        b_xnt = Buf("xnt")
        b_xnf = [[Buf(f"xnf{i}_{d}") for d in range(9)] for i in range(2)]
        b_hf = Buf("hf")
        b_hacc = Buf("hacc")
        b_vst = [Buf("vst0"), Buf("vst1")]
        b_ksq = Buf("ksq")
        b_bsc = Buf("bsc")

        ps1 = [pst(st1, f"ps1_{i}", [128, 512], F32) for i in range(7)]
        b_ps1 = [Buf(f"p1{i}") for i in range(7)]

        S.emit("gp", lambda e: e.memset(xn_f[0][:, :, 13:16], 0.0),
               writes=[b_xnf[0][8]], tag="halo0")

        def p1_tile(t):
            pi = t % 2
            own = t >= OWN0
            S.emit("sy", lambda e: e.dma_start(
                out=x_t[pi][:], in_=xseq[t * 128:(t + 1) * 128, :]),
                writes=[b_xt[pi]], dma=True, tag=f"lx{t}")
            S.emit("ac", lambda e: e.activation(
                out=xn_t[:], in_=x_t[pi][:], func=AF.Identity,
                scale=stats[:, t, 1:2], bias=stats[:, t, 0:1]),
                reads=[b_xt[pi], b_stats[0], b_stats[1]], writes=[b_xnt],
                tag=f"ln{t}")
            for dc in range(8):
                eng = "ac" if dc % 2 == 0 else "sy"
                S.emit(eng, lambda e, dc=dc: e.dma_start(
                    out=xn_f[pi][:, dc, 16:144],
                    in_=xn_t[:, dc * 128:(dc + 1) * 128], transpose=True),
                    reads=[b_xnt], writes=[b_xnf[pi][dc]], dma=True,
                    tag=f"tx{t}_{dc}")
            for i in range(4):
                for dc in range(8):
                    if i == 0:
                        S.emit("dv", lambda e, dc=dc: e.tensor_scalar(
                            out=h_acc[:, dc, :], in0=xn_f[pi][:, dc, 13:141],
                            scalar1=convw[:, dc, 0:1], scalar2=None,
                            op0=ALU.mult),
                            reads=[b_xnf[pi][dc], b_xnf[pi][8], b_convw],
                            writes=[b_hacc],
                            tag=f"cv{t}_{dc}_0")
                    else:
                        S.emit("dv", lambda e, dc=dc, i=i:
                               e.scalar_tensor_tensor(
                                   out=h_acc[:, dc, :],
                                   in0=xn_f[pi][:, dc, 13 + i:141 + i],
                                   scalar=convw[:, dc, i:i + 1],
                                   in1=h_acc[:, dc, :],
                                   op0=ALU.mult, op1=ALU.add),
                               reads=[b_xnf[pi][dc], b_xnf[pi][8], b_convw,
                                      b_hacc],
                               writes=[b_hacc], tag=f"cv{t}_{dc}_{i}")
            S.emit("ac", lambda e: e.activation(
                out=h_f[:].rearrange("p a n -> p (a n)"),
                in_=h_acc[:].rearrange("p a n -> p (a n)"), func=AF.Silu),
                reads=[b_hacc], writes=[b_hf], tag=f"si{t}")

            if t + 1 < NCH:
                S.emit("dv", lambda e: e.tensor_copy(
                    out=xn_f[1 - pi][:, :, 13:16],
                    in_=xn_f[pi][:, :, 141:144]),
                    reads=list(b_xnf[pi][0:8]), writes=[b_xnf[1 - pi][8]],
                    tag=f"hl{t}")
            pieces = [(512, 1024, 1), (1024, 1536, 2), (1536, 2048, 3),
                      (3072, 3076, 6)]
            if own:
                pieces = [(0, 512, 0)] + pieces + \
                         [(2048, 2560, 4), (2560, 3072, 5)]
            for dc in range(8):
                for (c0, c1, pb) in pieces:
                    S.emit("te", lambda e, dc=dc, c0=c0, c1=c1, pb=pb:
                           e.matmul(ps1[pb][:, 0:c1 - c0], h_f[:, dc, :],
                                    Wcat[:, dc, c0:c1],
                                    start=(dc == 0), stop=(dc == 7)),
                           reads=[b_hf, b_wcat], writes=[b_ps1[pb]],
                           tag=f"mm{t}_{dc}_{pb}")
            S.emit("ac", lambda e, t=t: e.copy(
                out=k_raw[:, t, :], in_=ps1[1][:]),
                reads=[b_ps1[1]], writes=[b_kraw[t]], tag=f"ek{t}")
            S.emit("ac", lambda e: e.activation(
                out=ksq[:], in_=ps1[1][:], func=AF.Square),
                reads=[b_ps1[1]], writes=[b_ksq], tag=f"k2{t}")
            S.emit("dv", lambda e, t=t: e.reduce_sum(
                out=s2k_a[:, t, :],
                in_=ksq[:].rearrange("p (h n) -> p h n", n=128), axis=AX.X),
                reads=[b_ksq], writes=[b_s2k[t]], tag=f"k3{t}")
            S.emit("dv", lambda e: e.tensor_copy(
                out=vst[:, 0:512], in_=ps1[2][:]),
                reads=[b_ps1[2]], writes=[b_vst[0]], tag=f"ev1{t}")
            S.emit("ac", lambda e: e.copy(
                out=vst[:, 512:1024], in_=ps1[3][:]),
                reads=[b_ps1[3]], writes=[b_vst[1]], tag=f"ev2{t}")
            S.emit("sy", lambda e, t=t: e.dma_start(
                out=v_h[t], in_=vst[:]),
                reads=[b_vst[0], b_vst[1]], writes=[b_vh[t]], dma=True,
                tag=f"sv{t}")
            S.emit("ac", lambda e: e.activation(
                out=bsc[:], in_=ps1[6][:, 0:4], func=AF.Tanh, scale=0.5),
                reads=[b_ps1[6]], writes=[b_bsc], tag=f"bt{t}")
            S.emit("dv", lambda e, t=t: e.tensor_scalar(
                out=beta_a[:, t, :], in0=bsc[:], scalar1=0.5, scalar2=0.5,
                op0=ALU.mult, op1=ALU.add),
                reads=[b_bsc], writes=[b_beta[t]], tag=f"bs{t}")

            if own:
                tt = t - OWN0
                S.emit("ac", lambda e, t=t: e.copy(
                    out=q_raw[:, t - OWN0, :], in_=ps1[0][:]),
                    reads=[b_ps1[0]], writes=[b_qraw[tt]], tag=f"eq{t}")
                S.emit("ac", lambda e: e.activation(
                    out=ksq[:], in_=ps1[0][:], func=AF.Square),
                    reads=[b_ps1[0]], writes=[b_ksq], tag=f"q2{t}")
                S.emit("dv", lambda e, tt=tt: e.reduce_sum(
                    out=s2q_a[:, tt, :],
                    in_=ksq[:].rearrange("p (h n) -> p h n", n=128),
                    axis=AX.X),
                    reads=[b_ksq], writes=[b_s2q[tt]], tag=f"q3{t}")
                S.emit("ac", lambda e, tt=tt: e.activation(
                    out=g_s[:, tt, 0:512], in_=ps1[4][:], func=AF.Silu),
                    reads=[b_ps1[4]], writes=[b_gs[tt]], tag=f"eg1{t}")
                S.emit("ac", lambda e, tt=tt: e.activation(
                    out=g_s[:, tt, 512:1024], in_=ps1[5][:], func=AF.Silu),
                    reads=[b_ps1[5]], writes=[b_gs[tt]], tag=f"eg2{t}")

        wq_pos = [0]
        for t in range(NCH):
            p1_tile(t)
            tgt = (t + 1) * NJOB // NCH
            while wq_pos[0] < tgt:
                wq_job(wq_pos[0])
                wq_pos[0] += 1

        st1.close()
        stW.close()
        S.fence()

        if USE_COLLECTIVE:
            S.emit("gp", lambda e: e.collective_compute(
                "AllGather", ALU.bypass,
                ins=[qw_my[:]], outs=[qw_all[:]],
                replica_groups=[list(range(8))]),
                reads=[b_qwmy], writes=[b_qwall], tag="ag")
        else:
            b_qwall.w = list(b_qwmy.w)

        st2 = contextlib.ExitStack()
        def hpos(h):
            return (h % 2) * 2 + (h // 2)
        kn_t = sbt(st2, "kn_t", [128, NH, 128], BF16)
        kn_f = sbt(st2, "kn_f", [128, NH, 128], BF16)
        kbn_f = sbt(st2, "kbn_f", [128, NH, 128], BF16)
        kbn_t = sbt(st2, "kbn_t", [128, NH, 128], BF16)
        qn_t = sbt(st2, "qn_t", [128, NH, 128], BF16)
        qn_f = sbt(st2, "qn_f", [128, NH, 128], BF16)
        v_t = [sbt(st2, f"v_t{i}", [128, D], BF16) for i in range(2)]
        vb_t = sbt(st2, "vb_t", [128, NH, 256], BF16)
        Xp = [sbt(st2, f"Xp{i}", [128, NH, 128], BF16) for i in range(2)]
        XpT = [sbt(st2, f"XpT{i}", [128, NH, 128], BF16) for i in range(2)]
        Rsn = sbt(st2, "Rsn", [128, NH, 256], BF16)
        U_t = sbt(st2, "U_t", [128, NH, 256], BF16)
        Pt_s = sbt(st2, "Pt_s", [128, NH, 128], BF16)
        settle = sbt(st2, "settle", [128, 64], BF16)
        osb = sbt(st2, "osb", [128, 512], BF16)
        b_osb = Buf("osb")
        b_settle = Buf("settle")
        S_ma = sbt(st2, "S_ma", [128, NH, 256], F32)
        b_Sma = [Buf(f"Sma{i}") for i in range(2)]
        inv_k = sbt(st2, "inv_k", [128, 4], F32)
        inv_q = sbt(st2, "inv_q", [128, 4], F32)
        nbk = sbt(st2, "nbk", [128, 4], F32)
        invr = sbt(st2, "invr", [128, 4], F32)
        osq = sbt(st2, "osq", [128, 512], BF16)
        r2 = sbt(st2, "r2", [128, 4], F32)
        gat = sbt(st2, "gat", [128, D], BF16)
        b_knt, b_kbnt, b_qnt = (Buf(n) for n in ("knt", "kbnt", "qnt"))
        b_knf = [Buf(f"knf{h}") for h in range(NH)]
        b_kbnf = [Buf(f"kbnf{h}") for h in range(NH)]
        b_qnf = [Buf(f"qnf{h}") for h in range(NH)]
        b_vt = [Buf(f"vt{i}") for i in range(2)]
        b_vbt = Buf("vbt")
        b_Xp = [Buf(f"Xp{i}") for i in range(2)]
        b_XpT = [Buf(f"XpT{i}") for i in range(2)]
        b_Rsn = [Buf("Rsn0"), Buf("Rsn1")]
        b_Ut = [Buf("Ut0"), Buf("Ut1")]
        b_Pts = Buf("Pts")
        b_invk, b_invq, b_nbk, b_invr = \
            (Buf(n) for n in ("invk", "invq", "nbk", "invr"))
        b_osq = Buf("osq")
        b_r2 = Buf("r2")
        b_gat = [Buf(f"gat{h}") for h in range(NH)]

        S.emit("gp", lambda e: e.memset(S_ma[:], 0.0),
               writes=list(b_Sma), tag="Sma0")
        S_ps = [pst(st2, f"S_ps{i}", [128, 512], F32) for i in range(2)]
        R_ps = [pst(st2, f"R_ps{i}", [128, 512], F32) for i in range(2)]
        P_ps = [pst(st2, f"P_ps{i}", [128, 512], F32) for i in range(2)]
        o_ps = [pst(st2, f"o_ps{i}", [128, 512], F32) for i in range(2)]
        b_Sps = [Buf(f"Sps{i}") for i in range(2)]
        b_Rps = [Buf(f"Rps{i}") for i in range(2)]
        b_Pps = [Buf(f"Pps{i}") for i in range(2)]
        b_ops = [Buf(f"ops{i}") for i in range(2)]

        def s_slot(h):
            return S_ps[h % 2][:, (h // 2) * 256:(h // 2) * 256 + 256]

        def r_slot(h):
            return R_ps[h % 2][:, (h // 2) * 256:(h // 2) * 256 + 256]

        def o_slot(h):
            return o_ps[h % 2][:, (h // 2) * 256:(h // 2) * 256 + 256]

        def prod_mm(pbank, h, lhsT, rhs, reads, tag):
            S.emit("te", lambda e: e.matmul(
                P_ps[pbank][:, h * 128:(h + 1) * 128], lhsT, rhs,
                start=True, stop=True),
                reads=reads, writes=[b_Pps[pbank]], tag=tag)

        def sc_chunk(c):
            own = c >= OWN0
            tt = c - OWN0
            pi = c % 2
            S.emit("sy", lambda e, c=c: e.dma_start(
                out=v_t[pi][:], in_=v_h[c]),
                reads=[b_vh[c]], writes=[b_vt[pi]], dma=True, tag=f"lv{c}")
            S.emit("dv", lambda e, c=c: e.tensor_scalar(
                out=inv_k[:], in0=s2k_a[:, c, :], scalar1=1e-24, scalar2=None,
                op0=ALU.max), reads=[b_s2k[c]], writes=[b_invk], tag=f"ik{c}")
            S.emit("dv", lambda e: e.reciprocal(out=inv_k[:], in_=inv_k[:]),
                   reads=[b_invk], writes=[b_invk], tag=f"ik2{c}")
            S.emit("ac", lambda e: e.activation(
                out=inv_k[:], in_=inv_k[:], func=AF.Sqrt),
                reads=[b_invk], writes=[b_invk], tag=f"ik3{c}")
            S.emit("dv", lambda e, c=c: e.tensor_tensor(
                out=nbk[:], in0=inv_k[:], in1=beta_a[:, c, :], op=ALU.mult),
                reads=[b_invk, b_beta[c]], writes=[b_nbk], tag=f"nb{c}")
            if own:
                S.emit("dv", lambda e, tt=tt: e.tensor_scalar(
                    out=inv_q[:], in0=s2q_a[:, tt, :], scalar1=1e-24,
                    scalar2=None, op0=ALU.max),
                    reads=[b_s2q[tt]], writes=[b_invq], tag=f"iq{c}")
                S.emit("dv", lambda e: e.reciprocal(out=inv_q[:],
                                                    in_=inv_q[:]),
                       reads=[b_invq], writes=[b_invq], tag=f"iq2{c}")
                S.emit("ac", lambda e: e.activation(
                    out=inv_q[:], in_=inv_q[:], func=AF.Sqrt),
                    reads=[b_invq], writes=[b_invq], tag=f"iq3{c}")
            for h in range(NH):
                S.emit("dv", lambda e, c=c, h=h: e.tensor_scalar(
                    out=kn_t[:, h, :], in0=k_raw[:, c, h * 128:(h + 1) * 128],
                    scalar1=inv_k[:, h:h + 1], scalar2=None, op0=ALU.mult),
                    reads=[b_kraw[c], b_invk], writes=[b_knt], tag=f"kn{c}{h}")
                S.emit("dv", lambda e, c=c, h=h: e.tensor_scalar(
                    out=kbn_t[:, h, :],
                    in0=k_raw[:, c, h * 128:(h + 1) * 128],
                    scalar1=nbk[:, h:h + 1], scalar2=-1.0,
                    op0=ALU.mult, op1=ALU.mult),
                    reads=[b_kraw[c], b_nbk], writes=[b_kbnt], tag=f"kb{c}{h}")
                S.emit("dv", lambda e, c=c, h=h: e.tensor_scalar(
                    out=vb_t[:, h, :],
                    in0=v_t[pi][:, h * 256:(h + 1) * 256],
                    scalar1=beta_a[:, c, h:h + 1], scalar2=None,
                    op0=ALU.mult),
                    reads=[b_vt[pi], b_beta[c]], writes=[b_vbt],
                    tag=f"vb{c}{h}")
                if own:
                    S.emit("dv", lambda e, tt=tt, h=h: e.tensor_scalar(
                        out=qn_t[:, h, :],
                        in0=q_raw[:, tt, h * 128:(h + 1) * 128],
                        scalar1=inv_q[:, h:h + 1], scalar2=None,
                        op0=ALU.mult),
                        reads=[b_qraw[tt], b_invq], writes=[b_qnt],
                        tag=f"qn{c}{h}")
            for h in range(NH):
                S.emit("ac", lambda e, h=h: e.dma_start(
                    out=kn_f[:, h, :], in_=kn_t[:, h, :], transpose=True),
                    reads=[b_knt], writes=[b_knf[h]], dma=True, tag=f"tk{c}{h}")
                S.emit("sy", lambda e, h=h: e.dma_start(
                    out=kbn_f[:, h, :], in_=kbn_t[:, h, :], transpose=True),
                    reads=[b_kbnt], writes=[b_kbnf[h]], dma=True, tag=f"tb{c}{h}")
                if own:
                    S.emit("ac", lambda e, h=h: e.dma_start(
                        out=qn_f[:, h, :], in_=qn_t[:, h, :], transpose=True),
                        reads=[b_qnt], writes=[b_qnf[h]], dma=True,
                        tag=f"tq{c}{h}")
            for h in range(NH):
                prod_mm(0, h, kbn_f[:, h, :], kn_f[:, h, :],
                        [b_kbnf[h], b_knf[h]], f"X{c}{h}")
            S.emit("dv", lambda e: e.tensor_tensor(
                out=Xp[0][:].rearrange("p a n -> p (a n)"),
                in0=P_ps[0][:, 0:512], in1=maskSL[:], op=ALU.mult),
                reads=[b_Pps[0], b_maskSL], writes=[b_Xp[0]], tag=f"eX{c}")
            for h in range(NH):
                prod_mm(1, h, kn_f[:, h, :], kbn_f[:, h, :],
                        [b_knf[h], b_kbnf[h]], f"XT{c}{h}")
            S.emit("dv", lambda e: e.tensor_tensor(
                out=XpT[0][:].rearrange("p a n -> p (a n)"),
                in0=P_ps[1][:, 0:512], in1=maskSU[:], op=ALU.mult),
                reads=[b_Pps[1], b_maskSU], writes=[b_XpT[0]], tag=f"eXT{c}")
            for h in range(NH):
                S.emit("te", lambda e, h=h: e.matmul(
                    r_slot(h), i_sb[:], vb_t[:, h, :],
                    start=(h < 2), stop=False, skip_group_check=True),
                    reads=[b_isb, b_vbt], writes=[b_Rps[h % 2]],
                    tag=f"Ri{c}{h}")
            for h in range(NH):
                S.emit("te", lambda e, h=h: e.matmul(
                    r_slot(h), kbn_f[:, h, :], S_bf[:, hpos(h), :],
                    start=False, stop=(h >= 2), skip_group_check=True),
                    reads=[b_kbnf[h], b_Sbf[hpos(h)]], writes=[b_Rps[h % 2]],
                    tag=f"Rs{c}{h}")
            for lev in range(NLEV):
                cur = lev % 2
                S.emit("dv", lambda e: e.tensor_copy(
                    out=Rsn[:, 0:2, :].rearrange("p a n -> p (a n)"),
                    in_=R_ps[0][:]),
                    reads=[b_Rps[0]], writes=[b_Rsn[0]], tag=f"sn{c}_{lev}a")
                S.emit("dv", lambda e: e.tensor_copy(
                    out=Rsn[:, 2:4, :].rearrange("p a n -> p (a n)"),
                    in_=R_ps[1][:]),
                    reads=[b_Rps[1]], writes=[b_Rsn[1]], tag=f"sn{c}_{lev}b")
                if lev < NLEV - 1:
                    nxt = 1 - cur
                    last = (lev == NLEV - 2)
                    if not last:
                        for h in range(NH):
                            prod_mm(0, h, XpT[cur][:, h, :], Xp[cur][:, h, :],
                                    [b_XpT[cur], b_Xp[cur]], f"sq{c}_{lev}{h}")
                        S.emit("dv", lambda e, nxt=nxt: e.tensor_copy(
                            out=Xp[nxt][:].rearrange("p a n -> p (a n)"),
                            in_=P_ps[0][:, 0:512]),
                            reads=[b_Pps[0]], writes=[b_Xp[nxt]],
                            tag=f"esq{c}_{lev}")
                    for h in range(NH):
                        prod_mm(1, h, Xp[cur][:, h, :], XpT[cur][:, h, :],
                                [b_Xp[cur], b_XpT[cur]], f"sqT{c}_{lev}{h}")
                    S.emit("dv", lambda e, nxt=nxt: e.tensor_copy(
                        out=XpT[nxt][:].rearrange("p a n -> p (a n)"),
                        in_=P_ps[1][:, 0:512]),
                        reads=[b_Pps[1]], writes=[b_XpT[nxt]],
                        tag=f"esqT{c}_{lev}")
                for h in range(NH):
                    S.emit("te", lambda e, h=h, cur=cur: e.matmul(
                        r_slot(h), XpT[cur][:, h, :], Rsn[:, hpos(h), :],
                        start=False, stop=(h >= 2),
                        skip_group_check=True),
                        reads=[b_XpT[cur], b_Rsn[h % 2]],
                        writes=[b_Rps[h % 2]],
                        tag=f"Ra{c}_{lev}{h}")
            S.emit("dv", lambda e: e.tensor_copy(
                out=U_t[:, 0:2, :].rearrange("p a n -> p (a n)"),
                in_=R_ps[0][:]),
                reads=[b_Rps[0]], writes=[b_Ut[0]], tag=f"eU{c}a")
            S.emit("dv", lambda e: e.tensor_copy(
                out=U_t[:, 2:4, :].rearrange("p a n -> p (a n)"),
                in_=R_ps[1][:]),
                reads=[b_Rps[1]], writes=[b_Ut[1]], tag=f"eU{c}b")
            S.emit("dv", lambda e: e.tensor_copy(
                out=settle[:], in_=U_t[:, 0, 0:64]),
                reads=[b_Ut[0], b_Ut[1]], writes=[b_settle], tag=f"stl{c}")
            if own:
                for h in range(NH):
                    prod_mm(0, h, kn_f[:, h, :], qn_f[:, h, :],
                            [b_knf[h], b_qnf[h]], f"Pt{c}{h}")
                S.emit("dv", lambda e: e.tensor_tensor(
                    out=Pt_s[:].rearrange("p a n -> p (a n)"),
                    in0=P_ps[0][:, 0:512], in1=maskLE[:], op=ALU.mult),
                    reads=[b_Pps[0], b_maskLE], writes=[b_Pts], tag=f"eP{c}")
                for h in range(NH):
                    S.emit("te", lambda e, h=h: e.matmul(
                        o_slot(h), qn_f[:, h, :], S_bf[:, hpos(h), :],
                        start=True, stop=True, skip_group_check=True),
                        reads=[b_qnf[h], b_Sbf[hpos(h)]],
                        writes=[b_ops[h % 2]],
                        tag=f"o1{c}{h}")
                for h in range(NH):
                    S.emit("te", lambda e, h=h: e.matmul(
                        s_slot(h), Pt_s[:, h, :], U_t[:, hpos(h), :],
                        start=True, stop=True, skip_group_check=True),
                        reads=[b_Pts, b_Ut[h % 2], b_settle],
                        writes=[b_Sps[h % 2]],
                        tag=f"o2{c}{h}")
                for half in range(2):
                    S.emit("dv", lambda e, half=half: e.tensor_copy(
                        out=osb[:], in_=o_ps[half][:]),
                        reads=[b_ops[half]], writes=[b_osb],
                        tag=f"oe{c}{half}")
                    S.emit("dv", lambda e, half=half: e.scalar_tensor_tensor(
                        out=osb[:], in0=S_ps[half][:], scalar=1.0,
                        in1=osb[:], op0=ALU.mult, op1=ALU.add),
                        reads=[b_Sps[half], b_osb], writes=[b_osb],
                        tag=f"oc{c}{half}")
                    S.emit("ac", lambda e, half=half: e.activation(
                        out=osq[:], in_=osb[:], func=AF.Square),
                        reads=[b_osb], writes=[b_osq], tag=f"os{c}{half}")
                    S.emit("dv", lambda e, half=half: e.reduce_sum(
                        out=r2[:, half * 2:half * 2 + 2],
                        in_=osq[:].rearrange("p (h n) -> p h n", n=256),
                        axis=AX.X),
                        reads=[b_osq], writes=[b_r2], tag=f"or{c}{half}")
                    S.emit("dv", lambda e, half=half: e.tensor_copy(
                        out=gat[:].rearrange("p (a n) -> p a n", n=256)[
                            :, half * 2:half * 2 + 2, :].rearrange(
                            "p a n -> p (a n)"), in_=osb[:]),
                        reads=[b_osb],
                        writes=[b_gat[half], b_gat[half + 2]],
                        tag=f"ob{c}{half}")
                S.emit("dv", lambda e: e.tensor_scalar(
                    out=invr[:], in0=r2[:], scalar1=1.0 / 256.0, scalar2=1e-5,
                    op0=ALU.mult, op1=ALU.add),
                    reads=[b_r2], writes=[b_invr], tag=f"om{c}")
                S.emit("dv", lambda e: e.reciprocal(out=invr[:], in_=invr[:]),
                       reads=[b_invr], writes=[b_invr], tag=f"oc{c}")
                S.emit("ac", lambda e: e.activation(
                    out=invr[:], in_=invr[:], func=AF.Sqrt),
                    reads=[b_invr], writes=[b_invr], tag=f"oq{c}")
                for h in range(NH):
                    S.emit("dv", lambda e, h=h: e.tensor_scalar(
                        out=gat[:, hpos(h) * 256:hpos(h) * 256 + 256],
                        in0=gat[:, hpos(h) * 256:hpos(h) * 256 + 256],
                        scalar1=invr[:, hpos(h):hpos(h) + 1], scalar2=None,
                        op0=ALU.mult),
                        reads=[b_gat[hpos(h)], b_invr],
                        writes=[b_gat[hpos(h)]],
                        tag=f"og{c}{h}")
                for h in range(NH):
                    S.emit("dv", lambda e, h=h, tt=tt: e.tensor_tensor(
                        out=gat[:, hpos(h) * 256:hpos(h) * 256 + 256],
                        in0=gat[:, hpos(h) * 256:hpos(h) * 256 + 256],
                        in1=g_s[:, tt, h * 256:(h + 1) * 256], op=ALU.mult),
                        reads=[b_gat[hpos(h)], b_gs[tt]],
                        writes=[b_gat[hpos(h)]],
                        tag=f"gg{c}{h}")
                S.emit("sy", lambda e, tt=tt: e.dma_start(
                    out=g_h[tt], in_=gat[:]),
                    reads=list(b_gat), writes=[b_gh[tt]], dma=True,
                    tag=f"sg{c}")
            for h in range(NH):
                S.emit("te", lambda e, h=h: e.matmul(
                    s_slot(h), kn_t[:, h, :], U_t[:, hpos(h), :],
                    start=True, stop=True, skip_group_check=True),
                    reads=[b_knt, b_Ut[h % 2], b_settle],
                    writes=[b_Sps[h % 2]],
                    tag=f"Su{c}{h}")
            for half in range(2):
                S.emit("dv", lambda e, half=half: e.scalar_tensor_tensor(
                    out=S_ma[:, half * 2:half * 2 + 2, :].rearrange(
                        "p a n -> p (a n)"),
                    in0=S_ps[half][:], scalar=1.0,
                    in1=S_ma[:, half * 2:half * 2 + 2, :].rearrange(
                        "p a n -> p (a n)"),
                    op0=ALU.mult, op1=ALU.add),
                    reads=[b_Sps[half], b_Sma[half]],
                    writes=[b_Sma[half]],
                    tag=f"aS{c}{half}")
                S.emit("dv", lambda e, half=half: e.tensor_copy(
                    out=S_bf[:, half * 2:half * 2 + 2, :].rearrange(
                        "p a n -> p (a n)"),
                    in_=S_ma[:, half * 2:half * 2 + 2, :].rearrange(
                        "p a n -> p (a n)")),
                    reads=[b_Sma[half]],
                    writes=[b_Sbf[half * 2], b_Sbf[half * 2 + 1]],
                    tag=f"eS{c}{half}")
            if c == 0:
                dbg_dump(0, S_bf[:].rearrange("p a n -> p (a n)"),
                         list(b_Sbf))
            if c == 8:
                dbg_dump(1, S_bf[:].rearrange("p a n -> p (a n)"),
                         list(b_Sbf))

        for c in range(NCH):
            sc_chunk(c)
        st2.close()
        st12.close()
        S.fence()

        st3 = contextlib.ExitStack()
        wgqT = sbt(st3, "wgqT", [128, 8, IF], BF16)
        wuqT = sbt(st3, "wuqT", [128, 8, IF], BF16)
        wdqT = sbt(st3, "wdqT", [128, 22, D], BF16)
        Wo_b = sbt(st3, "Wo_b", [128, 8, D], BF16)
        b_wgqT = [Buf(f"wgqT{d}") for d in range(8)]
        b_wuqT = [Buf(f"wuqT{d}") for d in range(8)]
        b_wdqT = [Buf(f"wdqT{i}") for i in range(22)]
        b_wob = Buf("wob")
        qw_flat = qw_all.rearrange("a b -> (a b)")

        x3_t = [sbt(st3, f"x3_t{i}", [128, D], F32) for i in range(2)]
        b_x3 = [Buf(f"x3{i}") for i in range(2)]
        for dc in range(8):
            S.emit("sy", lambda e, dc=dc: e.dma_start(
                out=x3_t[dc % 2][:],
                in_=w_in["Wo"][dc * 128:(dc + 1) * 128, :].rearrange(
                    "(a p) n -> p (a n)", p=128)),
                writes=[b_x3[dc % 2]], dma=True, tag=f"lwo{dc}")
            S.emit("dv", lambda e, dc=dc: e.tensor_scalar(
                out=Wo_b[:, dc, :], in0=x3_t[dc % 2][:],
                scalar1=nwc[:, dc % 2:dc % 2 + 1], scalar2=None,
                op0=ALU.mult),
                reads=[b_x3[dc % 2], b_nwc], writes=[b_wob], tag=f"wos{dc}")
        for dc in range(8):
            S.emit("sy", lambda e, dc=dc: e.dma_start(
                out=wgqT[:, dc, :],
                in_=qw_all[0:IF, dc * 128:(dc + 1) * 128], transpose=True),
                reads=[b_qwall], writes=[b_wgqT[dc]], dma=True,
                tag=f"lwg{dc}")
            S.emit("sy", lambda e, dc=dc: e.dma_start(
                out=wuqT[:, dc, :],
                in_=qw_all[IF:2 * IF, dc * 128:(dc + 1) * 128],
                transpose=True),
                reads=[b_qwall], writes=[b_wuqT[dc]], dma=True,
                tag=f"lwu{dc}")
        wd_2d = qw_flat[2 * IF * 1024:].rearrange("(a b) -> a b", b=IF)
        for ic in range(22):
            S.emit("ac", lambda e, ic=ic: e.dma_start(
                out=wdqT[:, ic, :],
                in_=wd_2d[:, ic * 128:(ic + 1) * 128], transpose=True),
                reads=[b_qwall], writes=[b_wdqT[ic]], dma=True,
                tag=f"lwd{ic}")

        xn3 = sbt(st3, "xn3", [128, D], F32)
        xq_t = sbt(st3, "xq_t", [128, D], BF16)
        xq_f = sbt(st3, "xq_f", [128, 8, 128], BF16)
        gg_t = [sbt(st3, f"gg_t{i}", [128, D], BF16) for i in range(2)]
        gg_f = sbt(st3, "gg_f", [128, 8, 128], BF16)
        sg_t = sbt(st3, "sg_t", [128, 512], F32)
        h2 = sbt(st3, "h2", [128, 512], F32)
        h2q = sbt(st3, "h2q", [128, IF], BF16)
        h2q_f = sbt(st3, "h2q_f", [128, 22, 128], BF16)
        ot = [sbt(st3, "ot0", [128, D], F32)] * 2
        b_xn3 = Buf("xn3")
        b_xqt = Buf("xqt")
        b_xqf = [Buf(f"xqf{d}") for d in range(8)]
        b_ggt = [Buf(f"ggt{i}") for i in range(2)]
        b_ggf = [Buf(f"ggf{d}") for d in range(8)]
        b_sgt = Buf("sgt")
        b_h2 = Buf("h2")
        b_h2q = Buf("h2q")
        b_h2qf = [Buf(f"h2qf{i}") for i in range(22)]
        _otb = [Buf("ot_0"), Buf("ot_1")]
        b_ot = [_otb, _otb]
        b_outh = [Buf(f"oh{t}") for t in range(16)]

        gu_ps = [pst(st3, f"gu_ps{i}", [128, 512], F32) for i in range(4)]
        dn_ps = [pst(st3, f"dn_ps{i}", [128, 512], F32) for i in range(2)]
        b_gups = [Buf(f"gu{i}") for i in range(4)]
        b_dnps = [Buf(f"dn{i}") for i in range(2)]

        def p3_tile(tt):
            pi = tt % 2
            t = tt + OWN0
            S.emit("sy", lambda e, t=t: e.dma_start(
                out=x3_t[pi][:], in_=xseq[t * 128:(t + 1) * 128, :]),
                writes=[b_x3[pi]], dma=True, tag=f"3x{tt}")
            S.emit("ac", lambda e, t=t: e.activation(
                out=xn3[:], in_=x3_t[pi][:], func=AF.Identity,
                scale=stats[:, t, 1:2], bias=stats[:, t, 0:1]),
                reads=[b_x3[pi], b_stats[0], b_stats[1]], writes=[b_xn3],
                tag=f"3n{tt}")
            quant(xn3[:], b_xn3, 1024, xq_t[:], b_xqt)
            if tt == 0:
                dbg_dump(8, xq_t[:], [b_xqt])
                dbg_dump(9, wgqT[:, 0, 0:1024], [b_wgqT[0]])
                dbg_dump(10, wdqT[:, 0, :], [b_wdqT[0]])
                dbg_dump(11, gg_t[pi][:], [b_ggt[pi]])
            for dc in range(8):
                S.emit("sy", lambda e, dc=dc: e.dma_start(
                    out=xq_f[:, dc, :], in_=xq_t[:, dc * 128:(dc + 1) * 128],
                    transpose=True),
                    reads=[b_xqt], writes=[b_xqf[dc]], dma=True,
                    tag=f"3t{tt}{dc}")
            S.emit("ac", lambda e, tt=tt: e.dma_start(
                out=gg_t[pi][:], in_=g_h[tt]),
                reads=[b_gh[tt]], writes=[b_ggt[pi]], dma=True, tag=f"3g{tt}")
            for dc in range(8):
                S.emit("ac", lambda e, dc=dc: e.dma_start(
                    out=gg_f[:, dc, :],
                    in_=gg_t[pi][:, dc * 128:(dc + 1) * 128], transpose=True),
                    reads=[b_ggt[pi]], writes=[b_ggf[dc]], dma=True,
                    tag=f"3gt{tt}{dc}")
            for dc in range(8):
                for half in range(2):
                    S.emit("te", lambda e, dc=dc, half=half: e.matmul(
                        dn_ps[half][:], gg_f[:, dc, :],
                        Wo_b[:, dc, half * 512:(half + 1) * 512],
                        start=(dc == 0), stop=False, skip_group_check=True),
                        reads=[b_ggf[dc], b_wob], writes=[b_dnps[half]],
                        tag=f"wo{tt}_{dc}_{half}")
            pieces = [(i * 512, min(IF, (i + 1) * 512)) for i in range(6)]
            for pidx, (i0, i1) in enumerate(pieces):
                gb = (pidx % 2) * 2
                ub = gb + 1
                for dc in range(8):
                    S.emit("te", lambda e, dc=dc, i0=i0, i1=i1, gb=gb:
                           e.matmul(gu_ps[gb][:, 0:i1 - i0], xq_f[:, dc, :],
                                    wgqT[:, dc, i0:i1],
                                    start=(dc == 0), stop=(dc == 7)),
                           reads=[b_xqf[dc], b_wgqT[dc]],
                           writes=[b_gups[gb]],
                           tag=f"gm{tt}_{pidx}_{dc}")
                    S.emit("te", lambda e, dc=dc, i0=i0, i1=i1, ub=ub:
                           e.matmul(gu_ps[ub][:, 0:i1 - i0], xq_f[:, dc, :],
                                    wuqT[:, dc, i0:i1],
                                    start=(dc == 0), stop=(dc == 7)),
                           reads=[b_xqf[dc], b_wuqT[dc]],
                           writes=[b_gups[ub]],
                           tag=f"um{tt}_{pidx}_{dc}")
                S.emit("ac", lambda e, i0=i0, i1=i1, gb=gb: e.activation(
                    out=sg_t[:, 0:i1 - i0], in_=gu_ps[gb][:, 0:i1 - i0],
                    func=AF.Silu),
                    reads=[b_gups[gb]], writes=[b_sgt], tag=f"sg{tt}_{pidx}")
                S.emit("dv", lambda e, i0=i0, i1=i1, ub=ub: e.tensor_tensor(
                    out=h2[:, 0:i1 - i0], in0=gu_ps[ub][:, 0:i1 - i0],
                    in1=sg_t[:, 0:i1 - i0], op=ALU.mult),
                    reads=[b_gups[ub], b_sgt], writes=[b_h2],
                    tag=f"h2{tt}_{pidx}")
                quant_piece(h2[:, 0:i1 - i0], b_h2, i1 - i0,
                            h2q[:, i0:i1], b_h2q)
            if tt == 0 and pidx == 0:
                dbg_dump(12, sg_t[:, 0:512], [b_sgt])
                dbg_dump(13, h2q[:, 0:512], [b_h2q])
            for ic in range(22):
                eng = "sy" if ic % 2 == 0 else "ac"
                S.emit(eng, lambda e, ic=ic: e.dma_start(
                    out=h2q_f[:, ic, :],
                    in_=h2q[:, ic * 128:(ic + 1) * 128], transpose=True),
                    reads=[b_h2q], writes=[b_h2qf[ic]], dma=True,
                    tag=f"ht{tt}{ic}")
            for ic in range(22):
                for half in range(2):
                    S.emit("te", lambda e, ic=ic, half=half: e.matmul(
                        dn_ps[half][:], h2q_f[:, ic, :],
                        wdqT[:, ic, half * 512:(half + 1) * 512],
                        start=False, stop=(ic == 21), skip_group_check=True),
                        reads=[b_h2qf[ic], b_wdqT[ic]],
                        writes=[b_dnps[half]],
                        tag=f"dm{tt}_{ic}_{half}")
            if tt == 0:
                dbg_dump(14, gg_f[:].rearrange("p a n -> p (a n)"),
                         list(b_ggf))
            for half in range(2):
                S.emit("dv", lambda e, half=half: e.scalar_tensor_tensor(
                    out=ot[pi][:, half * 512:(half + 1) * 512],
                    in0=dn_ps[half][:], scalar=1.0,
                    in1=x3_t[pi][:, half * 512:(half + 1) * 512],
                    op0=ALU.mult, op1=ALU.add),
                    reads=[b_dnps[half], b_x3[pi]], writes=[b_ot[pi][half]],
                    tag=f"rs{tt}{half}")
            S.emit("sy", lambda e, tt=tt: e.dma_start(
                out=out_h[tt * 128:(tt + 1) * 128, :], in_=ot[pi][:]),
                reads=list(b_ot[pi]), writes=[b_outh[tt]], dma=True,
                tag=f"so{tt}")

        for tt in range(16):
            p3_tile(tt)
        st3.close()
        stQ.close()
        stW.close()

        fin = S.barrier_dep()
        fbuf = Buf("fin")
        fbuf.w = fin
        S.emit("sy", lambda e: e.nop(), reads=[fbuf], tag="fin")

        S.flush(block, sems)
    outer.close()
    return nc


_NC_CACHE = {}


USE = None


def kernel(L0_attn_hidden_states, L0_attn_conv_w, L0_attn_Wq, L0_attn_Wk,
           L0_attn_Wv, L0_attn_Wb, L0_attn_Wg, L0_attn_norm_w, L0_attn_Wo,
           L0_ffn_w_gate, L0_ffn_w_up, L0_ffn_w_down):
    x = np.asarray(L0_attn_hidden_states, dtype=np.float32)
    w = {
        "conv_w": np.ascontiguousarray(L0_attn_conv_w, dtype=np.float32),
        "Wq": np.ascontiguousarray(L0_attn_Wq, dtype=np.float32),
        "Wk": np.ascontiguousarray(L0_attn_Wk, dtype=np.float32),
        "Wv": np.ascontiguousarray(L0_attn_Wv, dtype=np.float32),
        "Wb": np.ascontiguousarray(L0_attn_Wb, dtype=np.float32),
        "Wg": np.ascontiguousarray(L0_attn_Wg, dtype=np.float32),
        "norm_w": np.ascontiguousarray(L0_attn_norm_w, dtype=np.float32),
        "Wo": np.ascontiguousarray(np.concatenate(
            [np.asarray(L0_attn_Wo, np.float32)[0:256],
             np.asarray(L0_attn_Wo, np.float32)[512:768],
             np.asarray(L0_attn_Wo, np.float32)[256:512],
             np.asarray(L0_attn_Wo, np.float32)[768:1024]], axis=0)),
    }
    flat = np.concatenate([
        np.asarray(L0_ffn_w_gate, np.float32).reshape(-1, 1024),
        np.asarray(L0_ffn_w_up, np.float32).reshape(-1, 1024),
        np.asarray(L0_ffn_w_down, np.float32).reshape(-1, 1024)], axis=0)
    if "nc" not in _NC_CACHE:
        _NC_CACHE["nc"] = build_program()
    nc = _NC_CACHE["nc"]
    in_maps = []
    for core in range(8):
        b, hf = core // 2, core % 2
        if hf == 0:
            seq = np.concatenate(
                [np.zeros((2048, D), np.float32), x[b, 0:2048]], 0)
        else:
            seq = np.ascontiguousarray(x[b])
        tri = np.tril(np.ones((128, 128), np.float32), -1)
        cns = np.stack([np.tile(tri, (1, 4)),
                        np.tile(tri.T, (1, 4)),
                        np.tile(np.triu(np.ones((128, 128), np.float32)),
                                (1, 4)),
                        np.concatenate([np.eye(128, dtype=np.float32),
                                        np.zeros((128, 384), np.float32)], 1)],
                       axis=1)
        if USE_COLLECTIVE:
            qsh = np.ascontiguousarray(flat[core * QSH:(core + 1) * QSH])
        else:
            qsh = flat
        im = {"xseq": seq, "qw_in": qsh, "consts": cns}
        im.update(w)
        in_maps.append(im)
    res = run_bass_kernel_spmd(nc, in_maps, core_ids=list(range(8)))
    if DBG:
        sys.modules[__name__].dbg_results = [r.get("dbg") for r in res.results]
    out = np.empty((B, L, D), np.float32)
    for core in range(8):
        b, hf = core // 2, core % 2
        out[b, hf * 2048:(hf + 1) * 2048] = res.results[core]["out"]
    return out


# version 17
# speedup vs baseline: 1.5451x; 1.3367x over previous
import contextlib
import sys
import numpy as np
import concourse.bass as bass
import concourse.mybir as mybir
from concourse.bass_utils import run_bass_kernel_spmd

F32 = mybir.dt.float32
BF16 = mybir.dt.bfloat16
I32 = mybir.dt.int32
AF = mybir.ActivationFunctionType
ALU = mybir.AluOpType
AX = mybir.AxisListType

B, L, D = 4, 4096, 1024
NH, DQH, DVH = 4, 128, 256
C = 128
NCH = L // C
OWN0 = NCH // 2
IF = 2816
NLEV = 7
USE_COLLECTIVE = False
SERIALIZE_TE = False
SERIALIZE_ALL = False
SERIALIZE_DMA = True
DBG = False

QROWS = (IF * D * 2 + D * IF) // 1024
QSH = QROWS // 8 if USE_COLLECTIVE else QROWS

ENGS = ("te", "dv", "ac", "gp", "sy")
ENG_SEM = {"te": "TE", "dv": "DV", "ac": "AC", "gp": "GP", "sy": "SY"}
DMA_Q = {"sy": "DS", "ac": "DA", "gp": "DG"}
NROT = 8
DMA_SEMS = {f"{q}{i}" for q in DMA_Q.values() for i in range(NROT)}


class Buf:
    __slots__ = ("w", "r", "name")

    def __init__(self, name=""):
        self.w = []
        self.r = []
        self.name = name


class Op:
    __slots__ = ("eng", "fn", "waits", "sem", "val", "tag", "need_inc")

    def __init__(self, eng, fn, waits, sem, val, tag):
        self.eng, self.fn, self.waits, self.sem, self.val, self.tag = \
            eng, fn, waits, sem, val, tag
        self.need_inc = False


class Sched:

    def __init__(self):
        self.count = {s: 0 for s in list(ENG_SEM.values()) + list(DMA_SEMS)}
        self.eng_pos = {e: 0 for e in ENGS}
        self.dma_uses = {q: 0 for q in DMA_Q.values()}
        self.ops = []
        self.by_sem_val = {}
        self.fence_map = {}
        self.dma_tok = Buf("dma_tok")

    def emit(self, eng, fn, reads=(), writes=(), dma=False, xpose=False,
             tag=""):
        if dma and SERIALIZE_DMA:
            if xpose:
                writes = list(writes) + [self.dma_tok]
            else:
                reads = list(reads) + [self.dma_tok]
        deps = []
        for b in reads:
            deps.extend(b.w)
        for b in writes:
            deps.extend(b.w)
            deps.extend(b.r)
        dmap = dict(self.fence_map)
        for s, v, pos in deps:
            if v > dmap.get(s, -1):
                dmap[s] = v
        if dma:
            q = DMA_Q[eng]
            k = self.dma_uses[q]
            self.dma_uses[q] = k + 1
            sem = f"{q}{k % NROT}"
            prev = self.count[sem]
            if prev > 0 and dmap.get(sem, -1) < prev:
                dmap[sem] = prev
            self.count[sem] += 16
        else:
            sem = ENG_SEM[eng]
            self.count[sem] += 1
        val = self.count[sem]
        waits = sorted(dmap.items())
        op = Op(eng, fn, waits, sem, val, tag)
        self.ops.append(op)
        self.by_sem_val[(sem, val)] = op
        self.eng_pos[eng] += 1
        dep = (sem, val, self.eng_pos[eng])
        for b in reads:
            b.r.append(dep)
        for b in writes:
            b.w = [dep]
            b.r = []
        return dep

    def barrier_dep(self):
        return [(s, v, 10 ** 9) for s, v in self.count.items() if v > 0]

    def fence(self):
        self.fence_map = {s: v for s, v in self.count.items() if v > 0}

    def finalize(self):
        engs = ENGS if SERIALIZE_ALL else (("te",) if SERIALIZE_TE else ())
        for eng in engs:
            sem = ENG_SEM[eng]
            prev = None
            for op in self.ops:
                if op.eng == eng and op.sem == sem:
                    if prev is not None:
                        if dict(op.waits).get(sem, -1) < prev:
                            w = dict(op.waits)
                            w[sem] = prev
                            op.waits = sorted(w.items())
                    prev = op.val
        waited = {}
        for op in self.ops:
            for s, v in op.waits:
                waited.setdefault(s, set()).add(v)
        for op in self.ops:
            ws = waited.get(op.sem)
            if ws and any(w >= op.val for w in ws):
                op.need_inc = True
        per = {e: [] for e in ENGS}
        for op in self.ops:
            per[op.eng].append(op)
        idx = {e: 0 for e in ENGS}
        cur = {s: 0 for s in self.count}
        done, total = 0, len(self.ops)
        while done < total:
            prog = False
            for e in ENGS:
                while idx[e] < len(per[e]):
                    op = per[e][idx[e]]
                    if all(cur[s] >= v for s, v in op.waits):
                        cur[op.sem] = op.val
                        idx[e] += 1
                        done += 1
                        prog = True
                    else:
                        break
            if not prog:
                stuck = []
                for e in ENGS:
                    if idx[e] < len(per[e]):
                        op = per[e][idx[e]]
                        bad = [(s, v, cur[s]) for s, v in op.waits
                               if cur[s] < v]
                        stuck.append((e, op.tag, bad[:3]))
                raise RuntimeError(f"deadlock: {stuck[:4]}")

    def flush(self, block, sems):
        self.finalize()
        per = {e: [] for e in ENGS}
        for op in self.ops:
            per[op.eng].append(op)

        def mk(lst, ename):
            sat = {}

            def f(e):
                for op in lst:
                    for s, v in op.waits:
                        if sat.get(s, -1) >= v:
                            continue
                        e.wait_ge(sems[s], v)
                        sat[s] = v
                    ins = op.fn(e)
                    if op.need_inc:
                        ins.then_inc(sems[op.sem],
                                     16 if op.sem in DMA_SEMS else 1)
            return f
        if per["te"]:
            block.tensor(mk(per["te"], "te"))
        if per["dv"]:
            block.vector(mk(per["dv"], "dv"))
        if per["ac"]:
            block.scalar(mk(per["ac"], "ac"))
        if per["gp"]:
            block.gpsimd(mk(per["gp"], "gp"))
        if per["sy"]:
            block.sync(mk(per["sy"], "sy"))


def build_program():
    nc = bass.Bass("TRN2", target_bir_lowering=False, debug=False)
    S = Sched()

    xseq = nc.dram_tensor("xseq", [L, D], F32, kind="ExternalInput").ap()
    w_in = {}
    for nm, shp in [("conv_w", [D, 4]), ("Wq", [D, 512]), ("Wk", [D, 512]),
                    ("Wv", [D, D]), ("Wb", [D, 4]), ("Wg", [D, D]),
                    ("norm_w", [DVH]), ("Wo", [D, D])]:
        w_in[nm] = nc.dram_tensor(nm, shp, F32, kind="ExternalInput").ap()
    qw_in = nc.dram_tensor("qw_in", [QSH, 1024], F32,
                           kind="ExternalInput").ap()
    consts = nc.dram_tensor("consts", [128, 4, 512], F32,
                            kind="ExternalInput").ap()
    out_h = nc.dram_tensor("out", [2048, D], F32, kind="ExternalOutput").ap()
    if DBG:
        dbg_h = nc.dram_tensor("dbg", [16, 128, 1536], BF16,
                               kind="ExternalOutput").ap()
        b_dbg = Buf("dbg")
        dbg_stk = contextlib.ExitStack()
        dbg_stage = dbg_stk.enter_context(
            nc.sbuf_tensor("dbg_st", [128, 1536], BF16))
        b_dbgst = Buf("dbgst")

        def dbg_dump(slot, ap, rbufs, eng="sy"):
            w = ap.shape[-1]
            S.emit("dv", lambda e: e.tensor_copy(
                out=dbg_stage[:, 0:w], in_=ap),
                reads=list(rbufs), writes=[b_dbgst], tag=f"dbgc{slot}")
            S.emit("sy", lambda e: e.dma_start(
                out=dbg_h[slot, :, 0:w], in_=dbg_stage[:, 0:w]),
                reads=[b_dbgst], writes=[b_dbg], dma=True,
                tag=f"dbg{slot}")
    else:
        def dbg_dump(slot, ap, rbufs, eng="sy"):
            pass

    v_h = nc.dram_tensor("v_sp", [NCH, 128, D], BF16).ap()
    g_h = nc.dram_tensor("g_sp", [16, 128, D], BF16).ap()
    if USE_COLLECTIVE:
        qw_my = nc.dram_tensor("qw_my", [QSH, 1024], BF16).ap()
        qw_all = nc.dram_tensor("qw_all", [QROWS, 1024], BF16,
                                addr_space="Local").ap()
    else:
        qw_my = nc.dram_tensor("qw_my", [QROWS, 1024], BF16).ap()
        qw_all = qw_my

    outer = contextlib.ExitStack()
    sems = {}
    for s in list(ENG_SEM.values()) + list(DMA_SEMS):
        sems[s] = outer.enter_context(nc.semaphore(s))

    def sbt(stk, name, shape, dt):
        return stk.enter_context(nc.sbuf_tensor(name, shape, dt))

    def pst(stk, name, shape, dt):
        return stk.enter_context(nc.psum_tensor(name, shape, dt))

    maskSL = sbt(outer, "maskSL", [128, 512], BF16)
    maskSU = sbt(outer, "maskSU", [128, 512], BF16)
    maskLE = sbt(outer, "maskLE", [128, 512], BF16)
    i_sb = sbt(outer, "i_sb", [128, 128], BF16)
    convw = sbt(outer, "convw", [128, 8, 4], F32)
    nwc = sbt(outer, "nwc", [128, 2], F32)
    stats = sbt(outer, "stats", [128, NCH, 2], F32)
    stQ = contextlib.ExitStack()
    QW = 512
    q_m = sbt(stQ, "q_m", [128, 32], F32)
    q_s = sbt(stQ, "q_s", [128, 32], F32)
    q_a = sbt(stQ, "q_a", [128, 32], F32)
    q_h = sbt(stQ, "q_h", [128, 32], F32)
    q_u = sbt(stQ, "q_u", [128, QW], F32)
    q_au = sbt(stQ, "q_au", [128, QW], BF16)
    q_f1 = sbt(stQ, "q_f1", [128, QW], BF16)
    q_f = sbt(stQ, "q_f", [128, QW], BF16)
    q_d = sbt(stQ, "q_d", [128, QW], BF16)
    q_w = sbt(stQ, "q_w", [128, QW], F32)
    q_ri = sbt(stQ, "q_ri", [128, QW], I32)
    q_rb = sbt(stQ, "q_rb", [128, QW], BF16)
    (b_qm, b_qs, b_qa, b_qh, b_qu, b_qau, b_qf1, b_qf, b_qd, b_qw,
     b_qri, b_qrb) = (Buf(n) for n in
                      ("qm", "qs", "qa", "qh", "qu", "qau", "qf1", "qf",
                       "qd", "qw", "qri", "qrb"))
    st12 = contextlib.ExitStack()
    beta_a = sbt(st12, "beta_a", [128, NCH, 4], F32)
    s2k_a = sbt(st12, "s2k_a", [128, NCH, 4], F32)
    s2q_a = sbt(st12, "s2q_a", [128, 16, 4], F32)
    k_raw = sbt(st12, "k_raw", [128, NCH, 512], BF16)
    q_raw = sbt(st12, "q_raw", [128, 16, 512], BF16)
    g_s = sbt(st12, "g_s", [128, 16, D], BF16)
    S_bf = sbt(st12, "S_bf", [128, NH, 256], BF16)

    b_maskSL, b_maskSU, b_maskLE, b_isb, b_convw, b_nwc = \
        (Buf(n) for n in ("mSL", "mSU", "mLE", "isb", "cw", "nwc"))
    b_stats = [Buf(f"st{t}") for t in range(NCH)]
    b_beta = [Buf(f"be{c}") for c in range(NCH)]
    b_s2k = [Buf(f"sk{c}") for c in range(NCH)]
    b_s2q = [Buf(f"sq{c}") for c in range(16)]
    b_kraw = [Buf(f"kr{c}") for c in range(NCH)]
    b_qraw = [Buf(f"qr{c}") for c in range(16)]
    b_gs = [Buf(f"gs{c}") for c in range(16)]
    b_Sbf = [Buf(f"Sb{h}") for h in range(NH)]
    b_vh = [Buf(f"vh{c}") for c in range(NCH)]
    b_gh = [Buf(f"gh{c}") for c in range(16)]
    b_qwmy = Buf("qwmy")
    b_qwall = Buf("qwall")

    with nc.Block() as block:
        S.emit("gp", lambda e: e.dma_start(out=maskSL[:], in_=consts[:, 0, :]),
               writes=[b_maskSL], dma=True, tag="m0")
        S.emit("gp", lambda e: e.dma_start(out=maskSU[:], in_=consts[:, 1, :]),
               writes=[b_maskSU], dma=True, tag="m1")
        S.emit("gp", lambda e: e.dma_start(out=maskLE[:], in_=consts[:, 2, :]),
               writes=[b_maskLE], dma=True, tag="m2")
        S.emit("gp", lambda e: e.dma_start(out=i_sb[:],
                                           in_=consts[:, 3, 0:128]),
               writes=[b_isb], dma=True, tag="m3")
        S.emit("gp", lambda e: e.memset(S_bf[:], 0.0),
               writes=list(b_Sbf), tag="S0")
        S.emit("sy", lambda e: e.dma_start(
            out=convw[:], in_=w_in["conv_w"].rearrange("(a p) k -> p a k",
                                                       p=128)),
            writes=[b_convw], dma=True, tag="ldcw")
        S.emit("sy", lambda e: e.dma_start(
            out=nwc[:], in_=w_in["norm_w"].rearrange("(a p) -> p a", p=128),
            allow_slow_non_contiguous=True),
            writes=[b_nwc], dma=True, tag="ldnw")


        stW = contextlib.ExitStack()
        Wcat = sbt(stW, "Wcat", [128, 8, 3076], BF16)
        wq_x = [sbt(stW, f"wq_x{i}", [128, 1024], F32) for i in range(2)]
        wq_o = sbt(stW, "wq_o", [128, 1024], BF16)
        b_wqx = [Buf(f"wqx{i}") for i in range(2)]
        b_wqo = Buf("wqo")
        b_wcat = Buf("wcat")
        for nm, c0, c1 in [("Wq", 0, 512), ("Wk", 512, 1024),
                           ("Wv", 1024, 2048), ("Wg", 2048, 3072),
                           ("Wb", 3072, 3076)]:
            S.emit("gp", lambda e, nm=nm, c0=c0, c1=c1: e.dma_start(
                out=Wcat[:, :, c0:c1],
                in_=w_in[nm].rearrange("(a p) n -> p a n", p=128)),
                writes=[b_wcat], dma=True, tag=f"ld{nm}")

        stP = contextlib.ExitStack()
        xp = [sbt(stP, f"xp{i}", [128, D], F32) for i in range(2)]
        b_xp = [Buf(f"xp{i}") for i in range(2)]
        bns = sbt(stP, "bns", [128, 12], F32)
        bna = sbt(stP, "bna", [128, NCH, 2], F32)
        b_bns = Buf("bns")
        b_bna = [Buf(f"bna{t}") for t in range(NCH)]
        for t in range(NCH):
            pi = t % 2
            S.emit("sy", lambda e, t=t, pi=pi: e.dma_start(
                out=xp[pi][:], in_=xseq[t * 128:(t + 1) * 128, :]),
                writes=[b_xp[pi]], dma=True, tag=f"px{t}")
            S.emit("dv", lambda e, pi=pi: e.bn_stats(
                out=bns[:, 0:6], in_=xp[pi][:, 0:512]),
                reads=[b_xp[pi]], writes=[b_bns], tag=f"pb{t}")
            S.emit("dv", lambda e, pi=pi: e.bn_stats(
                out=bns[:, 6:12], in_=xp[pi][:, 512:1024]),
                reads=[b_xp[pi]], writes=[b_bns], tag=f"pb2{t}")
            S.emit("dv", lambda e, t=t: e.bn_aggr(out=bna[:, t, :],
                                                  in_=bns[:]),
                   reads=[b_bns], writes=[b_bna[t]], tag=f"pa{t}")
        b_all_bna = list(b_bna)
        S.emit("dv", lambda e: e.tensor_scalar(
            out=stats[:, :, 1], in0=bna[:, :, 1], scalar1=1e-5, scalar2=None,
            op0=ALU.add), reads=b_all_bna, writes=[b_stats[0]], tag="pe")
        S.emit("dv", lambda e: e.reciprocal(
            out=stats[:, :, 1], in_=stats[:, :, 1]),
            reads=[b_stats[0]], writes=[b_stats[0]], tag="pr")
        S.emit("ac", lambda e: e.activation(
            out=stats[:, :, 1], in_=stats[:, :, 1], func=AF.Sqrt),
            reads=[b_stats[0]], writes=[b_stats[0]], tag="ps")
        S.emit("dv", lambda e: e.scalar_tensor_tensor(
            out=stats[:, :, 0], in0=bna[:, :, 0], scalar=-1.0,
            in1=stats[:, :, 1], op0=ALU.mult, op1=ALU.mult),
            reads=b_all_bna + [b_stats[0]], writes=[b_stats[1]], tag="pm")
        stP.close()
        S.fence()


        def quant(xap, bx, n, outap, bout, P=128):
            for p0 in range(0, n, QW):
                quant_piece(xap[:, p0:min(n, p0 + QW)], bx,
                            min(n, p0 + QW) - p0,
                            outap[:, p0:min(n, p0 + QW)], bout, P=P)

        def quant_piece(xap, bx, n, outap, bout, P=128):
            nb = n // 16
            x3 = xap.rearrange("p (b k) -> p b k", k=16)
            S.emit("dv", lambda e: e.reduce_max(
                out=q_m[:P, 0:nb], in_=x3, axis=AX.X,
                apply_absolute_value=True),
                reads=[bx], writes=[b_qm], tag="q_red")
            S.emit("dv", lambda e: e.tensor_scalar(
                out=q_s[:P, 0:nb], in0=q_m[:P, 0:nb], scalar1=1.0 / 6.0,
                scalar2=1e-12, op0=ALU.mult, op1=ALU.max),
                reads=[b_qm], writes=[b_qs], tag="q_s")
            S.emit("dv", lambda e: e.reciprocal(
                out=q_a[:P, 0:nb], in_=q_s[:P, 0:nb]),
                reads=[b_qs], writes=[b_qa], tag="q_a")
            S.emit("dv", lambda e: e.tensor_scalar(
                out=q_a[:P, 0:nb], in0=q_a[:P, 0:nb], scalar1=2.0,
                scalar2=None, op0=ALU.mult),
                reads=[b_qa], writes=[b_qa], tag="q_a2")
            S.emit("dv", lambda e: e.tensor_scalar(
                out=q_h[:P, 0:nb], in0=q_s[:P, 0:nb], scalar1=0.5,
                scalar2=None, op0=ALU.mult),
                reads=[b_qs], writes=[b_qh], tag="q_h")
            abc = q_a[:P, 0:nb].unsqueeze(2).broadcast_to([P, nb, 16])
            S.emit("dv", lambda e: e.tensor_tensor(
                out=q_u[:P, 0:n].rearrange("p (b k) -> p b k", k=16),
                in0=x3, in1=abc, op=ALU.mult),
                reads=[bx, b_qa], writes=[b_qu], tag="q_u")
            S.emit("ac", lambda e: e.activation(
                out=q_au[:P, 0:n], in_=q_u[:P, 0:n], func=AF.Abs),
                reads=[b_qu], writes=[b_qau], tag="q_au")
            S.emit("dv", lambda e: e.tensor_scalar(
                out=q_f1[:P, 0:n], in0=q_au[:P, 0:n], scalar1=4.0,
                scalar2=1.0, op0=ALU.is_gt, op1=ALU.add),
                reads=[b_qau], writes=[b_qf1], tag="q_f1")
            S.emit("dv", lambda e: e.tensor_scalar(
                out=q_f[:P, 0:n], in0=q_au[:P, 0:n], scalar1=8.0,
                scalar2=1.0, op0=ALU.is_gt, op1=ALU.add),
                reads=[b_qau], writes=[b_qf], tag="q_f2")
            S.emit("dv", lambda e: e.tensor_tensor(
                out=q_f[:P, 0:n], in0=q_f[:P, 0:n], in1=q_f1[:P, 0:n],
                op=ALU.mult),
                reads=[b_qf, b_qf1], writes=[b_qf], tag="q_f")
            S.emit("dv", lambda e: e.tensor_scalar(
                out=q_d[:P, 0:n], in0=q_f1[:P, 0:n], scalar1=-0.5,
                scalar2=1.5, op0=ALU.mult, op1=ALU.add),
                reads=[b_qf1], writes=[b_qd], tag="q_d1")
            S.emit("dv", lambda e: e.tensor_tensor(
                out=q_w[:P, 0:n], in0=q_u[:P, 0:n], in1=q_d[:P, 0:n],
                op=ALU.mult),
                reads=[b_qu, b_qd], writes=[b_qw], tag="q_w1")
            S.emit("dv", lambda e: e.tensor_scalar(
                out=q_d[:P, 0:n], in0=q_au[:P, 0:n], scalar1=8.0,
                scalar2=-0.5, op0=ALU.is_gt, op1=ALU.mult),
                reads=[b_qau], writes=[b_qd], tag="q_d2a")
            S.emit("dv", lambda e: e.tensor_scalar(
                out=q_d[:P, 0:n], in0=q_d[:P, 0:n], scalar1=1.0,
                scalar2=None, op0=ALU.add),
                reads=[b_qd], writes=[b_qd], tag="q_d2b")
            S.emit("dv", lambda e: e.tensor_tensor(
                out=q_w[:P, 0:n], in0=q_w[:P, 0:n], in1=q_d[:P, 0:n],
                op=ALU.mult),
                reads=[b_qw, b_qd], writes=[b_qw], tag="q_w2")
            S.emit("dv", lambda e: e.tensor_scalar(
                out=q_u[:P, 0:n], in0=q_w[:P, 0:n], scalar1=64.5,
                scalar2=None, op0=ALU.add),
                reads=[b_qw], writes=[b_qu], tag="q_yv")
            S.emit("dv", lambda e: e.tensor_copy(
                out=q_ri[:P, 0:n], in_=q_u[:P, 0:n]),
                reads=[b_qu], writes=[b_qri], tag="q_ri")
            S.emit("ac", lambda e: e.copy(
                out=q_w[:P, 0:n], in_=q_ri[:P, 0:n]),
                reads=[b_qri], writes=[b_qw], tag="q_rf")
            S.emit("dv", lambda e: e.tensor_tensor(
                out=q_d[:P, 0:n], in0=q_w[:P, 0:n], in1=q_u[:P, 0:n],
                op=ALU.is_gt),
                reads=[b_qw, b_qu], writes=[b_qd], tag="q_c")
            S.emit("dv", lambda e: e.scalar_tensor_tensor(
                out=q_rb[:P, 0:n], in0=q_w[:P, 0:n], scalar=-64.0,
                in1=q_d[:P, 0:n], op0=ALU.add, op1=ALU.subtract),
                reads=[b_qw, b_qd], writes=[b_qrb], tag="q_fix")
            S.emit("dv", lambda e: e.tensor_tensor(
                out=q_rb[:P, 0:n], in0=q_rb[:P, 0:n], in1=q_f[:P, 0:n],
                op=ALU.mult),
                reads=[b_qrb, b_qf], writes=[b_qrb], tag="q_z")
            hbc = q_h[:P, 0:nb].unsqueeze(2).broadcast_to([P, nb, 16])
            S.emit("dv", lambda e: e.tensor_tensor(
                out=outap.rearrange("p (b k) -> p b k", k=16),
                in0=q_rb[:P, 0:n].rearrange("p (b k) -> p b k", k=16),
                in1=hbc, op=ALU.mult),
                reads=[b_qrb, b_qh], writes=[bout], tag="q_y")

        NJOB = (QSH + 127) // 128

        def wq_job(j):
            pi = j % 2
            r0 = j * 128
            P = min(128, QSH - r0)
            S.emit("sy", lambda e: e.dma_start(
                out=wq_x[pi][:P, :], in_=qw_in[r0:r0 + P, :]),
                writes=[b_wqx[pi]], dma=True, tag=f"wl{j}")
            quant(wq_x[pi][:P, :], b_wqx[pi], 1024, wq_o[:P, :], b_wqo, P=P)
            S.emit("sy", lambda e: e.dma_start(
                out=qw_my[r0:r0 + P, :], in_=wq_o[:P, :]),
                reads=[b_wqo], writes=[b_qwmy], dma=True, tag=f"ws{j}")

        st1 = contextlib.ExitStack()
        x_t = [sbt(st1, f"x_t{i}", [128, D], F32) for i in range(2)]
        xn_t = sbt(st1, "xn_t", [128, D], BF16)
        xn_f = [sbt(st1, f"xn_f{i}", [128, 8, 144], BF16) for i in range(2)]
        h_f = sbt(st1, "h_f", [128, 8, 128], BF16)
        h_acc = sbt(st1, "h_acc", [128, 8, 128], BF16)
        vst = sbt(st1, "vst", [128, D], BF16)
        ksq = sbt(st1, "ksq", [128, 512], BF16)
        bsc = sbt(st1, "bsc", [128, 4], F32)
        b_xt = [Buf(f"x{i}") for i in range(2)]
        b_xnt = Buf("xnt")
        b_xnf = [[Buf(f"xnf{i}_{d}") for d in range(9)] for i in range(2)]
        b_hf = Buf("hf")
        b_hacc = Buf("hacc")
        b_vst = [Buf("vst0"), Buf("vst1")]
        b_ksq = Buf("ksq")
        b_bsc = Buf("bsc")

        ps1 = [pst(st1, f"ps1_{i}", [128, 512], F32) for i in range(7)]
        b_ps1 = [Buf(f"p1{i}") for i in range(7)]

        S.emit("gp", lambda e: e.memset(xn_f[0][:, :, 13:16], 0.0),
               writes=[b_xnf[0][8]], tag="halo0")

        def p1_tile(t):
            pi = t % 2
            own = t >= OWN0
            S.emit("sy", lambda e: e.dma_start(
                out=x_t[pi][:], in_=xseq[t * 128:(t + 1) * 128, :]),
                writes=[b_xt[pi]], dma=True, tag=f"lx{t}")
            S.emit("ac", lambda e: e.activation(
                out=xn_t[:], in_=x_t[pi][:], func=AF.Identity,
                scale=stats[:, t, 1:2], bias=stats[:, t, 0:1]),
                reads=[b_xt[pi], b_stats[0], b_stats[1]], writes=[b_xnt],
                tag=f"ln{t}")
            S.emit("ac", lambda e: e.dma_start(
                out=xn_f[pi][:, :, 16:144], in_=xn_t[:], transpose=True),
                reads=[b_xnt], writes=list(b_xnf[pi][0:8]), dma=True,
                xpose=True, tag=f"tx{t}")
            for i in range(4):
                for dc in range(8):
                    if i == 0:
                        S.emit("dv", lambda e, dc=dc: e.tensor_scalar(
                            out=h_acc[:, dc, :], in0=xn_f[pi][:, dc, 13:141],
                            scalar1=convw[:, dc, 0:1], scalar2=None,
                            op0=ALU.mult),
                            reads=[b_xnf[pi][dc], b_xnf[pi][8], b_convw],
                            writes=[b_hacc],
                            tag=f"cv{t}_{dc}_0")
                    else:
                        S.emit("dv", lambda e, dc=dc, i=i:
                               e.scalar_tensor_tensor(
                                   out=h_acc[:, dc, :],
                                   in0=xn_f[pi][:, dc, 13 + i:141 + i],
                                   scalar=convw[:, dc, i:i + 1],
                                   in1=h_acc[:, dc, :],
                                   op0=ALU.mult, op1=ALU.add),
                               reads=[b_xnf[pi][dc], b_xnf[pi][8], b_convw,
                                      b_hacc],
                               writes=[b_hacc], tag=f"cv{t}_{dc}_{i}")
            S.emit("ac", lambda e: e.activation(
                out=h_f[:].rearrange("p a n -> p (a n)"),
                in_=h_acc[:].rearrange("p a n -> p (a n)"), func=AF.Silu),
                reads=[b_hacc], writes=[b_hf], tag=f"si{t}")

            if t + 1 < NCH:
                S.emit("dv", lambda e: e.tensor_copy(
                    out=xn_f[1 - pi][:, :, 13:16],
                    in_=xn_f[pi][:, :, 141:144]),
                    reads=list(b_xnf[pi][0:8]), writes=[b_xnf[1 - pi][8]],
                    tag=f"hl{t}")
            pieces = [(512, 1024, 1), (1024, 1536, 2), (1536, 2048, 3),
                      (3072, 3076, 6)]
            if own:
                pieces = [(0, 512, 0)] + pieces + \
                         [(2048, 2560, 4), (2560, 3072, 5)]
            for dc in range(8):
                for (c0, c1, pb) in pieces:
                    S.emit("te", lambda e, dc=dc, c0=c0, c1=c1, pb=pb:
                           e.matmul(ps1[pb][:, 0:c1 - c0], h_f[:, dc, :],
                                    Wcat[:, dc, c0:c1],
                                    start=(dc == 0), stop=(dc == 7)),
                           reads=[b_hf, b_wcat], writes=[b_ps1[pb]],
                           tag=f"mm{t}_{dc}_{pb}")
            S.emit("ac", lambda e, t=t: e.copy(
                out=k_raw[:, t, :], in_=ps1[1][:]),
                reads=[b_ps1[1]], writes=[b_kraw[t]], tag=f"ek{t}")
            S.emit("ac", lambda e: e.activation(
                out=ksq[:], in_=ps1[1][:], func=AF.Square),
                reads=[b_ps1[1]], writes=[b_ksq], tag=f"k2{t}")
            S.emit("dv", lambda e, t=t: e.reduce_sum(
                out=s2k_a[:, t, :],
                in_=ksq[:].rearrange("p (h n) -> p h n", n=128), axis=AX.X),
                reads=[b_ksq], writes=[b_s2k[t]], tag=f"k3{t}")
            S.emit("dv", lambda e: e.tensor_copy(
                out=vst[:, 0:512], in_=ps1[2][:]),
                reads=[b_ps1[2]], writes=[b_vst[0]], tag=f"ev1{t}")
            S.emit("ac", lambda e: e.copy(
                out=vst[:, 512:1024], in_=ps1[3][:]),
                reads=[b_ps1[3]], writes=[b_vst[1]], tag=f"ev2{t}")
            S.emit("sy", lambda e, t=t: e.dma_start(
                out=v_h[t], in_=vst[:]),
                reads=[b_vst[0], b_vst[1]], writes=[b_vh[t]], dma=True,
                tag=f"sv{t}")
            S.emit("ac", lambda e: e.activation(
                out=bsc[:], in_=ps1[6][:, 0:4], func=AF.Tanh, scale=0.5),
                reads=[b_ps1[6]], writes=[b_bsc], tag=f"bt{t}")
            S.emit("dv", lambda e, t=t: e.tensor_scalar(
                out=beta_a[:, t, :], in0=bsc[:], scalar1=0.5, scalar2=0.5,
                op0=ALU.mult, op1=ALU.add),
                reads=[b_bsc], writes=[b_beta[t]], tag=f"bs{t}")

            if own:
                tt = t - OWN0
                S.emit("ac", lambda e, t=t: e.copy(
                    out=q_raw[:, t - OWN0, :], in_=ps1[0][:]),
                    reads=[b_ps1[0]], writes=[b_qraw[tt]], tag=f"eq{t}")
                S.emit("ac", lambda e: e.activation(
                    out=ksq[:], in_=ps1[0][:], func=AF.Square),
                    reads=[b_ps1[0]], writes=[b_ksq], tag=f"q2{t}")
                S.emit("dv", lambda e, tt=tt: e.reduce_sum(
                    out=s2q_a[:, tt, :],
                    in_=ksq[:].rearrange("p (h n) -> p h n", n=128),
                    axis=AX.X),
                    reads=[b_ksq], writes=[b_s2q[tt]], tag=f"q3{t}")
                S.emit("ac", lambda e, tt=tt: e.activation(
                    out=g_s[:, tt, 0:512], in_=ps1[4][:], func=AF.Silu),
                    reads=[b_ps1[4]], writes=[b_gs[tt]], tag=f"eg1{t}")
                S.emit("ac", lambda e, tt=tt: e.activation(
                    out=g_s[:, tt, 512:1024], in_=ps1[5][:], func=AF.Silu),
                    reads=[b_ps1[5]], writes=[b_gs[tt]], tag=f"eg2{t}")

        wq_pos = [0]
        for t in range(NCH):
            p1_tile(t)
            tgt = (t + 1) * NJOB // NCH
            while wq_pos[0] < tgt:
                wq_job(wq_pos[0])
                wq_pos[0] += 1

        st1.close()
        stW.close()
        S.fence()

        if USE_COLLECTIVE:
            S.emit("gp", lambda e: e.collective_compute(
                "AllGather", ALU.bypass,
                ins=[qw_my[:]], outs=[qw_all[:]],
                replica_groups=[list(range(8))]),
                reads=[b_qwmy], writes=[b_qwall], tag="ag")
        else:
            b_qwall.w = list(b_qwmy.w)

        st2 = contextlib.ExitStack()
        def hpos(h):
            return (h % 2) * 2 + (h // 2)
        kn_t = sbt(st2, "kn_t", [128, NH, 128], BF16)
        kn_f = sbt(st2, "kn_f", [128, NH, 128], BF16)
        kbn_f = sbt(st2, "kbn_f", [128, NH, 128], BF16)
        kbn_t = sbt(st2, "kbn_t", [128, NH, 128], BF16)
        qn_t = sbt(st2, "qn_t", [128, NH, 128], BF16)
        qn_f = sbt(st2, "qn_f", [128, NH, 128], BF16)
        v_t = [sbt(st2, f"v_t{i}", [128, D], BF16) for i in range(2)]
        vb_t = sbt(st2, "vb_t", [128, NH, 256], BF16)
        Xp = [sbt(st2, f"Xp{i}", [128, NH, 128], BF16) for i in range(2)]
        XpT = [sbt(st2, f"XpT{i}", [128, NH, 128], BF16) for i in range(2)]
        Rsn = sbt(st2, "Rsn", [128, NH, 256], BF16)
        U_t = sbt(st2, "U_t", [128, NH, 256], BF16)
        Pt_s = sbt(st2, "Pt_s", [128, NH, 128], BF16)
        settle = sbt(st2, "settle", [128, 64], BF16)
        osb = sbt(st2, "osb", [128, 512], BF16)
        b_osb = Buf("osb")
        b_settle = Buf("settle")
        S_ma = sbt(st2, "S_ma", [128, NH, 256], F32)
        b_Sma = [Buf(f"Sma{i}") for i in range(2)]
        inv_k = sbt(st2, "inv_k", [128, 4], F32)
        inv_q = sbt(st2, "inv_q", [128, 4], F32)
        nbk = sbt(st2, "nbk", [128, 4], F32)
        invr = sbt(st2, "invr", [128, 4], F32)
        osq = sbt(st2, "osq", [128, 512], BF16)
        r2 = sbt(st2, "r2", [128, 4], F32)
        gat = sbt(st2, "gat", [128, D], BF16)
        b_knt, b_kbnt, b_qnt = (Buf(n) for n in ("knt", "kbnt", "qnt"))
        b_knf = [Buf(f"knf{h}") for h in range(NH)]
        b_kbnf = [Buf(f"kbnf{h}") for h in range(NH)]
        b_qnf = [Buf(f"qnf{h}") for h in range(NH)]
        b_vt = [Buf(f"vt{i}") for i in range(2)]
        b_vbt = Buf("vbt")
        b_Xp = [Buf(f"Xp{i}") for i in range(2)]
        b_XpT = [Buf(f"XpT{i}") for i in range(2)]
        b_Rsn = [Buf("Rsn0"), Buf("Rsn1")]
        b_Ut = [Buf("Ut0"), Buf("Ut1")]
        b_Pts = Buf("Pts")
        b_invk, b_invq, b_nbk, b_invr = \
            (Buf(n) for n in ("invk", "invq", "nbk", "invr"))
        b_osq = Buf("osq")
        b_r2 = Buf("r2")
        b_gat = [Buf(f"gat{h}") for h in range(NH)]

        S.emit("gp", lambda e: e.memset(S_ma[:], 0.0),
               writes=list(b_Sma), tag="Sma0")
        S_ps = [pst(st2, f"S_ps{i}", [128, 512], F32) for i in range(2)]
        R_ps = [pst(st2, f"R_ps{i}", [128, 512], F32) for i in range(2)]
        P_ps = [pst(st2, f"P_ps{i}", [128, 512], F32) for i in range(2)]
        o_ps = [pst(st2, f"o_ps{i}", [128, 512], F32) for i in range(2)]
        b_Sps = [Buf(f"Sps{i}") for i in range(2)]
        b_Rps = [Buf(f"Rps{i}") for i in range(2)]
        b_Pps = [Buf(f"Pps{i}") for i in range(2)]
        b_ops = [Buf(f"ops{i}") for i in range(2)]

        def s_slot(h):
            return S_ps[h % 2][:, (h // 2) * 256:(h // 2) * 256 + 256]

        def r_slot(h):
            return R_ps[h % 2][:, (h // 2) * 256:(h // 2) * 256 + 256]

        def o_slot(h):
            return o_ps[h % 2][:, (h // 2) * 256:(h // 2) * 256 + 256]

        def prod_mm(pbank, h, lhsT, rhs, reads, tag):
            S.emit("te", lambda e: e.matmul(
                P_ps[pbank][:, h * 128:(h + 1) * 128], lhsT, rhs,
                start=True, stop=True),
                reads=reads, writes=[b_Pps[pbank]], tag=tag)

        def sc_chunk(c):
            own = c >= OWN0
            tt = c - OWN0
            pi = c % 2
            S.emit("sy", lambda e, c=c: e.dma_start(
                out=v_t[pi][:], in_=v_h[c]),
                reads=[b_vh[c]], writes=[b_vt[pi]], dma=True, tag=f"lv{c}")
            S.emit("dv", lambda e, c=c: e.tensor_scalar(
                out=inv_k[:], in0=s2k_a[:, c, :], scalar1=1e-24, scalar2=None,
                op0=ALU.max), reads=[b_s2k[c]], writes=[b_invk], tag=f"ik{c}")
            S.emit("dv", lambda e: e.reciprocal(out=inv_k[:], in_=inv_k[:]),
                   reads=[b_invk], writes=[b_invk], tag=f"ik2{c}")
            S.emit("ac", lambda e: e.activation(
                out=inv_k[:], in_=inv_k[:], func=AF.Sqrt),
                reads=[b_invk], writes=[b_invk], tag=f"ik3{c}")
            S.emit("dv", lambda e, c=c: e.tensor_tensor(
                out=nbk[:], in0=inv_k[:], in1=beta_a[:, c, :], op=ALU.mult),
                reads=[b_invk, b_beta[c]], writes=[b_nbk], tag=f"nb{c}")
            if own:
                S.emit("dv", lambda e, tt=tt: e.tensor_scalar(
                    out=inv_q[:], in0=s2q_a[:, tt, :], scalar1=1e-24,
                    scalar2=None, op0=ALU.max),
                    reads=[b_s2q[tt]], writes=[b_invq], tag=f"iq{c}")
                S.emit("dv", lambda e: e.reciprocal(out=inv_q[:],
                                                    in_=inv_q[:]),
                       reads=[b_invq], writes=[b_invq], tag=f"iq2{c}")
                S.emit("ac", lambda e: e.activation(
                    out=inv_q[:], in_=inv_q[:], func=AF.Sqrt),
                    reads=[b_invq], writes=[b_invq], tag=f"iq3{c}")
            for h in range(NH):
                S.emit("dv", lambda e, c=c, h=h: e.tensor_scalar(
                    out=kn_t[:, h, :], in0=k_raw[:, c, h * 128:(h + 1) * 128],
                    scalar1=inv_k[:, h:h + 1], scalar2=None, op0=ALU.mult),
                    reads=[b_kraw[c], b_invk], writes=[b_knt], tag=f"kn{c}{h}")
                S.emit("dv", lambda e, c=c, h=h: e.tensor_scalar(
                    out=kbn_t[:, h, :],
                    in0=k_raw[:, c, h * 128:(h + 1) * 128],
                    scalar1=nbk[:, h:h + 1], scalar2=-1.0,
                    op0=ALU.mult, op1=ALU.mult),
                    reads=[b_kraw[c], b_nbk], writes=[b_kbnt], tag=f"kb{c}{h}")
                S.emit("dv", lambda e, c=c, h=h: e.tensor_scalar(
                    out=vb_t[:, h, :],
                    in0=v_t[pi][:, h * 256:(h + 1) * 256],
                    scalar1=beta_a[:, c, h:h + 1], scalar2=None,
                    op0=ALU.mult),
                    reads=[b_vt[pi], b_beta[c]], writes=[b_vbt],
                    tag=f"vb{c}{h}")
                if own:
                    S.emit("dv", lambda e, tt=tt, h=h: e.tensor_scalar(
                        out=qn_t[:, h, :],
                        in0=q_raw[:, tt, h * 128:(h + 1) * 128],
                        scalar1=inv_q[:, h:h + 1], scalar2=None,
                        op0=ALU.mult),
                        reads=[b_qraw[tt], b_invq], writes=[b_qnt],
                        tag=f"qn{c}{h}")
            S.emit("ac", lambda e: e.dma_start(
                out=kn_f[:], in_=kn_t[:].rearrange("p a n -> p (a n)"),
                transpose=True),
                reads=[b_knt], writes=list(b_knf), dma=True, xpose=True,
                tag=f"tk{c}")
            S.emit("sy", lambda e: e.dma_start(
                out=kbn_f[:], in_=kbn_t[:].rearrange("p a n -> p (a n)"),
                transpose=True),
                reads=[b_kbnt], writes=list(b_kbnf), dma=True, xpose=True,
                tag=f"tb{c}")
            if own:
                S.emit("ac", lambda e: e.dma_start(
                    out=qn_f[:], in_=qn_t[:].rearrange("p a n -> p (a n)"),
                    transpose=True),
                    reads=[b_qnt], writes=list(b_qnf), dma=True, xpose=True,
                    tag=f"tq{c}")
            for h in range(NH):
                prod_mm(0, h, kbn_f[:, h, :], kn_f[:, h, :],
                        [b_kbnf[h], b_knf[h]], f"X{c}{h}")
            S.emit("dv", lambda e: e.tensor_tensor(
                out=Xp[0][:].rearrange("p a n -> p (a n)"),
                in0=P_ps[0][:, 0:512], in1=maskSL[:], op=ALU.mult),
                reads=[b_Pps[0], b_maskSL], writes=[b_Xp[0]], tag=f"eX{c}")
            for h in range(NH):
                prod_mm(1, h, kn_f[:, h, :], kbn_f[:, h, :],
                        [b_knf[h], b_kbnf[h]], f"XT{c}{h}")
            S.emit("dv", lambda e: e.tensor_tensor(
                out=XpT[0][:].rearrange("p a n -> p (a n)"),
                in0=P_ps[1][:, 0:512], in1=maskSU[:], op=ALU.mult),
                reads=[b_Pps[1], b_maskSU], writes=[b_XpT[0]], tag=f"eXT{c}")
            for h in range(NH):
                S.emit("te", lambda e, h=h: e.matmul(
                    r_slot(h), i_sb[:], vb_t[:, h, :],
                    start=(h < 2), stop=False, skip_group_check=True),
                    reads=[b_isb, b_vbt], writes=[b_Rps[h % 2]],
                    tag=f"Ri{c}{h}")
            for h in range(NH):
                S.emit("te", lambda e, h=h: e.matmul(
                    r_slot(h), kbn_f[:, h, :], S_bf[:, hpos(h), :],
                    start=False, stop=(h >= 2), skip_group_check=True),
                    reads=[b_kbnf[h], b_Sbf[hpos(h)]], writes=[b_Rps[h % 2]],
                    tag=f"Rs{c}{h}")
            for lev in range(NLEV):
                cur = lev % 2
                S.emit("dv", lambda e: e.tensor_copy(
                    out=Rsn[:, 0:2, :].rearrange("p a n -> p (a n)"),
                    in_=R_ps[0][:]),
                    reads=[b_Rps[0]], writes=[b_Rsn[0]], tag=f"sn{c}_{lev}a")
                S.emit("dv", lambda e: e.tensor_copy(
                    out=Rsn[:, 2:4, :].rearrange("p a n -> p (a n)"),
                    in_=R_ps[1][:]),
                    reads=[b_Rps[1]], writes=[b_Rsn[1]], tag=f"sn{c}_{lev}b")
                if lev < NLEV - 1:
                    nxt = 1 - cur
                    last = (lev == NLEV - 2)
                    if not last:
                        for h in range(NH):
                            prod_mm(0, h, XpT[cur][:, h, :], Xp[cur][:, h, :],
                                    [b_XpT[cur], b_Xp[cur]], f"sq{c}_{lev}{h}")
                        S.emit("dv", lambda e, nxt=nxt: e.tensor_copy(
                            out=Xp[nxt][:].rearrange("p a n -> p (a n)"),
                            in_=P_ps[0][:, 0:512]),
                            reads=[b_Pps[0]], writes=[b_Xp[nxt]],
                            tag=f"esq{c}_{lev}")
                    for h in range(NH):
                        prod_mm(1, h, Xp[cur][:, h, :], XpT[cur][:, h, :],
                                [b_Xp[cur], b_XpT[cur]], f"sqT{c}_{lev}{h}")
                    S.emit("dv", lambda e, nxt=nxt: e.tensor_copy(
                        out=XpT[nxt][:].rearrange("p a n -> p (a n)"),
                        in_=P_ps[1][:, 0:512]),
                        reads=[b_Pps[1]], writes=[b_XpT[nxt]],
                        tag=f"esqT{c}_{lev}")
                for h in range(NH):
                    S.emit("te", lambda e, h=h, cur=cur: e.matmul(
                        r_slot(h), XpT[cur][:, h, :], Rsn[:, hpos(h), :],
                        start=False, stop=(h >= 2),
                        skip_group_check=True),
                        reads=[b_XpT[cur], b_Rsn[h % 2]],
                        writes=[b_Rps[h % 2]],
                        tag=f"Ra{c}_{lev}{h}")
            S.emit("dv", lambda e: e.tensor_copy(
                out=U_t[:, 0:2, :].rearrange("p a n -> p (a n)"),
                in_=R_ps[0][:]),
                reads=[b_Rps[0]], writes=[b_Ut[0]], tag=f"eU{c}a")
            S.emit("dv", lambda e: e.tensor_copy(
                out=U_t[:, 2:4, :].rearrange("p a n -> p (a n)"),
                in_=R_ps[1][:]),
                reads=[b_Rps[1]], writes=[b_Ut[1]], tag=f"eU{c}b")
            S.emit("dv", lambda e: e.tensor_copy(
                out=settle[:], in_=U_t[:, 0, 0:64]),
                reads=[b_Ut[0], b_Ut[1]], writes=[b_settle], tag=f"stl{c}")
            if own:
                for h in range(NH):
                    prod_mm(0, h, kn_f[:, h, :], qn_f[:, h, :],
                            [b_knf[h], b_qnf[h]], f"Pt{c}{h}")
                S.emit("dv", lambda e: e.tensor_tensor(
                    out=Pt_s[:].rearrange("p a n -> p (a n)"),
                    in0=P_ps[0][:, 0:512], in1=maskLE[:], op=ALU.mult),
                    reads=[b_Pps[0], b_maskLE], writes=[b_Pts], tag=f"eP{c}")
                for h in range(NH):
                    S.emit("te", lambda e, h=h: e.matmul(
                        o_slot(h), qn_f[:, h, :], S_bf[:, hpos(h), :],
                        start=True, stop=True, skip_group_check=True),
                        reads=[b_qnf[h], b_Sbf[hpos(h)]],
                        writes=[b_ops[h % 2]],
                        tag=f"o1{c}{h}")
                for h in range(NH):
                    S.emit("te", lambda e, h=h: e.matmul(
                        s_slot(h), Pt_s[:, h, :], U_t[:, hpos(h), :],
                        start=True, stop=True, skip_group_check=True),
                        reads=[b_Pts, b_Ut[h % 2], b_settle],
                        writes=[b_Sps[h % 2]],
                        tag=f"o2{c}{h}")
                for half in range(2):
                    S.emit("dv", lambda e, half=half: e.tensor_copy(
                        out=osb[:], in_=o_ps[half][:]),
                        reads=[b_ops[half]], writes=[b_osb],
                        tag=f"oe{c}{half}")
                    S.emit("dv", lambda e, half=half: e.scalar_tensor_tensor(
                        out=osb[:], in0=S_ps[half][:], scalar=1.0,
                        in1=osb[:], op0=ALU.mult, op1=ALU.add),
                        reads=[b_Sps[half], b_osb], writes=[b_osb],
                        tag=f"oc{c}{half}")
                    S.emit("ac", lambda e, half=half: e.activation(
                        out=osq[:], in_=osb[:], func=AF.Square),
                        reads=[b_osb], writes=[b_osq], tag=f"os{c}{half}")
                    S.emit("dv", lambda e, half=half: e.reduce_sum(
                        out=r2[:, half * 2:half * 2 + 2],
                        in_=osq[:].rearrange("p (h n) -> p h n", n=256),
                        axis=AX.X),
                        reads=[b_osq], writes=[b_r2], tag=f"or{c}{half}")
                    S.emit("dv", lambda e, half=half: e.tensor_copy(
                        out=gat[:].rearrange("p (a n) -> p a n", n=256)[
                            :, half * 2:half * 2 + 2, :].rearrange(
                            "p a n -> p (a n)"), in_=osb[:]),
                        reads=[b_osb],
                        writes=[b_gat[half], b_gat[half + 2]],
                        tag=f"ob{c}{half}")
                S.emit("dv", lambda e: e.tensor_scalar(
                    out=invr[:], in0=r2[:], scalar1=1.0 / 256.0, scalar2=1e-5,
                    op0=ALU.mult, op1=ALU.add),
                    reads=[b_r2], writes=[b_invr], tag=f"om{c}")
                S.emit("dv", lambda e: e.reciprocal(out=invr[:], in_=invr[:]),
                       reads=[b_invr], writes=[b_invr], tag=f"oc{c}")
                S.emit("ac", lambda e: e.activation(
                    out=invr[:], in_=invr[:], func=AF.Sqrt),
                    reads=[b_invr], writes=[b_invr], tag=f"oq{c}")
                for h in range(NH):
                    S.emit("dv", lambda e, h=h: e.tensor_scalar(
                        out=gat[:, hpos(h) * 256:hpos(h) * 256 + 256],
                        in0=gat[:, hpos(h) * 256:hpos(h) * 256 + 256],
                        scalar1=invr[:, hpos(h):hpos(h) + 1], scalar2=None,
                        op0=ALU.mult),
                        reads=[b_gat[hpos(h)], b_invr],
                        writes=[b_gat[hpos(h)]],
                        tag=f"og{c}{h}")
                for h in range(NH):
                    S.emit("dv", lambda e, h=h, tt=tt: e.tensor_tensor(
                        out=gat[:, hpos(h) * 256:hpos(h) * 256 + 256],
                        in0=gat[:, hpos(h) * 256:hpos(h) * 256 + 256],
                        in1=g_s[:, tt, h * 256:(h + 1) * 256], op=ALU.mult),
                        reads=[b_gat[hpos(h)], b_gs[tt]],
                        writes=[b_gat[hpos(h)]],
                        tag=f"gg{c}{h}")
                S.emit("sy", lambda e, tt=tt: e.dma_start(
                    out=g_h[tt], in_=gat[:]),
                    reads=list(b_gat), writes=[b_gh[tt]], dma=True,
                    tag=f"sg{c}")
            for h in range(NH):
                S.emit("te", lambda e, h=h: e.matmul(
                    s_slot(h), kn_t[:, h, :], U_t[:, hpos(h), :],
                    start=True, stop=True, skip_group_check=True),
                    reads=[b_knt, b_Ut[h % 2], b_settle],
                    writes=[b_Sps[h % 2]],
                    tag=f"Su{c}{h}")
            for half in range(2):
                S.emit("dv", lambda e, half=half: e.scalar_tensor_tensor(
                    out=S_ma[:, half * 2:half * 2 + 2, :].rearrange(
                        "p a n -> p (a n)"),
                    in0=S_ps[half][:], scalar=1.0,
                    in1=S_ma[:, half * 2:half * 2 + 2, :].rearrange(
                        "p a n -> p (a n)"),
                    op0=ALU.mult, op1=ALU.add),
                    reads=[b_Sps[half], b_Sma[half]],
                    writes=[b_Sma[half]],
                    tag=f"aS{c}{half}")
                S.emit("dv", lambda e, half=half: e.tensor_copy(
                    out=S_bf[:, half * 2:half * 2 + 2, :].rearrange(
                        "p a n -> p (a n)"),
                    in_=S_ma[:, half * 2:half * 2 + 2, :].rearrange(
                        "p a n -> p (a n)")),
                    reads=[b_Sma[half]],
                    writes=[b_Sbf[half * 2], b_Sbf[half * 2 + 1]],
                    tag=f"eS{c}{half}")
            if c == 0:
                dbg_dump(0, S_bf[:].rearrange("p a n -> p (a n)"),
                         list(b_Sbf))
            if c == 8:
                dbg_dump(1, S_bf[:].rearrange("p a n -> p (a n)"),
                         list(b_Sbf))

        for c in range(NCH):
            sc_chunk(c)
        st2.close()
        st12.close()
        S.fence()

        st3 = contextlib.ExitStack()
        wgqT = sbt(st3, "wgqT", [128, 8, IF], BF16)
        wuqT = sbt(st3, "wuqT", [128, 8, IF], BF16)
        wdqT = sbt(st3, "wdqT", [128, 22, D], BF16)
        Wo_b = sbt(st3, "Wo_b", [128, 8, D], BF16)
        b_wgqT = [Buf(f"wgqT{d}") for d in range(8)]
        b_wuqT = [Buf(f"wuqT{d}") for d in range(8)]
        b_wdqT = [Buf(f"wdqT{i}") for i in range(22)]
        b_wob = Buf("wob")
        qw_flat = qw_all.rearrange("a b -> (a b)")

        x3_t = [sbt(st3, f"x3_t{i}", [128, D], F32) for i in range(2)]
        b_x3 = [Buf(f"x3{i}") for i in range(2)]
        for dc in range(8):
            S.emit("sy", lambda e, dc=dc: e.dma_start(
                out=x3_t[dc % 2][:],
                in_=w_in["Wo"][dc * 128:(dc + 1) * 128, :].rearrange(
                    "(a p) n -> p (a n)", p=128)),
                writes=[b_x3[dc % 2]], dma=True, tag=f"lwo{dc}")
            S.emit("dv", lambda e, dc=dc: e.tensor_scalar(
                out=Wo_b[:, dc, :], in0=x3_t[dc % 2][:],
                scalar1=nwc[:, dc % 2:dc % 2 + 1], scalar2=None,
                op0=ALU.mult),
                reads=[b_x3[dc % 2], b_nwc], writes=[b_wob], tag=f"wos{dc}")
        S.emit("sy", lambda e: e.dma_start(
            out=wgqT[:], in_=qw_all[0:IF, 0:1024], transpose=True),
            reads=[b_qwall], writes=list(b_wgqT), dma=True, xpose=True,
            tag="lwg")
        S.emit("sy", lambda e: e.dma_start(
            out=wuqT[:], in_=qw_all[IF:2 * IF, 0:1024], transpose=True),
            reads=[b_qwall], writes=list(b_wuqT), dma=True, xpose=True,
            tag="lwu")
        wd_2d = qw_flat[2 * IF * 1024:].rearrange("(a b) -> a b", b=IF)
        S.emit("ac", lambda e: e.dma_start(
            out=wdqT[:], in_=wd_2d, transpose=True),
            reads=[b_qwall], writes=list(b_wdqT), dma=True, xpose=True,
            tag="lwd")

        xn3 = sbt(st3, "xn3", [128, D], F32)
        xq_t = sbt(st3, "xq_t", [128, D], BF16)
        xq_f = sbt(st3, "xq_f", [128, 8, 128], BF16)
        gg_t = [sbt(st3, f"gg_t{i}", [128, D], BF16) for i in range(2)]
        gg_f = sbt(st3, "gg_f", [128, 8, 128], BF16)
        sg_t = sbt(st3, "sg_t", [128, 512], F32)
        h2 = sbt(st3, "h2", [128, 512], F32)
        h2q = sbt(st3, "h2q", [128, IF], BF16)
        h2q_f = sbt(st3, "h2q_f", [128, 22, 128], BF16)
        ot = [sbt(st3, "ot0", [128, D], F32)] * 2
        b_xn3 = Buf("xn3")
        b_xqt = Buf("xqt")
        b_xqf = [Buf(f"xqf{d}") for d in range(8)]
        b_ggt = [Buf(f"ggt{i}") for i in range(2)]
        b_ggf = [Buf(f"ggf{d}") for d in range(8)]
        b_sgt = Buf("sgt")
        b_h2 = Buf("h2")
        b_h2q = Buf("h2q")
        b_h2qf = [Buf(f"h2qf{i}") for i in range(22)]
        _otb = [Buf("ot_0"), Buf("ot_1")]
        b_ot = [_otb, _otb]
        b_outh = [Buf(f"oh{t}") for t in range(16)]

        gu_ps = [pst(st3, f"gu_ps{i}", [128, 512], F32) for i in range(4)]
        dn_ps = [pst(st3, f"dn_ps{i}", [128, 512], F32) for i in range(2)]
        b_gups = [Buf(f"gu{i}") for i in range(4)]
        b_dnps = [Buf(f"dn{i}") for i in range(2)]

        def p3_tile(tt):
            pi = tt % 2
            t = tt + OWN0
            S.emit("sy", lambda e, t=t: e.dma_start(
                out=x3_t[pi][:], in_=xseq[t * 128:(t + 1) * 128, :]),
                writes=[b_x3[pi]], dma=True, tag=f"3x{tt}")
            S.emit("ac", lambda e, t=t: e.activation(
                out=xn3[:], in_=x3_t[pi][:], func=AF.Identity,
                scale=stats[:, t, 1:2], bias=stats[:, t, 0:1]),
                reads=[b_x3[pi], b_stats[0], b_stats[1]], writes=[b_xn3],
                tag=f"3n{tt}")
            quant(xn3[:], b_xn3, 1024, xq_t[:], b_xqt)
            if tt == 0:
                dbg_dump(8, xq_t[:], [b_xqt])
                dbg_dump(9, wgqT[:, 0, 0:1024], [b_wgqT[0]])
                dbg_dump(10, wdqT[:, 0, :], [b_wdqT[0]])
                dbg_dump(11, gg_t[pi][:], [b_ggt[pi]])
            S.emit("sy", lambda e: e.dma_start(
                out=xq_f[:], in_=xq_t[:], transpose=True),
                reads=[b_xqt], writes=list(b_xqf), dma=True, xpose=True,
                tag=f"3t{tt}")
            S.emit("ac", lambda e, tt=tt: e.dma_start(
                out=gg_t[pi][:], in_=g_h[tt]),
                reads=[b_gh[tt]], writes=[b_ggt[pi]], dma=True, tag=f"3g{tt}")
            S.emit("ac", lambda e: e.dma_start(
                out=gg_f[:], in_=gg_t[pi][:], transpose=True),
                reads=[b_ggt[pi]], writes=list(b_ggf), dma=True, xpose=True,
                tag=f"3gt{tt}")
            for dc in range(8):
                for half in range(2):
                    S.emit("te", lambda e, dc=dc, half=half: e.matmul(
                        dn_ps[half][:], gg_f[:, dc, :],
                        Wo_b[:, dc, half * 512:(half + 1) * 512],
                        start=(dc == 0), stop=False, skip_group_check=True),
                        reads=[b_ggf[dc], b_wob], writes=[b_dnps[half]],
                        tag=f"wo{tt}_{dc}_{half}")
            pieces = [(i * 512, min(IF, (i + 1) * 512)) for i in range(6)]
            for pidx, (i0, i1) in enumerate(pieces):
                gb = (pidx % 2) * 2
                ub = gb + 1
                for dc in range(8):
                    S.emit("te", lambda e, dc=dc, i0=i0, i1=i1, gb=gb:
                           e.matmul(gu_ps[gb][:, 0:i1 - i0], xq_f[:, dc, :],
                                    wgqT[:, dc, i0:i1],
                                    start=(dc == 0), stop=(dc == 7)),
                           reads=[b_xqf[dc], b_wgqT[dc]],
                           writes=[b_gups[gb]],
                           tag=f"gm{tt}_{pidx}_{dc}")
                    S.emit("te", lambda e, dc=dc, i0=i0, i1=i1, ub=ub:
                           e.matmul(gu_ps[ub][:, 0:i1 - i0], xq_f[:, dc, :],
                                    wuqT[:, dc, i0:i1],
                                    start=(dc == 0), stop=(dc == 7)),
                           reads=[b_xqf[dc], b_wuqT[dc]],
                           writes=[b_gups[ub]],
                           tag=f"um{tt}_{pidx}_{dc}")
                S.emit("ac", lambda e, i0=i0, i1=i1, gb=gb: e.activation(
                    out=sg_t[:, 0:i1 - i0], in_=gu_ps[gb][:, 0:i1 - i0],
                    func=AF.Silu),
                    reads=[b_gups[gb]], writes=[b_sgt], tag=f"sg{tt}_{pidx}")
                S.emit("dv", lambda e, i0=i0, i1=i1, ub=ub: e.tensor_tensor(
                    out=h2[:, 0:i1 - i0], in0=gu_ps[ub][:, 0:i1 - i0],
                    in1=sg_t[:, 0:i1 - i0], op=ALU.mult),
                    reads=[b_gups[ub], b_sgt], writes=[b_h2],
                    tag=f"h2{tt}_{pidx}")
                quant_piece(h2[:, 0:i1 - i0], b_h2, i1 - i0,
                            h2q[:, i0:i1], b_h2q)
            if tt == 0 and pidx == 0:
                dbg_dump(12, sg_t[:, 0:512], [b_sgt])
                dbg_dump(13, h2q[:, 0:512], [b_h2q])
            S.emit("sy", lambda e: e.dma_start(
                out=h2q_f[:], in_=h2q[:], transpose=True),
                reads=[b_h2q], writes=list(b_h2qf), dma=True, xpose=True,
                tag=f"ht{tt}")
            for ic in range(22):
                for half in range(2):
                    S.emit("te", lambda e, ic=ic, half=half: e.matmul(
                        dn_ps[half][:], h2q_f[:, ic, :],
                        wdqT[:, ic, half * 512:(half + 1) * 512],
                        start=False, stop=(ic == 21), skip_group_check=True),
                        reads=[b_h2qf[ic], b_wdqT[ic]],
                        writes=[b_dnps[half]],
                        tag=f"dm{tt}_{ic}_{half}")
            if tt == 0:
                dbg_dump(14, gg_f[:].rearrange("p a n -> p (a n)"),
                         list(b_ggf))
            for half in range(2):
                S.emit("dv", lambda e, half=half: e.scalar_tensor_tensor(
                    out=ot[pi][:, half * 512:(half + 1) * 512],
                    in0=dn_ps[half][:], scalar=1.0,
                    in1=x3_t[pi][:, half * 512:(half + 1) * 512],
                    op0=ALU.mult, op1=ALU.add),
                    reads=[b_dnps[half], b_x3[pi]], writes=[b_ot[pi][half]],
                    tag=f"rs{tt}{half}")
            S.emit("sy", lambda e, tt=tt: e.dma_start(
                out=out_h[tt * 128:(tt + 1) * 128, :], in_=ot[pi][:]),
                reads=list(b_ot[pi]), writes=[b_outh[tt]], dma=True,
                tag=f"so{tt}")

        for tt in range(16):
            p3_tile(tt)
        st3.close()
        stQ.close()
        stW.close()

        fin = S.barrier_dep()
        fbuf = Buf("fin")
        fbuf.w = fin
        S.emit("sy", lambda e: e.nop(), reads=[fbuf], tag="fin")

        S.flush(block, sems)
    outer.close()
    return nc


_NC_CACHE = {}


USE = None


def kernel(L0_attn_hidden_states, L0_attn_conv_w, L0_attn_Wq, L0_attn_Wk,
           L0_attn_Wv, L0_attn_Wb, L0_attn_Wg, L0_attn_norm_w, L0_attn_Wo,
           L0_ffn_w_gate, L0_ffn_w_up, L0_ffn_w_down):
    x = np.asarray(L0_attn_hidden_states, dtype=np.float32)
    w = {
        "conv_w": np.ascontiguousarray(L0_attn_conv_w, dtype=np.float32),
        "Wq": np.ascontiguousarray(L0_attn_Wq, dtype=np.float32),
        "Wk": np.ascontiguousarray(L0_attn_Wk, dtype=np.float32),
        "Wv": np.ascontiguousarray(L0_attn_Wv, dtype=np.float32),
        "Wb": np.ascontiguousarray(L0_attn_Wb, dtype=np.float32),
        "Wg": np.ascontiguousarray(L0_attn_Wg, dtype=np.float32),
        "norm_w": np.ascontiguousarray(L0_attn_norm_w, dtype=np.float32),
        "Wo": np.ascontiguousarray(np.concatenate(
            [np.asarray(L0_attn_Wo, np.float32)[0:256],
             np.asarray(L0_attn_Wo, np.float32)[512:768],
             np.asarray(L0_attn_Wo, np.float32)[256:512],
             np.asarray(L0_attn_Wo, np.float32)[768:1024]], axis=0)),
    }
    flat = np.concatenate([
        np.asarray(L0_ffn_w_gate, np.float32).reshape(-1, 1024),
        np.asarray(L0_ffn_w_up, np.float32).reshape(-1, 1024),
        np.asarray(L0_ffn_w_down, np.float32).reshape(-1, 1024)], axis=0)
    if "nc" not in _NC_CACHE:
        _NC_CACHE["nc"] = build_program()
    nc = _NC_CACHE["nc"]
    in_maps = []
    for core in range(8):
        b, hf = core // 2, core % 2
        if hf == 0:
            seq = np.concatenate(
                [np.zeros((2048, D), np.float32), x[b, 0:2048]], 0)
        else:
            seq = np.ascontiguousarray(x[b])
        tri = np.tril(np.ones((128, 128), np.float32), -1)
        cns = np.stack([np.tile(tri, (1, 4)),
                        np.tile(tri.T, (1, 4)),
                        np.tile(np.triu(np.ones((128, 128), np.float32)),
                                (1, 4)),
                        np.concatenate([np.eye(128, dtype=np.float32),
                                        np.zeros((128, 384), np.float32)], 1)],
                       axis=1)
        if USE_COLLECTIVE:
            qsh = np.ascontiguousarray(flat[core * QSH:(core + 1) * QSH])
        else:
            qsh = flat
        im = {"xseq": seq, "qw_in": qsh, "consts": cns}
        im.update(w)
        in_maps.append(im)
    res = run_bass_kernel_spmd(nc, in_maps, core_ids=list(range(8)))
    if DBG:
        sys.modules[__name__].dbg_results = [r.get("dbg") for r in res.results]
    out = np.empty((B, L, D), np.float32)
    for core in range(8):
        b, hf = core // 2, core % 2
        out[b, hf * 2048:(hf + 1) * 2048] = res.results[core]["out"]
    return out


# version 18
# speedup vs baseline: 1.5520x; 1.0045x over previous
import contextlib
import sys
import numpy as np
import concourse.bass as bass
import concourse.mybir as mybir
from concourse.bass_utils import run_bass_kernel_spmd

F32 = mybir.dt.float32
BF16 = mybir.dt.bfloat16
I32 = mybir.dt.int32
AF = mybir.ActivationFunctionType
ALU = mybir.AluOpType
AX = mybir.AxisListType

B, L, D = 4, 4096, 1024
NH, DQH, DVH = 4, 128, 256
C = 128
NCH = L // C
OWN0 = NCH // 2
IF = 2816
NLEV = 7
USE_COLLECTIVE = False
SERIALIZE_TE = False
SERIALIZE_ALL = False
SERIALIZE_DMA = True
DBG = False

QROWS = (IF * D * 2 + D * IF) // 1024
QSH = QROWS // 8 if USE_COLLECTIVE else QROWS

ENGS = ("te", "dv", "ac", "gp", "sy")
ENG_SEM = {"te": "TE", "dv": "DV", "ac": "AC", "gp": "GP", "sy": "SY"}
DMA_Q = {"sy": "DS", "ac": "DA", "gp": "DG"}
NROT = 8
DMA_SEMS = {f"{q}{i}" for q in DMA_Q.values() for i in range(NROT)}


class Buf:
    __slots__ = ("w", "r", "name")

    def __init__(self, name=""):
        self.w = []
        self.r = []
        self.name = name


class Op:
    __slots__ = ("eng", "fn", "waits", "sem", "val", "tag", "need_inc")

    def __init__(self, eng, fn, waits, sem, val, tag):
        self.eng, self.fn, self.waits, self.sem, self.val, self.tag = \
            eng, fn, waits, sem, val, tag
        self.need_inc = False


class Sched:

    def __init__(self):
        self.count = {s: 0 for s in list(ENG_SEM.values()) + list(DMA_SEMS)}
        self.eng_pos = {e: 0 for e in ENGS}
        self.dma_uses = {q: 0 for q in DMA_Q.values()}
        self.ops = []
        self.by_sem_val = {}
        self.fence_map = {}
        self.dma_tok = Buf("dma_tok")

    def emit(self, eng, fn, reads=(), writes=(), dma=False, xpose=False,
             tag=""):
        if dma and SERIALIZE_DMA and xpose:
            writes = list(writes) + [self.dma_tok]
        deps = []
        for b in reads:
            deps.extend(b.w)
        for b in writes:
            deps.extend(b.w)
            deps.extend(b.r)
        dmap = dict(self.fence_map)
        for s, v, pos in deps:
            if v > dmap.get(s, -1):
                dmap[s] = v
        if dma:
            q = DMA_Q[eng]
            k = self.dma_uses[q]
            self.dma_uses[q] = k + 1
            sem = f"{q}{k % NROT}"
            prev = self.count[sem]
            if prev > 0 and dmap.get(sem, -1) < prev:
                dmap[sem] = prev
            self.count[sem] += 16
        else:
            sem = ENG_SEM[eng]
            self.count[sem] += 1
        val = self.count[sem]
        waits = sorted(dmap.items())
        op = Op(eng, fn, waits, sem, val, tag)
        self.ops.append(op)
        self.by_sem_val[(sem, val)] = op
        self.eng_pos[eng] += 1
        dep = (sem, val, self.eng_pos[eng])
        for b in reads:
            b.r.append(dep)
        for b in writes:
            b.w = [dep]
            b.r = []
        return dep

    def barrier_dep(self):
        return [(s, v, 10 ** 9) for s, v in self.count.items() if v > 0]

    def fence(self):
        self.fence_map = {s: v for s, v in self.count.items() if v > 0}

    def finalize(self):
        engs = ENGS if SERIALIZE_ALL else (("te",) if SERIALIZE_TE else ())
        for eng in engs:
            sem = ENG_SEM[eng]
            prev = None
            for op in self.ops:
                if op.eng == eng and op.sem == sem:
                    if prev is not None:
                        if dict(op.waits).get(sem, -1) < prev:
                            w = dict(op.waits)
                            w[sem] = prev
                            op.waits = sorted(w.items())
                    prev = op.val
        waited = {}
        for op in self.ops:
            for s, v in op.waits:
                waited.setdefault(s, set()).add(v)
        for op in self.ops:
            ws = waited.get(op.sem)
            if ws and any(w >= op.val for w in ws):
                op.need_inc = True
        per = {e: [] for e in ENGS}
        for op in self.ops:
            per[op.eng].append(op)
        idx = {e: 0 for e in ENGS}
        cur = {s: 0 for s in self.count}
        done, total = 0, len(self.ops)
        while done < total:
            prog = False
            for e in ENGS:
                while idx[e] < len(per[e]):
                    op = per[e][idx[e]]
                    if all(cur[s] >= v for s, v in op.waits):
                        cur[op.sem] = op.val
                        idx[e] += 1
                        done += 1
                        prog = True
                    else:
                        break
            if not prog:
                stuck = []
                for e in ENGS:
                    if idx[e] < len(per[e]):
                        op = per[e][idx[e]]
                        bad = [(s, v, cur[s]) for s, v in op.waits
                               if cur[s] < v]
                        stuck.append((e, op.tag, bad[:3]))
                raise RuntimeError(f"deadlock: {stuck[:4]}")

    def flush(self, block, sems):
        self.finalize()
        per = {e: [] for e in ENGS}
        for op in self.ops:
            per[op.eng].append(op)

        def mk(lst, ename):
            sat = {}

            def f(e):
                for op in lst:
                    for s, v in op.waits:
                        if sat.get(s, -1) >= v:
                            continue
                        e.wait_ge(sems[s], v)
                        sat[s] = v
                    ins = op.fn(e)
                    if op.need_inc:
                        ins.then_inc(sems[op.sem],
                                     16 if op.sem in DMA_SEMS else 1)
            return f
        if per["te"]:
            block.tensor(mk(per["te"], "te"))
        if per["dv"]:
            block.vector(mk(per["dv"], "dv"))
        if per["ac"]:
            block.scalar(mk(per["ac"], "ac"))
        if per["gp"]:
            block.gpsimd(mk(per["gp"], "gp"))
        if per["sy"]:
            block.sync(mk(per["sy"], "sy"))


def build_program():
    nc = bass.Bass("TRN2", target_bir_lowering=False, debug=False)
    S = Sched()

    xseq = nc.dram_tensor("xseq", [L, D], F32, kind="ExternalInput").ap()
    w_in = {}
    for nm, shp in [("conv_w", [D, 4]), ("Wq", [D, 512]), ("Wk", [D, 512]),
                    ("Wv", [D, D]), ("Wb", [D, 4]), ("Wg", [D, D]),
                    ("norm_w", [DVH]), ("Wo", [D, D])]:
        w_in[nm] = nc.dram_tensor(nm, shp, F32, kind="ExternalInput").ap()
    qw_in = nc.dram_tensor("qw_in", [QSH, 1024], F32,
                           kind="ExternalInput").ap()
    consts = nc.dram_tensor("consts", [128, 4, 512], F32,
                            kind="ExternalInput").ap()
    out_h = nc.dram_tensor("out", [2048, D], F32, kind="ExternalOutput").ap()
    if DBG:
        dbg_h = nc.dram_tensor("dbg", [16, 128, 1536], BF16,
                               kind="ExternalOutput").ap()
        b_dbg = Buf("dbg")
        dbg_stk = contextlib.ExitStack()
        dbg_stage = dbg_stk.enter_context(
            nc.sbuf_tensor("dbg_st", [128, 1536], BF16))
        b_dbgst = Buf("dbgst")

        def dbg_dump(slot, ap, rbufs, eng="sy"):
            w = ap.shape[-1]
            S.emit("dv", lambda e: e.tensor_copy(
                out=dbg_stage[:, 0:w], in_=ap),
                reads=list(rbufs), writes=[b_dbgst], tag=f"dbgc{slot}")
            S.emit("sy", lambda e: e.dma_start(
                out=dbg_h[slot, :, 0:w], in_=dbg_stage[:, 0:w]),
                reads=[b_dbgst], writes=[b_dbg], dma=True,
                tag=f"dbg{slot}")
    else:
        def dbg_dump(slot, ap, rbufs, eng="sy"):
            pass

    v_h = nc.dram_tensor("v_sp", [NCH, 128, D], BF16).ap()
    g_h = nc.dram_tensor("g_sp", [16, 128, D], BF16).ap()
    if USE_COLLECTIVE:
        qw_my = nc.dram_tensor("qw_my", [QSH, 1024], BF16).ap()
        qw_all = nc.dram_tensor("qw_all", [QROWS, 1024], BF16,
                                addr_space="Local").ap()
    else:
        qw_my = nc.dram_tensor("qw_my", [QROWS, 1024], BF16).ap()
        qw_all = qw_my

    outer = contextlib.ExitStack()
    sems = {}
    for s in list(ENG_SEM.values()) + list(DMA_SEMS):
        sems[s] = outer.enter_context(nc.semaphore(s))

    def sbt(stk, name, shape, dt):
        return stk.enter_context(nc.sbuf_tensor(name, shape, dt))

    def pst(stk, name, shape, dt):
        return stk.enter_context(nc.psum_tensor(name, shape, dt))

    maskSL = sbt(outer, "maskSL", [128, 512], BF16)
    maskSU = sbt(outer, "maskSU", [128, 512], BF16)
    maskLE = sbt(outer, "maskLE", [128, 512], BF16)
    i_sb = sbt(outer, "i_sb", [128, 128], BF16)
    convw = sbt(outer, "convw", [128, 8, 4], F32)
    nwc = sbt(outer, "nwc", [128, 2], F32)
    stats = sbt(outer, "stats", [128, NCH, 2], F32)
    stQ = contextlib.ExitStack()
    QW = 512
    q_m = sbt(stQ, "q_m", [128, 32], F32)
    q_s = sbt(stQ, "q_s", [128, 32], F32)
    q_a = sbt(stQ, "q_a", [128, 32], F32)
    q_h = sbt(stQ, "q_h", [128, 32], F32)
    q_u = sbt(stQ, "q_u", [128, QW], F32)
    q_au = sbt(stQ, "q_au", [128, QW], BF16)
    q_f1 = sbt(stQ, "q_f1", [128, QW], BF16)
    q_f = sbt(stQ, "q_f", [128, QW], BF16)
    q_d = sbt(stQ, "q_d", [128, QW], BF16)
    q_w = sbt(stQ, "q_w", [128, QW], F32)
    q_ri = sbt(stQ, "q_ri", [128, QW], I32)
    q_rb = sbt(stQ, "q_rb", [128, QW], BF16)
    (b_qm, b_qs, b_qa, b_qh, b_qu, b_qau, b_qf1, b_qf, b_qd, b_qw,
     b_qri, b_qrb) = (Buf(n) for n in
                      ("qm", "qs", "qa", "qh", "qu", "qau", "qf1", "qf",
                       "qd", "qw", "qri", "qrb"))
    st12 = contextlib.ExitStack()
    beta_a = sbt(st12, "beta_a", [128, NCH, 4], F32)
    s2k_a = sbt(st12, "s2k_a", [128, NCH, 4], F32)
    s2q_a = sbt(st12, "s2q_a", [128, 16, 4], F32)
    k_raw = sbt(st12, "k_raw", [128, NCH, 512], BF16)
    q_raw = sbt(st12, "q_raw", [128, 16, 512], BF16)
    g_s = sbt(st12, "g_s", [128, 16, D], BF16)
    S_bf = sbt(st12, "S_bf", [128, NH, 256], BF16)

    b_maskSL, b_maskSU, b_maskLE, b_isb, b_convw, b_nwc = \
        (Buf(n) for n in ("mSL", "mSU", "mLE", "isb", "cw", "nwc"))
    b_stats = [Buf(f"st{t}") for t in range(NCH)]
    b_beta = [Buf(f"be{c}") for c in range(NCH)]
    b_s2k = [Buf(f"sk{c}") for c in range(NCH)]
    b_s2q = [Buf(f"sq{c}") for c in range(16)]
    b_kraw = [Buf(f"kr{c}") for c in range(NCH)]
    b_qraw = [Buf(f"qr{c}") for c in range(16)]
    b_gs = [Buf(f"gs{c}") for c in range(16)]
    b_Sbf = [Buf(f"Sb{h}") for h in range(NH)]
    b_vh = [Buf(f"vh{c}") for c in range(NCH)]
    b_gh = [Buf(f"gh{c}") for c in range(16)]
    b_qwmy = Buf("qwmy")
    b_qwall = Buf("qwall")

    with nc.Block() as block:
        S.emit("gp", lambda e: e.dma_start(out=maskSL[:], in_=consts[:, 0, :]),
               writes=[b_maskSL], dma=True, tag="m0")
        S.emit("gp", lambda e: e.dma_start(out=maskSU[:], in_=consts[:, 1, :]),
               writes=[b_maskSU], dma=True, tag="m1")
        S.emit("gp", lambda e: e.dma_start(out=maskLE[:], in_=consts[:, 2, :]),
               writes=[b_maskLE], dma=True, tag="m2")
        S.emit("gp", lambda e: e.dma_start(out=i_sb[:],
                                           in_=consts[:, 3, 0:128]),
               writes=[b_isb], dma=True, tag="m3")
        S.emit("gp", lambda e: e.memset(S_bf[:], 0.0),
               writes=list(b_Sbf), tag="S0")
        S.emit("sy", lambda e: e.dma_start(
            out=convw[:], in_=w_in["conv_w"].rearrange("(a p) k -> p a k",
                                                       p=128)),
            writes=[b_convw], dma=True, tag="ldcw")
        S.emit("sy", lambda e: e.dma_start(
            out=nwc[:], in_=w_in["norm_w"].rearrange("(a p) -> p a", p=128),
            allow_slow_non_contiguous=True),
            writes=[b_nwc], dma=True, tag="ldnw")


        stW = contextlib.ExitStack()
        Wcat = sbt(stW, "Wcat", [128, 8, 3076], BF16)
        wq_x = [sbt(stW, f"wq_x{i}", [128, 1024], F32) for i in range(2)]
        wq_o = sbt(stW, "wq_o", [128, 1024], BF16)
        b_wqx = [Buf(f"wqx{i}") for i in range(2)]
        b_wqo = Buf("wqo")
        b_wcat = Buf("wcat")
        for nm, c0, c1 in [("Wq", 0, 512), ("Wk", 512, 1024),
                           ("Wv", 1024, 2048), ("Wg", 2048, 3072),
                           ("Wb", 3072, 3076)]:
            S.emit("gp", lambda e, nm=nm, c0=c0, c1=c1: e.dma_start(
                out=Wcat[:, :, c0:c1],
                in_=w_in[nm].rearrange("(a p) n -> p a n", p=128)),
                writes=[b_wcat], dma=True, tag=f"ld{nm}")

        stP = contextlib.ExitStack()
        xp = [sbt(stP, f"xp{i}", [128, D], F32) for i in range(2)]
        b_xp = [Buf(f"xp{i}") for i in range(2)]
        bns = sbt(stP, "bns", [128, 12], F32)
        bna = sbt(stP, "bna", [128, NCH, 2], F32)
        b_bns = Buf("bns")
        b_bna = [Buf(f"bna{t}") for t in range(NCH)]
        for t in range(NCH):
            pi = t % 2
            S.emit("sy", lambda e, t=t, pi=pi: e.dma_start(
                out=xp[pi][:], in_=xseq[t * 128:(t + 1) * 128, :]),
                writes=[b_xp[pi]], dma=True, tag=f"px{t}")
            S.emit("dv", lambda e, pi=pi: e.bn_stats(
                out=bns[:, 0:6], in_=xp[pi][:, 0:512]),
                reads=[b_xp[pi]], writes=[b_bns], tag=f"pb{t}")
            S.emit("dv", lambda e, pi=pi: e.bn_stats(
                out=bns[:, 6:12], in_=xp[pi][:, 512:1024]),
                reads=[b_xp[pi]], writes=[b_bns], tag=f"pb2{t}")
            S.emit("dv", lambda e, t=t: e.bn_aggr(out=bna[:, t, :],
                                                  in_=bns[:]),
                   reads=[b_bns], writes=[b_bna[t]], tag=f"pa{t}")
        b_all_bna = list(b_bna)
        S.emit("dv", lambda e: e.tensor_scalar(
            out=stats[:, :, 1], in0=bna[:, :, 1], scalar1=1e-5, scalar2=None,
            op0=ALU.add), reads=b_all_bna, writes=[b_stats[0]], tag="pe")
        S.emit("dv", lambda e: e.reciprocal(
            out=stats[:, :, 1], in_=stats[:, :, 1]),
            reads=[b_stats[0]], writes=[b_stats[0]], tag="pr")
        S.emit("ac", lambda e: e.activation(
            out=stats[:, :, 1], in_=stats[:, :, 1], func=AF.Sqrt),
            reads=[b_stats[0]], writes=[b_stats[0]], tag="ps")
        S.emit("dv", lambda e: e.scalar_tensor_tensor(
            out=stats[:, :, 0], in0=bna[:, :, 0], scalar=-1.0,
            in1=stats[:, :, 1], op0=ALU.mult, op1=ALU.mult),
            reads=b_all_bna + [b_stats[0]], writes=[b_stats[1]], tag="pm")
        stP.close()
        S.fence()


        def quant(xap, bx, n, outap, bout, P=128):
            for p0 in range(0, n, QW):
                quant_piece(xap[:, p0:min(n, p0 + QW)], bx,
                            min(n, p0 + QW) - p0,
                            outap[:, p0:min(n, p0 + QW)], bout, P=P)

        def quant_piece(xap, bx, n, outap, bout, P=128):
            nb = n // 16
            x3 = xap.rearrange("p (b k) -> p b k", k=16)
            S.emit("dv", lambda e: e.reduce_max(
                out=q_m[:P, 0:nb], in_=x3, axis=AX.X,
                apply_absolute_value=True),
                reads=[bx], writes=[b_qm], tag="q_red")
            S.emit("dv", lambda e: e.tensor_scalar(
                out=q_s[:P, 0:nb], in0=q_m[:P, 0:nb], scalar1=1.0 / 6.0,
                scalar2=1e-12, op0=ALU.mult, op1=ALU.max),
                reads=[b_qm], writes=[b_qs], tag="q_s")
            S.emit("dv", lambda e: e.reciprocal(
                out=q_a[:P, 0:nb], in_=q_s[:P, 0:nb]),
                reads=[b_qs], writes=[b_qa], tag="q_a")
            S.emit("dv", lambda e: e.tensor_scalar(
                out=q_a[:P, 0:nb], in0=q_a[:P, 0:nb], scalar1=2.0,
                scalar2=None, op0=ALU.mult),
                reads=[b_qa], writes=[b_qa], tag="q_a2")
            S.emit("dv", lambda e: e.tensor_scalar(
                out=q_h[:P, 0:nb], in0=q_s[:P, 0:nb], scalar1=0.5,
                scalar2=None, op0=ALU.mult),
                reads=[b_qs], writes=[b_qh], tag="q_h")
            abc = q_a[:P, 0:nb].unsqueeze(2).broadcast_to([P, nb, 16])
            S.emit("dv", lambda e: e.tensor_tensor(
                out=q_u[:P, 0:n].rearrange("p (b k) -> p b k", k=16),
                in0=x3, in1=abc, op=ALU.mult),
                reads=[bx, b_qa], writes=[b_qu], tag="q_u")
            S.emit("ac", lambda e: e.activation(
                out=q_au[:P, 0:n], in_=q_u[:P, 0:n], func=AF.Abs),
                reads=[b_qu], writes=[b_qau], tag="q_au")
            S.emit("dv", lambda e: e.tensor_scalar(
                out=q_f1[:P, 0:n], in0=q_au[:P, 0:n], scalar1=4.0,
                scalar2=1.0, op0=ALU.is_gt, op1=ALU.add),
                reads=[b_qau], writes=[b_qf1], tag="q_f1")
            S.emit("dv", lambda e: e.tensor_scalar(
                out=q_f[:P, 0:n], in0=q_au[:P, 0:n], scalar1=8.0,
                scalar2=1.0, op0=ALU.is_gt, op1=ALU.add),
                reads=[b_qau], writes=[b_qf], tag="q_f2")
            S.emit("dv", lambda e: e.tensor_tensor(
                out=q_f[:P, 0:n], in0=q_f[:P, 0:n], in1=q_f1[:P, 0:n],
                op=ALU.mult),
                reads=[b_qf, b_qf1], writes=[b_qf], tag="q_f")
            S.emit("dv", lambda e: e.tensor_scalar(
                out=q_d[:P, 0:n], in0=q_f1[:P, 0:n], scalar1=-0.5,
                scalar2=1.5, op0=ALU.mult, op1=ALU.add),
                reads=[b_qf1], writes=[b_qd], tag="q_d1")
            S.emit("dv", lambda e: e.tensor_tensor(
                out=q_w[:P, 0:n], in0=q_u[:P, 0:n], in1=q_d[:P, 0:n],
                op=ALU.mult),
                reads=[b_qu, b_qd], writes=[b_qw], tag="q_w1")
            S.emit("dv", lambda e: e.tensor_scalar(
                out=q_d[:P, 0:n], in0=q_au[:P, 0:n], scalar1=8.0,
                scalar2=-0.5, op0=ALU.is_gt, op1=ALU.mult),
                reads=[b_qau], writes=[b_qd], tag="q_d2a")
            S.emit("dv", lambda e: e.tensor_scalar(
                out=q_d[:P, 0:n], in0=q_d[:P, 0:n], scalar1=1.0,
                scalar2=None, op0=ALU.add),
                reads=[b_qd], writes=[b_qd], tag="q_d2b")
            S.emit("dv", lambda e: e.tensor_tensor(
                out=q_w[:P, 0:n], in0=q_w[:P, 0:n], in1=q_d[:P, 0:n],
                op=ALU.mult),
                reads=[b_qw, b_qd], writes=[b_qw], tag="q_w2")
            S.emit("dv", lambda e: e.tensor_scalar(
                out=q_u[:P, 0:n], in0=q_w[:P, 0:n], scalar1=64.5,
                scalar2=None, op0=ALU.add),
                reads=[b_qw], writes=[b_qu], tag="q_yv")
            S.emit("dv", lambda e: e.tensor_copy(
                out=q_ri[:P, 0:n], in_=q_u[:P, 0:n]),
                reads=[b_qu], writes=[b_qri], tag="q_ri")
            S.emit("ac", lambda e: e.copy(
                out=q_w[:P, 0:n], in_=q_ri[:P, 0:n]),
                reads=[b_qri], writes=[b_qw], tag="q_rf")
            S.emit("dv", lambda e: e.tensor_tensor(
                out=q_d[:P, 0:n], in0=q_w[:P, 0:n], in1=q_u[:P, 0:n],
                op=ALU.is_gt),
                reads=[b_qw, b_qu], writes=[b_qd], tag="q_c")
            S.emit("dv", lambda e: e.scalar_tensor_tensor(
                out=q_rb[:P, 0:n], in0=q_w[:P, 0:n], scalar=-64.0,
                in1=q_d[:P, 0:n], op0=ALU.add, op1=ALU.subtract),
                reads=[b_qw, b_qd], writes=[b_qrb], tag="q_fix")
            S.emit("dv", lambda e: e.tensor_tensor(
                out=q_rb[:P, 0:n], in0=q_rb[:P, 0:n], in1=q_f[:P, 0:n],
                op=ALU.mult),
                reads=[b_qrb, b_qf], writes=[b_qrb], tag="q_z")
            hbc = q_h[:P, 0:nb].unsqueeze(2).broadcast_to([P, nb, 16])
            S.emit("dv", lambda e: e.tensor_tensor(
                out=outap.rearrange("p (b k) -> p b k", k=16),
                in0=q_rb[:P, 0:n].rearrange("p (b k) -> p b k", k=16),
                in1=hbc, op=ALU.mult),
                reads=[b_qrb, b_qh], writes=[bout], tag="q_y")

        NJOB = (QSH + 127) // 128

        def wq_job(j):
            pi = j % 2
            r0 = j * 128
            P = min(128, QSH - r0)
            S.emit("sy", lambda e: e.dma_start(
                out=wq_x[pi][:P, :], in_=qw_in[r0:r0 + P, :]),
                writes=[b_wqx[pi]], dma=True, tag=f"wl{j}")
            quant(wq_x[pi][:P, :], b_wqx[pi], 1024, wq_o[:P, :], b_wqo, P=P)
            S.emit("sy", lambda e: e.dma_start(
                out=qw_my[r0:r0 + P, :], in_=wq_o[:P, :]),
                reads=[b_wqo], writes=[b_qwmy], dma=True, tag=f"ws{j}")

        st1 = contextlib.ExitStack()
        x_t = [sbt(st1, f"x_t{i}", [128, D], F32) for i in range(2)]
        xn_t = sbt(st1, "xn_t", [128, D], BF16)
        xn_f = [sbt(st1, f"xn_f{i}", [128, 8, 144], BF16) for i in range(2)]
        h_f = sbt(st1, "h_f", [128, 8, 128], BF16)
        h_acc = sbt(st1, "h_acc", [128, 8, 128], BF16)
        vst = sbt(st1, "vst", [128, D], BF16)
        ksq = sbt(st1, "ksq", [128, 512], BF16)
        bsc = sbt(st1, "bsc", [128, 4], F32)
        b_xt = [Buf(f"x{i}") for i in range(2)]
        b_xnt = Buf("xnt")
        b_xnf = [[Buf(f"xnf{i}_{d}") for d in range(9)] for i in range(2)]
        b_hf = Buf("hf")
        b_hacc = Buf("hacc")
        b_vst = [Buf("vst0"), Buf("vst1")]
        b_ksq = Buf("ksq")
        b_bsc = Buf("bsc")

        ps1 = [pst(st1, f"ps1_{i}", [128, 512], F32) for i in range(7)]
        b_ps1 = [Buf(f"p1{i}") for i in range(7)]

        S.emit("gp", lambda e: e.memset(xn_f[0][:, :, 13:16], 0.0),
               writes=[b_xnf[0][8]], tag="halo0")

        def p1_tile(t):
            pi = t % 2
            own = t >= OWN0
            S.emit("sy", lambda e: e.dma_start(
                out=x_t[pi][:], in_=xseq[t * 128:(t + 1) * 128, :]),
                writes=[b_xt[pi]], dma=True, tag=f"lx{t}")
            S.emit("ac", lambda e: e.activation(
                out=xn_t[:], in_=x_t[pi][:], func=AF.Identity,
                scale=stats[:, t, 1:2], bias=stats[:, t, 0:1]),
                reads=[b_xt[pi], b_stats[0], b_stats[1]], writes=[b_xnt],
                tag=f"ln{t}")
            S.emit("ac", lambda e: e.dma_start(
                out=xn_f[pi][:, :, 16:144], in_=xn_t[:], transpose=True),
                reads=[b_xnt], writes=list(b_xnf[pi][0:8]), dma=True,
                xpose=True, tag=f"tx{t}")
            for i in range(4):
                for dc in range(8):
                    if i == 0:
                        S.emit("dv", lambda e, dc=dc: e.tensor_scalar(
                            out=h_acc[:, dc, :], in0=xn_f[pi][:, dc, 13:141],
                            scalar1=convw[:, dc, 0:1], scalar2=None,
                            op0=ALU.mult),
                            reads=[b_xnf[pi][dc], b_xnf[pi][8], b_convw],
                            writes=[b_hacc],
                            tag=f"cv{t}_{dc}_0")
                    else:
                        S.emit("dv", lambda e, dc=dc, i=i:
                               e.scalar_tensor_tensor(
                                   out=h_acc[:, dc, :],
                                   in0=xn_f[pi][:, dc, 13 + i:141 + i],
                                   scalar=convw[:, dc, i:i + 1],
                                   in1=h_acc[:, dc, :],
                                   op0=ALU.mult, op1=ALU.add),
                               reads=[b_xnf[pi][dc], b_xnf[pi][8], b_convw,
                                      b_hacc],
                               writes=[b_hacc], tag=f"cv{t}_{dc}_{i}")
            S.emit("ac", lambda e: e.activation(
                out=h_f[:].rearrange("p a n -> p (a n)"),
                in_=h_acc[:].rearrange("p a n -> p (a n)"), func=AF.Silu),
                reads=[b_hacc], writes=[b_hf], tag=f"si{t}")

            if t + 1 < NCH:
                S.emit("dv", lambda e: e.tensor_copy(
                    out=xn_f[1 - pi][:, :, 13:16],
                    in_=xn_f[pi][:, :, 141:144]),
                    reads=list(b_xnf[pi][0:8]), writes=[b_xnf[1 - pi][8]],
                    tag=f"hl{t}")
            pieces = [(512, 1024, 1), (1024, 1536, 2), (1536, 2048, 3),
                      (3072, 3076, 6)]
            if own:
                pieces = [(0, 512, 0)] + pieces + \
                         [(2048, 2560, 4), (2560, 3072, 5)]
            for dc in range(8):
                for (c0, c1, pb) in pieces:
                    S.emit("te", lambda e, dc=dc, c0=c0, c1=c1, pb=pb:
                           e.matmul(ps1[pb][:, 0:c1 - c0], h_f[:, dc, :],
                                    Wcat[:, dc, c0:c1],
                                    start=(dc == 0), stop=(dc == 7)),
                           reads=[b_hf, b_wcat], writes=[b_ps1[pb]],
                           tag=f"mm{t}_{dc}_{pb}")
            S.emit("ac", lambda e, t=t: e.copy(
                out=k_raw[:, t, :], in_=ps1[1][:]),
                reads=[b_ps1[1]], writes=[b_kraw[t]], tag=f"ek{t}")
            S.emit("ac", lambda e: e.activation(
                out=ksq[:], in_=ps1[1][:], func=AF.Square),
                reads=[b_ps1[1]], writes=[b_ksq], tag=f"k2{t}")
            S.emit("dv", lambda e, t=t: e.reduce_sum(
                out=s2k_a[:, t, :],
                in_=ksq[:].rearrange("p (h n) -> p h n", n=128), axis=AX.X),
                reads=[b_ksq], writes=[b_s2k[t]], tag=f"k3{t}")
            S.emit("dv", lambda e: e.tensor_copy(
                out=vst[:, 0:512], in_=ps1[2][:]),
                reads=[b_ps1[2]], writes=[b_vst[0]], tag=f"ev1{t}")
            S.emit("ac", lambda e: e.copy(
                out=vst[:, 512:1024], in_=ps1[3][:]),
                reads=[b_ps1[3]], writes=[b_vst[1]], tag=f"ev2{t}")
            S.emit("sy", lambda e, t=t: e.dma_start(
                out=v_h[t], in_=vst[:]),
                reads=[b_vst[0], b_vst[1]], writes=[b_vh[t]], dma=True,
                tag=f"sv{t}")
            S.emit("ac", lambda e: e.activation(
                out=bsc[:], in_=ps1[6][:, 0:4], func=AF.Tanh, scale=0.5),
                reads=[b_ps1[6]], writes=[b_bsc], tag=f"bt{t}")
            S.emit("dv", lambda e, t=t: e.tensor_scalar(
                out=beta_a[:, t, :], in0=bsc[:], scalar1=0.5, scalar2=0.5,
                op0=ALU.mult, op1=ALU.add),
                reads=[b_bsc], writes=[b_beta[t]], tag=f"bs{t}")

            if own:
                tt = t - OWN0
                S.emit("ac", lambda e, t=t: e.copy(
                    out=q_raw[:, t - OWN0, :], in_=ps1[0][:]),
                    reads=[b_ps1[0]], writes=[b_qraw[tt]], tag=f"eq{t}")
                S.emit("ac", lambda e: e.activation(
                    out=ksq[:], in_=ps1[0][:], func=AF.Square),
                    reads=[b_ps1[0]], writes=[b_ksq], tag=f"q2{t}")
                S.emit("dv", lambda e, tt=tt: e.reduce_sum(
                    out=s2q_a[:, tt, :],
                    in_=ksq[:].rearrange("p (h n) -> p h n", n=128),
                    axis=AX.X),
                    reads=[b_ksq], writes=[b_s2q[tt]], tag=f"q3{t}")
                S.emit("ac", lambda e, tt=tt: e.activation(
                    out=g_s[:, tt, 0:512], in_=ps1[4][:], func=AF.Silu),
                    reads=[b_ps1[4]], writes=[b_gs[tt]], tag=f"eg1{t}")
                S.emit("ac", lambda e, tt=tt: e.activation(
                    out=g_s[:, tt, 512:1024], in_=ps1[5][:], func=AF.Silu),
                    reads=[b_ps1[5]], writes=[b_gs[tt]], tag=f"eg2{t}")

        wq_pos = [0]
        for t in range(NCH):
            p1_tile(t)
            tgt = (t + 1) * NJOB // NCH
            while wq_pos[0] < tgt:
                wq_job(wq_pos[0])
                wq_pos[0] += 1

        st1.close()
        stW.close()
        S.fence()

        if USE_COLLECTIVE:
            S.emit("gp", lambda e: e.collective_compute(
                "AllGather", ALU.bypass,
                ins=[qw_my[:]], outs=[qw_all[:]],
                replica_groups=[list(range(8))]),
                reads=[b_qwmy], writes=[b_qwall], tag="ag")
        else:
            b_qwall.w = list(b_qwmy.w)

        st2 = contextlib.ExitStack()
        def hpos(h):
            return (h % 2) * 2 + (h // 2)
        kn_t = sbt(st2, "kn_t", [128, NH, 128], BF16)
        kn_f = sbt(st2, "kn_f", [128, NH, 128], BF16)
        kbn_f = sbt(st2, "kbn_f", [128, NH, 128], BF16)
        kbn_t = sbt(st2, "kbn_t", [128, NH, 128], BF16)
        qn_t = sbt(st2, "qn_t", [128, NH, 128], BF16)
        qn_f = sbt(st2, "qn_f", [128, NH, 128], BF16)
        v_t = [sbt(st2, f"v_t{i}", [128, D], BF16) for i in range(2)]
        vb_t = sbt(st2, "vb_t", [128, NH, 256], BF16)
        Xp = [sbt(st2, f"Xp{i}", [128, NH, 128], BF16) for i in range(2)]
        XpT = [sbt(st2, f"XpT{i}", [128, NH, 128], BF16) for i in range(2)]
        Rsn = sbt(st2, "Rsn", [128, NH, 256], BF16)
        U_t = sbt(st2, "U_t", [128, NH, 256], BF16)
        Pt_s = sbt(st2, "Pt_s", [128, NH, 128], BF16)
        settle = sbt(st2, "settle", [128, 64], BF16)
        osb = sbt(st2, "osb", [128, 512], BF16)
        b_osb = Buf("osb")
        b_settle = Buf("settle")
        S_ma = sbt(st2, "S_ma", [128, NH, 256], F32)
        b_Sma = [Buf(f"Sma{i}") for i in range(2)]
        inv_k = sbt(st2, "inv_k", [128, 4], F32)
        inv_q = sbt(st2, "inv_q", [128, 4], F32)
        nbk = sbt(st2, "nbk", [128, 4], F32)
        invr = sbt(st2, "invr", [128, 4], F32)
        osq = sbt(st2, "osq", [128, 512], BF16)
        r2 = sbt(st2, "r2", [128, 4], F32)
        gat = sbt(st2, "gat", [128, D], BF16)
        b_knt, b_kbnt, b_qnt = (Buf(n) for n in ("knt", "kbnt", "qnt"))
        b_knf = [Buf(f"knf{h}") for h in range(NH)]
        b_kbnf = [Buf(f"kbnf{h}") for h in range(NH)]
        b_qnf = [Buf(f"qnf{h}") for h in range(NH)]
        b_vt = [Buf(f"vt{i}") for i in range(2)]
        b_vbt = Buf("vbt")
        b_Xp = [Buf(f"Xp{i}") for i in range(2)]
        b_XpT = [Buf(f"XpT{i}") for i in range(2)]
        b_Rsn = [Buf("Rsn0"), Buf("Rsn1")]
        b_Ut = [Buf("Ut0"), Buf("Ut1")]
        b_Pts = Buf("Pts")
        b_invk, b_invq, b_nbk, b_invr = \
            (Buf(n) for n in ("invk", "invq", "nbk", "invr"))
        b_osq = Buf("osq")
        b_r2 = Buf("r2")
        b_gat = [Buf(f"gat{h}") for h in range(NH)]

        S.emit("gp", lambda e: e.memset(S_ma[:], 0.0),
               writes=list(b_Sma), tag="Sma0")
        S_ps = [pst(st2, f"S_ps{i}", [128, 512], F32) for i in range(2)]
        R_ps = [pst(st2, f"R_ps{i}", [128, 512], F32) for i in range(2)]
        P_ps = [pst(st2, f"P_ps{i}", [128, 512], F32) for i in range(2)]
        o_ps = [pst(st2, f"o_ps{i}", [128, 512], F32) for i in range(2)]
        b_Sps = [Buf(f"Sps{i}") for i in range(2)]
        b_Rps = [Buf(f"Rps{i}") for i in range(2)]
        b_Pps = [Buf(f"Pps{i}") for i in range(2)]
        b_ops = [Buf(f"ops{i}") for i in range(2)]

        def s_slot(h):
            return S_ps[h % 2][:, (h // 2) * 256:(h // 2) * 256 + 256]

        def r_slot(h):
            return R_ps[h % 2][:, (h // 2) * 256:(h // 2) * 256 + 256]

        def o_slot(h):
            return o_ps[h % 2][:, (h // 2) * 256:(h // 2) * 256 + 256]

        def prod_mm(pbank, h, lhsT, rhs, reads, tag):
            S.emit("te", lambda e: e.matmul(
                P_ps[pbank][:, h * 128:(h + 1) * 128], lhsT, rhs,
                start=True, stop=True),
                reads=reads, writes=[b_Pps[pbank]], tag=tag)

        def sc_chunk(c):
            own = c >= OWN0
            tt = c - OWN0
            pi = c % 2
            S.emit("sy", lambda e, c=c: e.dma_start(
                out=v_t[pi][:], in_=v_h[c]),
                reads=[b_vh[c]], writes=[b_vt[pi]], dma=True, tag=f"lv{c}")
            S.emit("dv", lambda e, c=c: e.tensor_scalar(
                out=inv_k[:], in0=s2k_a[:, c, :], scalar1=1e-24, scalar2=None,
                op0=ALU.max), reads=[b_s2k[c]], writes=[b_invk], tag=f"ik{c}")
            S.emit("dv", lambda e: e.reciprocal(out=inv_k[:], in_=inv_k[:]),
                   reads=[b_invk], writes=[b_invk], tag=f"ik2{c}")
            S.emit("ac", lambda e: e.activation(
                out=inv_k[:], in_=inv_k[:], func=AF.Sqrt),
                reads=[b_invk], writes=[b_invk], tag=f"ik3{c}")
            S.emit("dv", lambda e, c=c: e.tensor_tensor(
                out=nbk[:], in0=inv_k[:], in1=beta_a[:, c, :], op=ALU.mult),
                reads=[b_invk, b_beta[c]], writes=[b_nbk], tag=f"nb{c}")
            if own:
                S.emit("dv", lambda e, tt=tt: e.tensor_scalar(
                    out=inv_q[:], in0=s2q_a[:, tt, :], scalar1=1e-24,
                    scalar2=None, op0=ALU.max),
                    reads=[b_s2q[tt]], writes=[b_invq], tag=f"iq{c}")
                S.emit("dv", lambda e: e.reciprocal(out=inv_q[:],
                                                    in_=inv_q[:]),
                       reads=[b_invq], writes=[b_invq], tag=f"iq2{c}")
                S.emit("ac", lambda e: e.activation(
                    out=inv_q[:], in_=inv_q[:], func=AF.Sqrt),
                    reads=[b_invq], writes=[b_invq], tag=f"iq3{c}")
            for h in range(NH):
                S.emit("dv", lambda e, c=c, h=h: e.tensor_scalar(
                    out=kn_t[:, h, :], in0=k_raw[:, c, h * 128:(h + 1) * 128],
                    scalar1=inv_k[:, h:h + 1], scalar2=None, op0=ALU.mult),
                    reads=[b_kraw[c], b_invk], writes=[b_knt], tag=f"kn{c}{h}")
                S.emit("dv", lambda e, c=c, h=h: e.tensor_scalar(
                    out=kbn_t[:, h, :],
                    in0=k_raw[:, c, h * 128:(h + 1) * 128],
                    scalar1=nbk[:, h:h + 1], scalar2=-1.0,
                    op0=ALU.mult, op1=ALU.mult),
                    reads=[b_kraw[c], b_nbk], writes=[b_kbnt], tag=f"kb{c}{h}")
                S.emit("dv", lambda e, c=c, h=h: e.tensor_scalar(
                    out=vb_t[:, h, :],
                    in0=v_t[pi][:, h * 256:(h + 1) * 256],
                    scalar1=beta_a[:, c, h:h + 1], scalar2=None,
                    op0=ALU.mult),
                    reads=[b_vt[pi], b_beta[c]], writes=[b_vbt],
                    tag=f"vb{c}{h}")
                if own:
                    S.emit("dv", lambda e, tt=tt, h=h: e.tensor_scalar(
                        out=qn_t[:, h, :],
                        in0=q_raw[:, tt, h * 128:(h + 1) * 128],
                        scalar1=inv_q[:, h:h + 1], scalar2=None,
                        op0=ALU.mult),
                        reads=[b_qraw[tt], b_invq], writes=[b_qnt],
                        tag=f"qn{c}{h}")
            S.emit("ac", lambda e: e.dma_start(
                out=kn_f[:], in_=kn_t[:].rearrange("p a n -> p (a n)"),
                transpose=True),
                reads=[b_knt], writes=list(b_knf), dma=True, xpose=True,
                tag=f"tk{c}")
            S.emit("sy", lambda e: e.dma_start(
                out=kbn_f[:], in_=kbn_t[:].rearrange("p a n -> p (a n)"),
                transpose=True),
                reads=[b_kbnt], writes=list(b_kbnf), dma=True, xpose=True,
                tag=f"tb{c}")
            if own:
                S.emit("ac", lambda e: e.dma_start(
                    out=qn_f[:], in_=qn_t[:].rearrange("p a n -> p (a n)"),
                    transpose=True),
                    reads=[b_qnt], writes=list(b_qnf), dma=True, xpose=True,
                    tag=f"tq{c}")
            for h in range(NH):
                prod_mm(0, h, kbn_f[:, h, :], kn_f[:, h, :],
                        [b_kbnf[h], b_knf[h]], f"X{c}{h}")
            S.emit("dv", lambda e: e.tensor_tensor(
                out=Xp[0][:].rearrange("p a n -> p (a n)"),
                in0=P_ps[0][:, 0:512], in1=maskSL[:], op=ALU.mult),
                reads=[b_Pps[0], b_maskSL], writes=[b_Xp[0]], tag=f"eX{c}")
            for h in range(NH):
                prod_mm(1, h, kn_f[:, h, :], kbn_f[:, h, :],
                        [b_knf[h], b_kbnf[h]], f"XT{c}{h}")
            S.emit("dv", lambda e: e.tensor_tensor(
                out=XpT[0][:].rearrange("p a n -> p (a n)"),
                in0=P_ps[1][:, 0:512], in1=maskSU[:], op=ALU.mult),
                reads=[b_Pps[1], b_maskSU], writes=[b_XpT[0]], tag=f"eXT{c}")
            for h in range(NH):
                S.emit("te", lambda e, h=h: e.matmul(
                    r_slot(h), i_sb[:], vb_t[:, h, :],
                    start=(h < 2), stop=False, skip_group_check=True),
                    reads=[b_isb, b_vbt], writes=[b_Rps[h % 2]],
                    tag=f"Ri{c}{h}")
            for h in range(NH):
                S.emit("te", lambda e, h=h: e.matmul(
                    r_slot(h), kbn_f[:, h, :], S_bf[:, hpos(h), :],
                    start=False, stop=(h >= 2), skip_group_check=True),
                    reads=[b_kbnf[h], b_Sbf[hpos(h)]], writes=[b_Rps[h % 2]],
                    tag=f"Rs{c}{h}")
            for lev in range(NLEV):
                cur = lev % 2
                S.emit("dv", lambda e: e.tensor_copy(
                    out=Rsn[:, 0:2, :].rearrange("p a n -> p (a n)"),
                    in_=R_ps[0][:]),
                    reads=[b_Rps[0]], writes=[b_Rsn[0]], tag=f"sn{c}_{lev}a")
                S.emit("dv", lambda e: e.tensor_copy(
                    out=Rsn[:, 2:4, :].rearrange("p a n -> p (a n)"),
                    in_=R_ps[1][:]),
                    reads=[b_Rps[1]], writes=[b_Rsn[1]], tag=f"sn{c}_{lev}b")
                if lev < NLEV - 1:
                    nxt = 1 - cur
                    last = (lev == NLEV - 2)
                    if not last:
                        for h in range(NH):
                            prod_mm(0, h, XpT[cur][:, h, :], Xp[cur][:, h, :],
                                    [b_XpT[cur], b_Xp[cur]], f"sq{c}_{lev}{h}")
                        S.emit("dv", lambda e, nxt=nxt: e.tensor_copy(
                            out=Xp[nxt][:].rearrange("p a n -> p (a n)"),
                            in_=P_ps[0][:, 0:512]),
                            reads=[b_Pps[0]], writes=[b_Xp[nxt]],
                            tag=f"esq{c}_{lev}")
                    for h in range(NH):
                        prod_mm(1, h, Xp[cur][:, h, :], XpT[cur][:, h, :],
                                [b_Xp[cur], b_XpT[cur]], f"sqT{c}_{lev}{h}")
                    S.emit("dv", lambda e, nxt=nxt: e.tensor_copy(
                        out=XpT[nxt][:].rearrange("p a n -> p (a n)"),
                        in_=P_ps[1][:, 0:512]),
                        reads=[b_Pps[1]], writes=[b_XpT[nxt]],
                        tag=f"esqT{c}_{lev}")
                for h in range(NH):
                    S.emit("te", lambda e, h=h, cur=cur: e.matmul(
                        r_slot(h), XpT[cur][:, h, :], Rsn[:, hpos(h), :],
                        start=False, stop=(h >= 2),
                        skip_group_check=True),
                        reads=[b_XpT[cur], b_Rsn[h % 2]],
                        writes=[b_Rps[h % 2]],
                        tag=f"Ra{c}_{lev}{h}")
            S.emit("dv", lambda e: e.tensor_copy(
                out=U_t[:, 0:2, :].rearrange("p a n -> p (a n)"),
                in_=R_ps[0][:]),
                reads=[b_Rps[0]], writes=[b_Ut[0]], tag=f"eU{c}a")
            S.emit("dv", lambda e: e.tensor_copy(
                out=U_t[:, 2:4, :].rearrange("p a n -> p (a n)"),
                in_=R_ps[1][:]),
                reads=[b_Rps[1]], writes=[b_Ut[1]], tag=f"eU{c}b")
            S.emit("dv", lambda e: e.tensor_copy(
                out=settle[:], in_=U_t[:, 0, 0:64]),
                reads=[b_Ut[0], b_Ut[1]], writes=[b_settle], tag=f"stl{c}")
            if own:
                for h in range(NH):
                    prod_mm(0, h, kn_f[:, h, :], qn_f[:, h, :],
                            [b_knf[h], b_qnf[h]], f"Pt{c}{h}")
                S.emit("dv", lambda e: e.tensor_tensor(
                    out=Pt_s[:].rearrange("p a n -> p (a n)"),
                    in0=P_ps[0][:, 0:512], in1=maskLE[:], op=ALU.mult),
                    reads=[b_Pps[0], b_maskLE], writes=[b_Pts], tag=f"eP{c}")
                for h in range(NH):
                    S.emit("te", lambda e, h=h: e.matmul(
                        o_slot(h), qn_f[:, h, :], S_bf[:, hpos(h), :],
                        start=True, stop=True, skip_group_check=True),
                        reads=[b_qnf[h], b_Sbf[hpos(h)]],
                        writes=[b_ops[h % 2]],
                        tag=f"o1{c}{h}")
                for h in range(NH):
                    S.emit("te", lambda e, h=h: e.matmul(
                        s_slot(h), Pt_s[:, h, :], U_t[:, hpos(h), :],
                        start=True, stop=True, skip_group_check=True),
                        reads=[b_Pts, b_Ut[h % 2], b_settle],
                        writes=[b_Sps[h % 2]],
                        tag=f"o2{c}{h}")
                for half in range(2):
                    S.emit("dv", lambda e, half=half: e.tensor_copy(
                        out=osb[:], in_=o_ps[half][:]),
                        reads=[b_ops[half]], writes=[b_osb],
                        tag=f"oe{c}{half}")
                    S.emit("dv", lambda e, half=half: e.scalar_tensor_tensor(
                        out=osb[:], in0=S_ps[half][:], scalar=1.0,
                        in1=osb[:], op0=ALU.mult, op1=ALU.add),
                        reads=[b_Sps[half], b_osb], writes=[b_osb],
                        tag=f"oc{c}{half}")
                    S.emit("ac", lambda e, half=half: e.activation(
                        out=osq[:], in_=osb[:], func=AF.Square),
                        reads=[b_osb], writes=[b_osq], tag=f"os{c}{half}")
                    S.emit("dv", lambda e, half=half: e.reduce_sum(
                        out=r2[:, half * 2:half * 2 + 2],
                        in_=osq[:].rearrange("p (h n) -> p h n", n=256),
                        axis=AX.X),
                        reads=[b_osq], writes=[b_r2], tag=f"or{c}{half}")
                    S.emit("dv", lambda e, half=half: e.tensor_copy(
                        out=gat[:].rearrange("p (a n) -> p a n", n=256)[
                            :, half * 2:half * 2 + 2, :].rearrange(
                            "p a n -> p (a n)"), in_=osb[:]),
                        reads=[b_osb],
                        writes=[b_gat[half], b_gat[half + 2]],
                        tag=f"ob{c}{half}")
                S.emit("dv", lambda e: e.tensor_scalar(
                    out=invr[:], in0=r2[:], scalar1=1.0 / 256.0, scalar2=1e-5,
                    op0=ALU.mult, op1=ALU.add),
                    reads=[b_r2], writes=[b_invr], tag=f"om{c}")
                S.emit("dv", lambda e: e.reciprocal(out=invr[:], in_=invr[:]),
                       reads=[b_invr], writes=[b_invr], tag=f"oc{c}")
                S.emit("ac", lambda e: e.activation(
                    out=invr[:], in_=invr[:], func=AF.Sqrt),
                    reads=[b_invr], writes=[b_invr], tag=f"oq{c}")
                for h in range(NH):
                    S.emit("dv", lambda e, h=h: e.tensor_scalar(
                        out=gat[:, hpos(h) * 256:hpos(h) * 256 + 256],
                        in0=gat[:, hpos(h) * 256:hpos(h) * 256 + 256],
                        scalar1=invr[:, hpos(h):hpos(h) + 1], scalar2=None,
                        op0=ALU.mult),
                        reads=[b_gat[hpos(h)], b_invr],
                        writes=[b_gat[hpos(h)]],
                        tag=f"og{c}{h}")
                for h in range(NH):
                    S.emit("dv", lambda e, h=h, tt=tt: e.tensor_tensor(
                        out=gat[:, hpos(h) * 256:hpos(h) * 256 + 256],
                        in0=gat[:, hpos(h) * 256:hpos(h) * 256 + 256],
                        in1=g_s[:, tt, h * 256:(h + 1) * 256], op=ALU.mult),
                        reads=[b_gat[hpos(h)], b_gs[tt]],
                        writes=[b_gat[hpos(h)]],
                        tag=f"gg{c}{h}")
                S.emit("sy", lambda e, tt=tt: e.dma_start(
                    out=g_h[tt], in_=gat[:]),
                    reads=list(b_gat), writes=[b_gh[tt]], dma=True,
                    tag=f"sg{c}")
            for h in range(NH):
                S.emit("te", lambda e, h=h: e.matmul(
                    s_slot(h), kn_t[:, h, :], U_t[:, hpos(h), :],
                    start=True, stop=True, skip_group_check=True),
                    reads=[b_knt, b_Ut[h % 2], b_settle],
                    writes=[b_Sps[h % 2]],
                    tag=f"Su{c}{h}")
            for half in range(2):
                S.emit("dv", lambda e, half=half: e.scalar_tensor_tensor(
                    out=S_ma[:, half * 2:half * 2 + 2, :].rearrange(
                        "p a n -> p (a n)"),
                    in0=S_ps[half][:], scalar=1.0,
                    in1=S_ma[:, half * 2:half * 2 + 2, :].rearrange(
                        "p a n -> p (a n)"),
                    op0=ALU.mult, op1=ALU.add),
                    reads=[b_Sps[half], b_Sma[half]],
                    writes=[b_Sma[half]],
                    tag=f"aS{c}{half}")
                S.emit("dv", lambda e, half=half: e.tensor_copy(
                    out=S_bf[:, half * 2:half * 2 + 2, :].rearrange(
                        "p a n -> p (a n)"),
                    in_=S_ma[:, half * 2:half * 2 + 2, :].rearrange(
                        "p a n -> p (a n)")),
                    reads=[b_Sma[half]],
                    writes=[b_Sbf[half * 2], b_Sbf[half * 2 + 1]],
                    tag=f"eS{c}{half}")
            if c == 0:
                dbg_dump(0, S_bf[:].rearrange("p a n -> p (a n)"),
                         list(b_Sbf))
            if c == 8:
                dbg_dump(1, S_bf[:].rearrange("p a n -> p (a n)"),
                         list(b_Sbf))

        for c in range(NCH):
            sc_chunk(c)
        st2.close()
        st12.close()
        S.fence()

        st3 = contextlib.ExitStack()
        wgqT = sbt(st3, "wgqT", [128, 8, IF], BF16)
        wuqT = sbt(st3, "wuqT", [128, 8, IF], BF16)
        wdqT = sbt(st3, "wdqT", [128, 22, D], BF16)
        Wo_b = sbt(st3, "Wo_b", [128, 8, D], BF16)
        b_wgqT = [Buf(f"wgqT{d}") for d in range(8)]
        b_wuqT = [Buf(f"wuqT{d}") for d in range(8)]
        b_wdqT = [Buf(f"wdqT{i}") for i in range(22)]
        b_wob = Buf("wob")
        qw_flat = qw_all.rearrange("a b -> (a b)")

        x3_t = [sbt(st3, f"x3_t{i}", [128, D], F32) for i in range(2)]
        b_x3 = [Buf(f"x3{i}") for i in range(2)]
        for dc in range(8):
            S.emit("sy", lambda e, dc=dc: e.dma_start(
                out=x3_t[dc % 2][:],
                in_=w_in["Wo"][dc * 128:(dc + 1) * 128, :].rearrange(
                    "(a p) n -> p (a n)", p=128)),
                writes=[b_x3[dc % 2]], dma=True, tag=f"lwo{dc}")
            S.emit("dv", lambda e, dc=dc: e.tensor_scalar(
                out=Wo_b[:, dc, :], in0=x3_t[dc % 2][:],
                scalar1=nwc[:, dc % 2:dc % 2 + 1], scalar2=None,
                op0=ALU.mult),
                reads=[b_x3[dc % 2], b_nwc], writes=[b_wob], tag=f"wos{dc}")
        S.emit("sy", lambda e: e.dma_start(
            out=wgqT[:], in_=qw_all[0:IF, 0:1024], transpose=True),
            reads=[b_qwall], writes=list(b_wgqT), dma=True, xpose=True,
            tag="lwg")
        S.emit("sy", lambda e: e.dma_start(
            out=wuqT[:], in_=qw_all[IF:2 * IF, 0:1024], transpose=True),
            reads=[b_qwall], writes=list(b_wuqT), dma=True, xpose=True,
            tag="lwu")
        wd_2d = qw_flat[2 * IF * 1024:].rearrange("(a b) -> a b", b=IF)
        S.emit("ac", lambda e: e.dma_start(
            out=wdqT[:], in_=wd_2d, transpose=True),
            reads=[b_qwall], writes=list(b_wdqT), dma=True, xpose=True,
            tag="lwd")

        xn3 = sbt(st3, "xn3", [128, D], F32)
        xq_t = sbt(st3, "xq_t", [128, D], BF16)
        xq_f = sbt(st3, "xq_f", [128, 8, 128], BF16)
        gg_t = [sbt(st3, f"gg_t{i}", [128, D], BF16) for i in range(2)]
        gg_f = sbt(st3, "gg_f", [128, 8, 128], BF16)
        sg_t = sbt(st3, "sg_t", [128, 512], F32)
        h2 = sbt(st3, "h2", [128, 512], F32)
        h2q = sbt(st3, "h2q", [128, IF], BF16)
        h2q_f = sbt(st3, "h2q_f", [128, 22, 128], BF16)
        ot = [sbt(st3, "ot0", [128, D], F32)] * 2
        b_xn3 = Buf("xn3")
        b_xqt = Buf("xqt")
        b_xqf = [Buf(f"xqf{d}") for d in range(8)]
        b_ggt = [Buf(f"ggt{i}") for i in range(2)]
        b_ggf = [Buf(f"ggf{d}") for d in range(8)]
        b_sgt = Buf("sgt")
        b_h2 = Buf("h2")
        b_h2q = Buf("h2q")
        b_h2qf = [Buf(f"h2qf{i}") for i in range(22)]
        _otb = [Buf("ot_0"), Buf("ot_1")]
        b_ot = [_otb, _otb]
        b_outh = [Buf(f"oh{t}") for t in range(16)]

        gu_ps = [pst(st3, f"gu_ps{i}", [128, 512], F32) for i in range(4)]
        dn_ps = [pst(st3, f"dn_ps{i}", [128, 512], F32) for i in range(2)]
        b_gups = [Buf(f"gu{i}") for i in range(4)]
        b_dnps = [Buf(f"dn{i}") for i in range(2)]

        def p3_tile(tt):
            pi = tt % 2
            t = tt + OWN0
            S.emit("sy", lambda e, t=t: e.dma_start(
                out=x3_t[pi][:], in_=xseq[t * 128:(t + 1) * 128, :]),
                writes=[b_x3[pi]], dma=True, tag=f"3x{tt}")
            S.emit("ac", lambda e, t=t: e.activation(
                out=xn3[:], in_=x3_t[pi][:], func=AF.Identity,
                scale=stats[:, t, 1:2], bias=stats[:, t, 0:1]),
                reads=[b_x3[pi], b_stats[0], b_stats[1]], writes=[b_xn3],
                tag=f"3n{tt}")
            quant(xn3[:], b_xn3, 1024, xq_t[:], b_xqt)
            if tt == 0:
                dbg_dump(8, xq_t[:], [b_xqt])
                dbg_dump(9, wgqT[:, 0, 0:1024], [b_wgqT[0]])
                dbg_dump(10, wdqT[:, 0, :], [b_wdqT[0]])
                dbg_dump(11, gg_t[pi][:], [b_ggt[pi]])
            S.emit("sy", lambda e: e.dma_start(
                out=xq_f[:], in_=xq_t[:], transpose=True),
                reads=[b_xqt], writes=list(b_xqf), dma=True, xpose=True,
                tag=f"3t{tt}")
            S.emit("ac", lambda e, tt=tt: e.dma_start(
                out=gg_t[pi][:], in_=g_h[tt]),
                reads=[b_gh[tt]], writes=[b_ggt[pi]], dma=True, tag=f"3g{tt}")
            S.emit("ac", lambda e: e.dma_start(
                out=gg_f[:], in_=gg_t[pi][:], transpose=True),
                reads=[b_ggt[pi]], writes=list(b_ggf), dma=True, xpose=True,
                tag=f"3gt{tt}")
            for dc in range(8):
                for half in range(2):
                    S.emit("te", lambda e, dc=dc, half=half: e.matmul(
                        dn_ps[half][:], gg_f[:, dc, :],
                        Wo_b[:, dc, half * 512:(half + 1) * 512],
                        start=(dc == 0), stop=False, skip_group_check=True),
                        reads=[b_ggf[dc], b_wob], writes=[b_dnps[half]],
                        tag=f"wo{tt}_{dc}_{half}")
            pieces = [(i * 512, min(IF, (i + 1) * 512)) for i in range(6)]
            for pidx, (i0, i1) in enumerate(pieces):
                gb = (pidx % 2) * 2
                ub = gb + 1
                for dc in range(8):
                    S.emit("te", lambda e, dc=dc, i0=i0, i1=i1, gb=gb:
                           e.matmul(gu_ps[gb][:, 0:i1 - i0], xq_f[:, dc, :],
                                    wgqT[:, dc, i0:i1],
                                    start=(dc == 0), stop=(dc == 7)),
                           reads=[b_xqf[dc], b_wgqT[dc]],
                           writes=[b_gups[gb]],
                           tag=f"gm{tt}_{pidx}_{dc}")
                    S.emit("te", lambda e, dc=dc, i0=i0, i1=i1, ub=ub:
                           e.matmul(gu_ps[ub][:, 0:i1 - i0], xq_f[:, dc, :],
                                    wuqT[:, dc, i0:i1],
                                    start=(dc == 0), stop=(dc == 7)),
                           reads=[b_xqf[dc], b_wuqT[dc]],
                           writes=[b_gups[ub]],
                           tag=f"um{tt}_{pidx}_{dc}")
                S.emit("ac", lambda e, i0=i0, i1=i1, gb=gb: e.activation(
                    out=sg_t[:, 0:i1 - i0], in_=gu_ps[gb][:, 0:i1 - i0],
                    func=AF.Silu),
                    reads=[b_gups[gb]], writes=[b_sgt], tag=f"sg{tt}_{pidx}")
                S.emit("dv", lambda e, i0=i0, i1=i1, ub=ub: e.tensor_tensor(
                    out=h2[:, 0:i1 - i0], in0=gu_ps[ub][:, 0:i1 - i0],
                    in1=sg_t[:, 0:i1 - i0], op=ALU.mult),
                    reads=[b_gups[ub], b_sgt], writes=[b_h2],
                    tag=f"h2{tt}_{pidx}")
                quant_piece(h2[:, 0:i1 - i0], b_h2, i1 - i0,
                            h2q[:, i0:i1], b_h2q)
            if tt == 0 and pidx == 0:
                dbg_dump(12, sg_t[:, 0:512], [b_sgt])
                dbg_dump(13, h2q[:, 0:512], [b_h2q])
            S.emit("sy", lambda e: e.dma_start(
                out=h2q_f[:], in_=h2q[:], transpose=True),
                reads=[b_h2q], writes=list(b_h2qf), dma=True, xpose=True,
                tag=f"ht{tt}")
            for ic in range(22):
                for half in range(2):
                    S.emit("te", lambda e, ic=ic, half=half: e.matmul(
                        dn_ps[half][:], h2q_f[:, ic, :],
                        wdqT[:, ic, half * 512:(half + 1) * 512],
                        start=False, stop=(ic == 21), skip_group_check=True),
                        reads=[b_h2qf[ic], b_wdqT[ic]],
                        writes=[b_dnps[half]],
                        tag=f"dm{tt}_{ic}_{half}")
            if tt == 0:
                dbg_dump(14, gg_f[:].rearrange("p a n -> p (a n)"),
                         list(b_ggf))
            for half in range(2):
                S.emit("dv", lambda e, half=half: e.scalar_tensor_tensor(
                    out=ot[pi][:, half * 512:(half + 1) * 512],
                    in0=dn_ps[half][:], scalar=1.0,
                    in1=x3_t[pi][:, half * 512:(half + 1) * 512],
                    op0=ALU.mult, op1=ALU.add),
                    reads=[b_dnps[half], b_x3[pi]], writes=[b_ot[pi][half]],
                    tag=f"rs{tt}{half}")
            S.emit("sy", lambda e, tt=tt: e.dma_start(
                out=out_h[tt * 128:(tt + 1) * 128, :], in_=ot[pi][:]),
                reads=list(b_ot[pi]), writes=[b_outh[tt]], dma=True,
                tag=f"so{tt}")

        for tt in range(16):
            p3_tile(tt)
        st3.close()
        stQ.close()
        stW.close()

        fin = S.barrier_dep()
        fbuf = Buf("fin")
        fbuf.w = fin
        S.emit("sy", lambda e: e.nop(), reads=[fbuf], tag="fin")

        S.flush(block, sems)
    outer.close()
    return nc


_NC_CACHE = {}


USE = None


def kernel(L0_attn_hidden_states, L0_attn_conv_w, L0_attn_Wq, L0_attn_Wk,
           L0_attn_Wv, L0_attn_Wb, L0_attn_Wg, L0_attn_norm_w, L0_attn_Wo,
           L0_ffn_w_gate, L0_ffn_w_up, L0_ffn_w_down):
    x = np.asarray(L0_attn_hidden_states, dtype=np.float32)
    w = {
        "conv_w": np.ascontiguousarray(L0_attn_conv_w, dtype=np.float32),
        "Wq": np.ascontiguousarray(L0_attn_Wq, dtype=np.float32),
        "Wk": np.ascontiguousarray(L0_attn_Wk, dtype=np.float32),
        "Wv": np.ascontiguousarray(L0_attn_Wv, dtype=np.float32),
        "Wb": np.ascontiguousarray(L0_attn_Wb, dtype=np.float32),
        "Wg": np.ascontiguousarray(L0_attn_Wg, dtype=np.float32),
        "norm_w": np.ascontiguousarray(L0_attn_norm_w, dtype=np.float32),
        "Wo": np.ascontiguousarray(np.concatenate(
            [np.asarray(L0_attn_Wo, np.float32)[0:256],
             np.asarray(L0_attn_Wo, np.float32)[512:768],
             np.asarray(L0_attn_Wo, np.float32)[256:512],
             np.asarray(L0_attn_Wo, np.float32)[768:1024]], axis=0)),
    }
    flat = np.concatenate([
        np.asarray(L0_ffn_w_gate, np.float32).reshape(-1, 1024),
        np.asarray(L0_ffn_w_up, np.float32).reshape(-1, 1024),
        np.asarray(L0_ffn_w_down, np.float32).reshape(-1, 1024)], axis=0)
    if "nc" not in _NC_CACHE:
        _NC_CACHE["nc"] = build_program()
    nc = _NC_CACHE["nc"]
    in_maps = []
    for core in range(8):
        b, hf = core // 2, core % 2
        if hf == 0:
            seq = np.concatenate(
                [np.zeros((2048, D), np.float32), x[b, 0:2048]], 0)
        else:
            seq = np.ascontiguousarray(x[b])
        tri = np.tril(np.ones((128, 128), np.float32), -1)
        cns = np.stack([np.tile(tri, (1, 4)),
                        np.tile(tri.T, (1, 4)),
                        np.tile(np.triu(np.ones((128, 128), np.float32)),
                                (1, 4)),
                        np.concatenate([np.eye(128, dtype=np.float32),
                                        np.zeros((128, 384), np.float32)], 1)],
                       axis=1)
        if USE_COLLECTIVE:
            qsh = np.ascontiguousarray(flat[core * QSH:(core + 1) * QSH])
        else:
            qsh = flat
        im = {"xseq": seq, "qw_in": qsh, "consts": cns}
        im.update(w)
        in_maps.append(im)
    res = run_bass_kernel_spmd(nc, in_maps, core_ids=list(range(8)))
    if DBG:
        sys.modules[__name__].dbg_results = [r.get("dbg") for r in res.results]
    out = np.empty((B, L, D), np.float32)
    for core in range(8):
        b, hf = core // 2, core % 2
        out[b, hf * 2048:(hf + 1) * 2048] = res.results[core]["out"]
    return out


# version 19
# speedup vs baseline: 1.7440x; 1.1237x over previous
import contextlib
import sys
import numpy as np
import concourse.bass as bass
import concourse.mybir as mybir
from concourse.bass_utils import run_bass_kernel_spmd

F32 = mybir.dt.float32
BF16 = mybir.dt.bfloat16
I32 = mybir.dt.int32
AF = mybir.ActivationFunctionType
ALU = mybir.AluOpType
AX = mybir.AxisListType

B, L, D = 4, 4096, 1024
NH, DQH, DVH = 4, 128, 256
C = 128
NCH = L // C
OWN0 = NCH // 2
IF = 2816
NLEV = 7
USE_COLLECTIVE = False
SERIALIZE_TE = False
SERIALIZE_ALL = False
SERIALIZE_DMA = True
DBG = False

QROWS = (IF * D * 2 + D * IF) // 1024
QSH = QROWS // 8 if USE_COLLECTIVE else QROWS

ENGS = ("te", "dv", "ac", "gp", "sy")
ENG_SEM = {"te": "TE", "dv": "DV", "ac": "AC", "gp": "GP", "sy": "SY"}
DMA_Q = {"sy": "DS", "ac": "DA", "gp": "DG"}
NROT = 8
DMA_SEMS = {f"{q}{i}" for q in DMA_Q.values() for i in range(NROT)}


class Buf:
    __slots__ = ("w", "r", "name")

    def __init__(self, name=""):
        self.w = []
        self.r = []
        self.name = name


class Op:
    __slots__ = ("eng", "fn", "waits", "sem", "val", "tag", "need_inc")

    def __init__(self, eng, fn, waits, sem, val, tag):
        self.eng, self.fn, self.waits, self.sem, self.val, self.tag = \
            eng, fn, waits, sem, val, tag
        self.need_inc = False


class Sched:

    def __init__(self):
        self.count = {s: 0 for s in list(ENG_SEM.values()) + list(DMA_SEMS)}
        self.eng_pos = {e: 0 for e in ENGS}
        self.dma_uses = {q: 0 for q in DMA_Q.values()}
        self.ops = []
        self.by_sem_val = {}
        self.fence_map = {}
        self.dma_tok = Buf("dma_tok")

    def emit(self, eng, fn, reads=(), writes=(), dma=False, xpose=False,
             tag=""):
        if dma and SERIALIZE_DMA and xpose:
            writes = list(writes) + [self.dma_tok]
        deps = []
        for b in reads:
            deps.extend(b.w)
        for b in writes:
            deps.extend(b.w)
            deps.extend(b.r)
        dmap = dict(self.fence_map)
        for s, v, pos in deps:
            if v > dmap.get(s, -1):
                dmap[s] = v
        if dma:
            q = DMA_Q[eng]
            k = self.dma_uses[q]
            self.dma_uses[q] = k + 1
            sem = f"{q}{k % NROT}"
            prev = self.count[sem]
            if prev > 0 and dmap.get(sem, -1) < prev:
                dmap[sem] = prev
            self.count[sem] += 16
        else:
            sem = ENG_SEM[eng]
            self.count[sem] += 1
        val = self.count[sem]
        waits = sorted(dmap.items())
        op = Op(eng, fn, waits, sem, val, tag)
        self.ops.append(op)
        self.by_sem_val[(sem, val)] = op
        self.eng_pos[eng] += 1
        dep = (sem, val, self.eng_pos[eng])
        for b in reads:
            b.r.append(dep)
        for b in writes:
            b.w = [dep]
            b.r = []
        return dep

    def barrier_dep(self):
        return [(s, v, 10 ** 9) for s, v in self.count.items() if v > 0]

    def fence(self):
        self.fence_map = {s: v for s, v in self.count.items() if v > 0}

    def finalize(self):
        engs = ENGS if SERIALIZE_ALL else (("te",) if SERIALIZE_TE else ())
        for eng in engs:
            sem = ENG_SEM[eng]
            prev = None
            for op in self.ops:
                if op.eng == eng and op.sem == sem:
                    if prev is not None:
                        if dict(op.waits).get(sem, -1) < prev:
                            w = dict(op.waits)
                            w[sem] = prev
                            op.waits = sorted(w.items())
                    prev = op.val
        waited = {}
        for op in self.ops:
            for s, v in op.waits:
                waited.setdefault(s, set()).add(v)
        for op in self.ops:
            ws = waited.get(op.sem)
            if ws and any(w >= op.val for w in ws):
                op.need_inc = True
        per = {e: [] for e in ENGS}
        for op in self.ops:
            per[op.eng].append(op)
        idx = {e: 0 for e in ENGS}
        cur = {s: 0 for s in self.count}
        done, total = 0, len(self.ops)
        while done < total:
            prog = False
            for e in ENGS:
                while idx[e] < len(per[e]):
                    op = per[e][idx[e]]
                    if all(cur[s] >= v for s, v in op.waits):
                        cur[op.sem] = op.val
                        idx[e] += 1
                        done += 1
                        prog = True
                    else:
                        break
            if not prog:
                stuck = []
                for e in ENGS:
                    if idx[e] < len(per[e]):
                        op = per[e][idx[e]]
                        bad = [(s, v, cur[s]) for s, v in op.waits
                               if cur[s] < v]
                        stuck.append((e, op.tag, bad[:3]))
                raise RuntimeError(f"deadlock: {stuck[:4]}")

    def flush(self, block, sems):
        self.finalize()
        per = {e: [] for e in ENGS}
        for op in self.ops:
            per[op.eng].append(op)

        def mk(lst, ename):
            sat = {}

            def f(e):
                for op in lst:
                    for s, v in op.waits:
                        if sat.get(s, -1) >= v:
                            continue
                        e.wait_ge(sems[s], v)
                        sat[s] = v
                    ins = op.fn(e)
                    if op.need_inc:
                        ins.then_inc(sems[op.sem],
                                     16 if op.sem in DMA_SEMS else 1)
            return f
        if per["te"]:
            block.tensor(mk(per["te"], "te"))
        if per["dv"]:
            block.vector(mk(per["dv"], "dv"))
        if per["ac"]:
            block.scalar(mk(per["ac"], "ac"))
        if per["gp"]:
            block.gpsimd(mk(per["gp"], "gp"))
        if per["sy"]:
            block.sync(mk(per["sy"], "sy"))


def build_program():
    nc = bass.Bass("TRN2", target_bir_lowering=False, debug=False)
    S = Sched()

    xseq = nc.dram_tensor("xseq", [L, D], F32, kind="ExternalInput").ap()
    w_in = {}
    for nm, shp in [("conv_w", [D, 4]), ("Wq", [D, 512]), ("Wk", [D, 512]),
                    ("Wv", [D, D]), ("Wb", [D, 4]), ("Wg", [D, D]),
                    ("norm_w", [DVH]), ("Wo", [D, D])]:
        w_in[nm] = nc.dram_tensor(nm, shp, F32, kind="ExternalInput").ap()
    qw_in = nc.dram_tensor("qw_in", [QSH, 1024], F32,
                           kind="ExternalInput").ap()
    consts = nc.dram_tensor("consts", [128, 4, 512], F32,
                            kind="ExternalInput").ap()
    out_h = nc.dram_tensor("out", [2048, D], F32, kind="ExternalOutput").ap()
    if DBG:
        dbg_h = nc.dram_tensor("dbg", [16, 128, 1536], BF16,
                               kind="ExternalOutput").ap()
        b_dbg = Buf("dbg")
        dbg_stk = contextlib.ExitStack()
        dbg_stage = dbg_stk.enter_context(
            nc.sbuf_tensor("dbg_st", [128, 1536], BF16))
        b_dbgst = Buf("dbgst")

        def dbg_dump(slot, ap, rbufs, eng="sy"):
            w = ap.shape[-1]
            S.emit("dv", lambda e: e.tensor_copy(
                out=dbg_stage[:, 0:w], in_=ap),
                reads=list(rbufs), writes=[b_dbgst], tag=f"dbgc{slot}")
            S.emit("sy", lambda e: e.dma_start(
                out=dbg_h[slot, :, 0:w], in_=dbg_stage[:, 0:w]),
                reads=[b_dbgst], writes=[b_dbg], dma=True,
                tag=f"dbg{slot}")
    else:
        def dbg_dump(slot, ap, rbufs, eng="sy"):
            pass

    v_h = nc.dram_tensor("v_sp", [NCH, 128, D], BF16).ap()
    g_h = nc.dram_tensor("g_sp", [16, 128, D], BF16).ap()
    if USE_COLLECTIVE:
        qw_my = nc.dram_tensor("qw_my", [QSH, 1024], BF16).ap()
        qw_all = nc.dram_tensor("qw_all", [QROWS, 1024], BF16,
                                addr_space="Local").ap()
    else:
        qw_my = nc.dram_tensor("qw_my", [QROWS, 1024], BF16).ap()
        qw_all = qw_my

    outer = contextlib.ExitStack()
    sems = {}
    for s in list(ENG_SEM.values()) + list(DMA_SEMS):
        sems[s] = outer.enter_context(nc.semaphore(s))

    def sbt(stk, name, shape, dt):
        return stk.enter_context(nc.sbuf_tensor(name, shape, dt))

    def pst(stk, name, shape, dt):
        return stk.enter_context(nc.psum_tensor(name, shape, dt))

    convw = sbt(outer, "convw", [128, 8, 4], F32)
    nwc = sbt(outer, "nwc", [128, 2], F32)
    stats = sbt(outer, "stats", [128, NCH, 2], F32)
    stQ = contextlib.ExitStack()
    QW = 1024
    q_m = sbt(stQ, "q_m", [128, 64], F32)
    q_s = sbt(stQ, "q_s", [128, 64], F32)
    q_a = sbt(stQ, "q_a", [128, 64], F32)
    q_h = sbt(stQ, "q_h", [128, 64], F32)
    q_u = sbt(stQ, "q_u", [128, QW], F32)
    q_au = sbt(stQ, "q_au", [128, QW], BF16)
    q_f1 = sbt(stQ, "q_f1", [128, QW], BF16)
    q_f = sbt(stQ, "q_f", [128, QW], BF16)
    q_d = sbt(stQ, "q_d", [128, QW], BF16)
    q_w = sbt(stQ, "q_w", [128, QW], F32)
    q_ri = sbt(stQ, "q_ri", [128, QW], I32)
    q_rb = sbt(stQ, "q_rb", [128, QW], BF16)
    (b_qm, b_qs, b_qa, b_qh, b_qu, b_qau, b_qf1, b_qf, b_qd, b_qw,
     b_qri, b_qrb) = (Buf(n) for n in
                      ("qm", "qs", "qa", "qh", "qu", "qau", "qf1", "qf",
                       "qd", "qw", "qri", "qrb"))
    st12 = contextlib.ExitStack()
    maskSL = sbt(st12, "maskSL", [128, 512], BF16)
    maskSU = sbt(st12, "maskSU", [128, 512], BF16)
    maskLE = sbt(st12, "maskLE", [128, 512], BF16)
    i_sb = sbt(st12, "i_sb", [128, 128], BF16)
    beta_a = sbt(st12, "beta_a", [128, NCH, 4], F32)
    s2k_a = sbt(st12, "s2k_a", [128, NCH, 4], F32)
    s2q_a = sbt(st12, "s2q_a", [128, 16, 4], F32)
    k_raw = sbt(st12, "k_raw", [128, NCH, 512], BF16)
    q_raw = sbt(st12, "q_raw", [128, 16, 512], BF16)
    g_s = sbt(st12, "g_s", [128, 16, D], BF16)
    S_bf = sbt(st12, "S_bf", [128, NH, 256], BF16)

    b_maskSL, b_maskSU, b_maskLE, b_isb, b_convw, b_nwc = \
        (Buf(n) for n in ("mSL", "mSU", "mLE", "isb", "cw", "nwc"))
    b_stats = [Buf(f"st{t}") for t in range(NCH)]
    b_beta = [Buf(f"be{c}") for c in range(NCH)]
    b_s2k = [Buf(f"sk{c}") for c in range(NCH)]
    b_s2q = [Buf(f"sq{c}") for c in range(16)]
    b_kraw = [Buf(f"kr{c}") for c in range(NCH)]
    b_qraw = [Buf(f"qr{c}") for c in range(16)]
    b_gs = [Buf(f"gs{c}") for c in range(16)]
    b_Sbf = [Buf(f"Sb{h}") for h in range(NH)]
    b_vh = [Buf(f"vh{c}") for c in range(NCH)]
    b_gh = [Buf(f"gh{c}") for c in range(16)]
    b_qwmy = Buf("qwmy")
    b_qwall = Buf("qwall")

    with nc.Block() as block:
        S.emit("gp", lambda e: e.dma_start(out=maskSL[:], in_=consts[:, 0, :]),
               writes=[b_maskSL], dma=True, tag="m0")
        S.emit("gp", lambda e: e.dma_start(out=maskSU[:], in_=consts[:, 1, :]),
               writes=[b_maskSU], dma=True, tag="m1")
        S.emit("gp", lambda e: e.dma_start(out=maskLE[:], in_=consts[:, 2, :]),
               writes=[b_maskLE], dma=True, tag="m2")
        S.emit("gp", lambda e: e.dma_start(out=i_sb[:],
                                           in_=consts[:, 3, 0:128]),
               writes=[b_isb], dma=True, tag="m3")
        S.emit("gp", lambda e: e.memset(S_bf[:], 0.0),
               writes=list(b_Sbf), tag="S0")
        S.emit("sy", lambda e: e.dma_start(
            out=convw[:], in_=w_in["conv_w"].rearrange("(a p) k -> p a k",
                                                       p=128)),
            writes=[b_convw], dma=True, tag="ldcw")
        S.emit("sy", lambda e: e.dma_start(
            out=nwc[:], in_=w_in["norm_w"].rearrange("(a p) -> p a", p=128),
            allow_slow_non_contiguous=True),
            writes=[b_nwc], dma=True, tag="ldnw")


        stW = contextlib.ExitStack()
        Wcat = sbt(stW, "Wcat", [128, 8, 3076], BF16)
        wq_x = [sbt(stW, f"wq_x{i}", [128, 1024], F32) for i in range(2)]
        wq_o = sbt(stW, "wq_o", [128, 1024], BF16)
        b_wqx = [Buf(f"wqx{i}") for i in range(2)]
        b_wqo = Buf("wqo")
        b_wcat = Buf("wcat")
        for nm, c0, c1 in [("Wq", 0, 512), ("Wk", 512, 1024),
                           ("Wv", 1024, 2048), ("Wg", 2048, 3072),
                           ("Wb", 3072, 3076)]:
            S.emit("gp", lambda e, nm=nm, c0=c0, c1=c1: e.dma_start(
                out=Wcat[:, :, c0:c1],
                in_=w_in[nm].rearrange("(a p) n -> p a n", p=128)),
                writes=[b_wcat], dma=True, tag=f"ld{nm}")

        stP = contextlib.ExitStack()
        xp = [sbt(stP, f"xp{i}", [128, D], F32) for i in range(2)]
        b_xp = [Buf(f"xp{i}") for i in range(2)]
        bns = sbt(stP, "bns", [128, 12], F32)
        bna = sbt(stP, "bna", [128, NCH, 2], F32)
        b_bns = Buf("bns")
        b_bna = [Buf(f"bna{t}") for t in range(NCH)]
        for t in range(NCH):
            pi = t % 2
            S.emit("sy", lambda e, t=t, pi=pi: e.dma_start(
                out=xp[pi][:], in_=xseq[t * 128:(t + 1) * 128, :]),
                writes=[b_xp[pi]], dma=True, tag=f"px{t}")
            S.emit("dv", lambda e, pi=pi: e.bn_stats(
                out=bns[:, 0:6], in_=xp[pi][:, 0:512]),
                reads=[b_xp[pi]], writes=[b_bns], tag=f"pb{t}")
            S.emit("dv", lambda e, pi=pi: e.bn_stats(
                out=bns[:, 6:12], in_=xp[pi][:, 512:1024]),
                reads=[b_xp[pi]], writes=[b_bns], tag=f"pb2{t}")
            S.emit("dv", lambda e, t=t: e.bn_aggr(out=bna[:, t, :],
                                                  in_=bns[:]),
                   reads=[b_bns], writes=[b_bna[t]], tag=f"pa{t}")
        b_all_bna = list(b_bna)
        S.emit("dv", lambda e: e.tensor_scalar(
            out=stats[:, :, 1], in0=bna[:, :, 1], scalar1=1e-5, scalar2=None,
            op0=ALU.add), reads=b_all_bna, writes=[b_stats[0]], tag="pe")
        S.emit("dv", lambda e: e.reciprocal(
            out=stats[:, :, 1], in_=stats[:, :, 1]),
            reads=[b_stats[0]], writes=[b_stats[0]], tag="pr")
        S.emit("ac", lambda e: e.activation(
            out=stats[:, :, 1], in_=stats[:, :, 1], func=AF.Sqrt),
            reads=[b_stats[0]], writes=[b_stats[0]], tag="ps")
        S.emit("dv", lambda e: e.scalar_tensor_tensor(
            out=stats[:, :, 0], in0=bna[:, :, 0], scalar=-1.0,
            in1=stats[:, :, 1], op0=ALU.mult, op1=ALU.mult),
            reads=b_all_bna + [b_stats[0]], writes=[b_stats[1]], tag="pm")
        stP.close()
        S.fence()


        def quant(xap, bx, n, outap, bout, P=128):
            for p0 in range(0, n, QW):
                quant_piece(xap[:, p0:min(n, p0 + QW)], bx,
                            min(n, p0 + QW) - p0,
                            outap[:, p0:min(n, p0 + QW)], bout, P=P)

        def quant_piece(xap, bx, n, outap, bout, P=128):
            nb = n // 16
            x3 = xap.rearrange("p (b k) -> p b k", k=16)
            S.emit("dv", lambda e: e.reduce_max(
                out=q_m[:P, 0:nb], in_=x3, axis=AX.X,
                apply_absolute_value=True),
                reads=[bx], writes=[b_qm], tag="q_red")
            S.emit("dv", lambda e: e.tensor_scalar(
                out=q_s[:P, 0:nb], in0=q_m[:P, 0:nb], scalar1=1.0 / 6.0,
                scalar2=1e-12, op0=ALU.mult, op1=ALU.max),
                reads=[b_qm], writes=[b_qs], tag="q_s")
            S.emit("dv", lambda e: e.reciprocal(
                out=q_a[:P, 0:nb], in_=q_s[:P, 0:nb]),
                reads=[b_qs], writes=[b_qa], tag="q_a")
            S.emit("dv", lambda e: e.tensor_scalar(
                out=q_a[:P, 0:nb], in0=q_a[:P, 0:nb], scalar1=2.0,
                scalar2=None, op0=ALU.mult),
                reads=[b_qa], writes=[b_qa], tag="q_a2")
            S.emit("dv", lambda e: e.tensor_scalar(
                out=q_h[:P, 0:nb], in0=q_s[:P, 0:nb], scalar1=0.5,
                scalar2=None, op0=ALU.mult),
                reads=[b_qs], writes=[b_qh], tag="q_h")
            abc = q_a[:P, 0:nb].unsqueeze(2).broadcast_to([P, nb, 16])
            S.emit("dv", lambda e: e.tensor_tensor(
                out=q_u[:P, 0:n].rearrange("p (b k) -> p b k", k=16),
                in0=x3, in1=abc, op=ALU.mult),
                reads=[bx, b_qa], writes=[b_qu], tag="q_u")
            S.emit("ac", lambda e: e.activation(
                out=q_au[:P, 0:n], in_=q_u[:P, 0:n], func=AF.Abs),
                reads=[b_qu], writes=[b_qau], tag="q_au")
            S.emit("dv", lambda e: e.tensor_scalar(
                out=q_f1[:P, 0:n], in0=q_au[:P, 0:n], scalar1=4.0,
                scalar2=1.0, op0=ALU.is_gt, op1=ALU.add),
                reads=[b_qau], writes=[b_qf1], tag="q_f1")
            S.emit("dv", lambda e: e.tensor_scalar(
                out=q_f[:P, 0:n], in0=q_au[:P, 0:n], scalar1=8.0,
                scalar2=1.0, op0=ALU.is_gt, op1=ALU.add),
                reads=[b_qau], writes=[b_qf], tag="q_f2")
            S.emit("dv", lambda e: e.tensor_tensor(
                out=q_f[:P, 0:n], in0=q_f[:P, 0:n], in1=q_f1[:P, 0:n],
                op=ALU.mult),
                reads=[b_qf, b_qf1], writes=[b_qf], tag="q_f")
            S.emit("dv", lambda e: e.tensor_scalar(
                out=q_d[:P, 0:n], in0=q_f1[:P, 0:n], scalar1=-0.5,
                scalar2=1.5, op0=ALU.mult, op1=ALU.add),
                reads=[b_qf1], writes=[b_qd], tag="q_d1")
            S.emit("dv", lambda e: e.tensor_tensor(
                out=q_w[:P, 0:n], in0=q_u[:P, 0:n], in1=q_d[:P, 0:n],
                op=ALU.mult),
                reads=[b_qu, b_qd], writes=[b_qw], tag="q_w1")
            S.emit("dv", lambda e: e.tensor_scalar(
                out=q_d[:P, 0:n], in0=q_au[:P, 0:n], scalar1=8.0,
                scalar2=-0.5, op0=ALU.is_gt, op1=ALU.mult),
                reads=[b_qau], writes=[b_qd], tag="q_d2a")
            S.emit("dv", lambda e: e.tensor_scalar(
                out=q_d[:P, 0:n], in0=q_d[:P, 0:n], scalar1=1.0,
                scalar2=None, op0=ALU.add),
                reads=[b_qd], writes=[b_qd], tag="q_d2b")
            S.emit("dv", lambda e: e.tensor_tensor(
                out=q_w[:P, 0:n], in0=q_w[:P, 0:n], in1=q_d[:P, 0:n],
                op=ALU.mult),
                reads=[b_qw, b_qd], writes=[b_qw], tag="q_w2")
            S.emit("dv", lambda e: e.tensor_scalar(
                out=q_u[:P, 0:n], in0=q_w[:P, 0:n], scalar1=64.5,
                scalar2=None, op0=ALU.add),
                reads=[b_qw], writes=[b_qu], tag="q_yv")
            S.emit("dv", lambda e: e.tensor_copy(
                out=q_ri[:P, 0:n], in_=q_u[:P, 0:n]),
                reads=[b_qu], writes=[b_qri], tag="q_ri")
            S.emit("ac", lambda e: e.copy(
                out=q_w[:P, 0:n], in_=q_ri[:P, 0:n]),
                reads=[b_qri], writes=[b_qw], tag="q_rf")
            S.emit("dv", lambda e: e.tensor_tensor(
                out=q_d[:P, 0:n], in0=q_w[:P, 0:n], in1=q_u[:P, 0:n],
                op=ALU.is_gt),
                reads=[b_qw, b_qu], writes=[b_qd], tag="q_c")
            S.emit("dv", lambda e: e.scalar_tensor_tensor(
                out=q_rb[:P, 0:n], in0=q_w[:P, 0:n], scalar=-64.0,
                in1=q_d[:P, 0:n], op0=ALU.add, op1=ALU.subtract),
                reads=[b_qw, b_qd], writes=[b_qrb], tag="q_fix")
            S.emit("dv", lambda e: e.tensor_tensor(
                out=q_rb[:P, 0:n], in0=q_rb[:P, 0:n], in1=q_f[:P, 0:n],
                op=ALU.mult),
                reads=[b_qrb, b_qf], writes=[b_qrb], tag="q_z")
            hbc = q_h[:P, 0:nb].unsqueeze(2).broadcast_to([P, nb, 16])
            S.emit("dv", lambda e: e.tensor_tensor(
                out=outap.rearrange("p (b k) -> p b k", k=16),
                in0=q_rb[:P, 0:n].rearrange("p (b k) -> p b k", k=16),
                in1=hbc, op=ALU.mult),
                reads=[b_qrb, b_qh], writes=[bout], tag="q_y")

        NJOB = (QSH + 127) // 128

        def wq_job(j):
            pi = j % 2
            r0 = j * 128
            P = min(128, QSH - r0)
            S.emit("sy", lambda e: e.dma_start(
                out=wq_x[pi][:P, :], in_=qw_in[r0:r0 + P, :]),
                writes=[b_wqx[pi]], dma=True, tag=f"wl{j}")
            quant(wq_x[pi][:P, :], b_wqx[pi], 1024, wq_o[:P, :], b_wqo, P=P)
            S.emit("sy", lambda e: e.dma_start(
                out=qw_my[r0:r0 + P, :], in_=wq_o[:P, :]),
                reads=[b_wqo], writes=[b_qwmy], dma=True, tag=f"ws{j}")

        st1 = contextlib.ExitStack()
        x_t = [sbt(st1, f"x_t{i}", [128, D], F32) for i in range(2)]
        xn_t = sbt(st1, "xn_t", [128, D], BF16)
        xn_f = [sbt(st1, f"xn_f{i}", [128, 8, 144], BF16) for i in range(2)]
        h_f = sbt(st1, "h_f", [128, 8, 128], BF16)
        h_acc = sbt(st1, "h_acc", [128, 8, 128], BF16)
        vst = sbt(st1, "vst", [128, D], BF16)
        ksq = sbt(st1, "ksq", [128, 512], BF16)
        bsc = sbt(st1, "bsc", [128, 4], F32)
        b_xt = [Buf(f"x{i}") for i in range(2)]
        b_xnt = Buf("xnt")
        b_xnf = [[Buf(f"xnf{i}_{d}") for d in range(9)] for i in range(2)]
        b_hf = Buf("hf")
        b_hacc = Buf("hacc")
        b_vst = [Buf("vst0"), Buf("vst1")]
        b_ksq = Buf("ksq")
        b_bsc = Buf("bsc")

        ps1 = [pst(st1, f"ps1_{i}", [128, 512], F32) for i in range(7)]
        b_ps1 = [Buf(f"p1{i}") for i in range(7)]

        S.emit("gp", lambda e: e.memset(xn_f[0][:, :, 13:16], 0.0),
               writes=[b_xnf[0][8]], tag="halo0")

        def p1_tile(t):
            pi = t % 2
            own = t >= OWN0
            S.emit("sy", lambda e: e.dma_start(
                out=x_t[pi][:], in_=xseq[t * 128:(t + 1) * 128, :]),
                writes=[b_xt[pi]], dma=True, tag=f"lx{t}")
            S.emit("ac", lambda e: e.activation(
                out=xn_t[:], in_=x_t[pi][:], func=AF.Identity,
                scale=stats[:, t, 1:2], bias=stats[:, t, 0:1]),
                reads=[b_xt[pi], b_stats[0], b_stats[1]], writes=[b_xnt],
                tag=f"ln{t}")
            S.emit("ac", lambda e: e.dma_start(
                out=xn_f[pi][:, :, 16:144], in_=xn_t[:], transpose=True),
                reads=[b_xnt], writes=list(b_xnf[pi][0:8]), dma=True,
                xpose=True, tag=f"tx{t}")
            for i in range(4):
                for dc in range(8):
                    if i == 0:
                        S.emit("dv", lambda e, dc=dc: e.tensor_scalar(
                            out=h_acc[:, dc, :], in0=xn_f[pi][:, dc, 13:141],
                            scalar1=convw[:, dc, 0:1], scalar2=None,
                            op0=ALU.mult),
                            reads=[b_xnf[pi][dc], b_xnf[pi][8], b_convw],
                            writes=[b_hacc],
                            tag=f"cv{t}_{dc}_0")
                    else:
                        S.emit("dv", lambda e, dc=dc, i=i:
                               e.scalar_tensor_tensor(
                                   out=h_acc[:, dc, :],
                                   in0=xn_f[pi][:, dc, 13 + i:141 + i],
                                   scalar=convw[:, dc, i:i + 1],
                                   in1=h_acc[:, dc, :],
                                   op0=ALU.mult, op1=ALU.add),
                               reads=[b_xnf[pi][dc], b_xnf[pi][8], b_convw,
                                      b_hacc],
                               writes=[b_hacc], tag=f"cv{t}_{dc}_{i}")
            S.emit("ac", lambda e: e.activation(
                out=h_f[:].rearrange("p a n -> p (a n)"),
                in_=h_acc[:].rearrange("p a n -> p (a n)"), func=AF.Silu),
                reads=[b_hacc], writes=[b_hf], tag=f"si{t}")

            if t + 1 < NCH:
                S.emit("dv", lambda e: e.tensor_copy(
                    out=xn_f[1 - pi][:, :, 13:16],
                    in_=xn_f[pi][:, :, 141:144]),
                    reads=list(b_xnf[pi][0:8]), writes=[b_xnf[1 - pi][8]],
                    tag=f"hl{t}")
            pieces = [(512, 1024, 1), (1024, 1536, 2), (1536, 2048, 3),
                      (3072, 3076, 6)]
            if own:
                pieces = [(0, 512, 0)] + pieces + \
                         [(2048, 2560, 4), (2560, 3072, 5)]
            for dc in range(8):
                for (c0, c1, pb) in pieces:
                    S.emit("te", lambda e, dc=dc, c0=c0, c1=c1, pb=pb:
                           e.matmul(ps1[pb][:, 0:c1 - c0], h_f[:, dc, :],
                                    Wcat[:, dc, c0:c1],
                                    start=(dc == 0), stop=(dc == 7)),
                           reads=[b_hf, b_wcat], writes=[b_ps1[pb]],
                           tag=f"mm{t}_{dc}_{pb}")
            S.emit("ac", lambda e, t=t: e.copy(
                out=k_raw[:, t, :], in_=ps1[1][:]),
                reads=[b_ps1[1]], writes=[b_kraw[t]], tag=f"ek{t}")
            S.emit("ac", lambda e: e.activation(
                out=ksq[:], in_=ps1[1][:], func=AF.Square),
                reads=[b_ps1[1]], writes=[b_ksq], tag=f"k2{t}")
            S.emit("dv", lambda e, t=t: e.reduce_sum(
                out=s2k_a[:, t, :],
                in_=ksq[:].rearrange("p (h n) -> p h n", n=128), axis=AX.X),
                reads=[b_ksq], writes=[b_s2k[t]], tag=f"k3{t}")
            S.emit("dv", lambda e: e.tensor_copy(
                out=vst[:, 0:512], in_=ps1[2][:]),
                reads=[b_ps1[2]], writes=[b_vst[0]], tag=f"ev1{t}")
            S.emit("ac", lambda e: e.copy(
                out=vst[:, 512:1024], in_=ps1[3][:]),
                reads=[b_ps1[3]], writes=[b_vst[1]], tag=f"ev2{t}")
            S.emit("sy", lambda e, t=t: e.dma_start(
                out=v_h[t], in_=vst[:]),
                reads=[b_vst[0], b_vst[1]], writes=[b_vh[t]], dma=True,
                tag=f"sv{t}")
            S.emit("ac", lambda e: e.activation(
                out=bsc[:], in_=ps1[6][:, 0:4], func=AF.Tanh, scale=0.5),
                reads=[b_ps1[6]], writes=[b_bsc], tag=f"bt{t}")
            S.emit("dv", lambda e, t=t: e.tensor_scalar(
                out=beta_a[:, t, :], in0=bsc[:], scalar1=0.5, scalar2=0.5,
                op0=ALU.mult, op1=ALU.add),
                reads=[b_bsc], writes=[b_beta[t]], tag=f"bs{t}")

            if own:
                tt = t - OWN0
                S.emit("ac", lambda e, t=t: e.copy(
                    out=q_raw[:, t - OWN0, :], in_=ps1[0][:]),
                    reads=[b_ps1[0]], writes=[b_qraw[tt]], tag=f"eq{t}")
                S.emit("ac", lambda e: e.activation(
                    out=ksq[:], in_=ps1[0][:], func=AF.Square),
                    reads=[b_ps1[0]], writes=[b_ksq], tag=f"q2{t}")
                S.emit("dv", lambda e, tt=tt: e.reduce_sum(
                    out=s2q_a[:, tt, :],
                    in_=ksq[:].rearrange("p (h n) -> p h n", n=128),
                    axis=AX.X),
                    reads=[b_ksq], writes=[b_s2q[tt]], tag=f"q3{t}")
                S.emit("ac", lambda e, tt=tt: e.activation(
                    out=g_s[:, tt, 0:512], in_=ps1[4][:], func=AF.Silu),
                    reads=[b_ps1[4]], writes=[b_gs[tt]], tag=f"eg1{t}")
                S.emit("ac", lambda e, tt=tt: e.activation(
                    out=g_s[:, tt, 512:1024], in_=ps1[5][:], func=AF.Silu),
                    reads=[b_ps1[5]], writes=[b_gs[tt]], tag=f"eg2{t}")

        wq_pos = [0]
        for t in range(NCH):
            p1_tile(t)
            tgt = (t + 1) * NJOB // NCH
            while wq_pos[0] < tgt:
                wq_job(wq_pos[0])
                wq_pos[0] += 1

        st1.close()
        stW.close()
        S.fence()

        if USE_COLLECTIVE:
            S.emit("gp", lambda e: e.collective_compute(
                "AllGather", ALU.bypass,
                ins=[qw_my[:]], outs=[qw_all[:]],
                replica_groups=[list(range(8))]),
                reads=[b_qwmy], writes=[b_qwall], tag="ag")
        else:
            b_qwall.w = list(b_qwmy.w)

        st2 = contextlib.ExitStack()
        def hpos(h):
            return (h % 2) * 2 + (h // 2)
        kn_t = sbt(st2, "kn_t", [128, NH, 128], BF16)
        kn_f = sbt(st2, "kn_f", [128, NH, 128], BF16)
        kbn_f = sbt(st2, "kbn_f", [128, NH, 128], BF16)
        kbn_t = sbt(st2, "kbn_t", [128, NH, 128], BF16)
        qn_t = sbt(st2, "qn_t", [128, NH, 128], BF16)
        qn_f = sbt(st2, "qn_f", [128, NH, 128], BF16)
        v_t = [sbt(st2, f"v_t{i}", [128, D], BF16) for i in range(2)]
        vb_t = sbt(st2, "vb_t", [128, NH, 256], BF16)
        Xp = [sbt(st2, f"Xp{i}", [128, NH, 128], BF16) for i in range(2)]
        XpT = [sbt(st2, f"XpT{i}", [128, NH, 128], BF16) for i in range(2)]
        Rsn = sbt(st2, "Rsn", [128, NH, 256], BF16)
        U_t = sbt(st2, "U_t", [128, NH, 256], BF16)
        Pt_s = sbt(st2, "Pt_s", [128, NH, 128], BF16)
        settle = sbt(st2, "settle", [128, 64], BF16)
        osb = sbt(st2, "osb", [128, 512], BF16)
        b_osb = Buf("osb")
        b_settle = Buf("settle")
        S_ma = sbt(st2, "S_ma", [128, NH, 256], F32)
        b_Sma = [Buf(f"Sma{i}") for i in range(2)]
        inv_k = sbt(st2, "inv_k", [128, 4], F32)
        inv_q = sbt(st2, "inv_q", [128, 4], F32)
        nbk = sbt(st2, "nbk", [128, 4], F32)
        invr = sbt(st2, "invr", [128, 4], F32)
        osq = sbt(st2, "osq", [128, 512], BF16)
        r2 = sbt(st2, "r2", [128, 4], F32)
        gat = sbt(st2, "gat", [128, D], BF16)
        b_knt, b_kbnt, b_qnt = (Buf(n) for n in ("knt", "kbnt", "qnt"))
        b_knf = [Buf(f"knf{h}") for h in range(NH)]
        b_kbnf = [Buf(f"kbnf{h}") for h in range(NH)]
        b_qnf = [Buf(f"qnf{h}") for h in range(NH)]
        b_vt = [Buf(f"vt{i}") for i in range(2)]
        b_vbt = Buf("vbt")
        b_Xp = [Buf(f"Xp{i}") for i in range(2)]
        b_XpT = [Buf(f"XpT{i}") for i in range(2)]
        b_Rsn = [Buf("Rsn0"), Buf("Rsn1")]
        b_Ut = [Buf("Ut0"), Buf("Ut1")]
        b_Pts = Buf("Pts")
        b_invk, b_invq, b_nbk, b_invr = \
            (Buf(n) for n in ("invk", "invq", "nbk", "invr"))
        b_osq = Buf("osq")
        b_r2 = Buf("r2")
        b_gat = [Buf(f"gat{h}") for h in range(NH)]

        S.emit("gp", lambda e: e.memset(S_ma[:], 0.0),
               writes=list(b_Sma), tag="Sma0")
        S_ps = [pst(st2, f"S_ps{i}", [128, 512], F32) for i in range(2)]
        R_ps = [pst(st2, f"R_ps{i}", [128, 512], F32) for i in range(2)]
        P_ps = [pst(st2, f"P_ps{i}", [128, 512], F32) for i in range(2)]
        o_ps = [pst(st2, f"o_ps{i}", [128, 512], F32) for i in range(2)]
        b_Sps = [Buf(f"Sps{i}") for i in range(2)]
        b_Rps = [Buf(f"Rps{i}") for i in range(2)]
        b_Pps = [Buf(f"Pps{i}") for i in range(2)]
        b_ops = [Buf(f"ops{i}") for i in range(2)]

        def s_slot(h):
            return S_ps[h % 2][:, (h // 2) * 256:(h // 2) * 256 + 256]

        def r_slot(h):
            return R_ps[h % 2][:, (h // 2) * 256:(h // 2) * 256 + 256]

        def o_slot(h):
            return o_ps[h % 2][:, (h // 2) * 256:(h // 2) * 256 + 256]

        def prod_mm(pbank, h, lhsT, rhs, reads, tag):
            S.emit("te", lambda e: e.matmul(
                P_ps[pbank][:, h * 128:(h + 1) * 128], lhsT, rhs,
                start=True, stop=True),
                reads=reads, writes=[b_Pps[pbank]], tag=tag)

        def sc_chunk(c):
            own = c >= OWN0
            tt = c - OWN0
            pi = c % 2
            S.emit("sy", lambda e, c=c: e.dma_start(
                out=v_t[pi][:], in_=v_h[c]),
                reads=[b_vh[c]], writes=[b_vt[pi]], dma=True, tag=f"lv{c}")
            S.emit("dv", lambda e, c=c: e.tensor_scalar(
                out=inv_k[:], in0=s2k_a[:, c, :], scalar1=1e-24, scalar2=None,
                op0=ALU.max), reads=[b_s2k[c]], writes=[b_invk], tag=f"ik{c}")
            S.emit("dv", lambda e: e.reciprocal(out=inv_k[:], in_=inv_k[:]),
                   reads=[b_invk], writes=[b_invk], tag=f"ik2{c}")
            S.emit("ac", lambda e: e.activation(
                out=inv_k[:], in_=inv_k[:], func=AF.Sqrt),
                reads=[b_invk], writes=[b_invk], tag=f"ik3{c}")
            S.emit("dv", lambda e, c=c: e.tensor_tensor(
                out=nbk[:], in0=inv_k[:], in1=beta_a[:, c, :], op=ALU.mult),
                reads=[b_invk, b_beta[c]], writes=[b_nbk], tag=f"nb{c}")
            if own:
                S.emit("dv", lambda e, tt=tt: e.tensor_scalar(
                    out=inv_q[:], in0=s2q_a[:, tt, :], scalar1=1e-24,
                    scalar2=None, op0=ALU.max),
                    reads=[b_s2q[tt]], writes=[b_invq], tag=f"iq{c}")
                S.emit("dv", lambda e: e.reciprocal(out=inv_q[:],
                                                    in_=inv_q[:]),
                       reads=[b_invq], writes=[b_invq], tag=f"iq2{c}")
                S.emit("ac", lambda e: e.activation(
                    out=inv_q[:], in_=inv_q[:], func=AF.Sqrt),
                    reads=[b_invq], writes=[b_invq], tag=f"iq3{c}")
            for h in range(NH):
                S.emit("dv", lambda e, c=c, h=h: e.tensor_scalar(
                    out=kn_t[:, h, :], in0=k_raw[:, c, h * 128:(h + 1) * 128],
                    scalar1=inv_k[:, h:h + 1], scalar2=None, op0=ALU.mult),
                    reads=[b_kraw[c], b_invk], writes=[b_knt], tag=f"kn{c}{h}")
                S.emit("dv", lambda e, c=c, h=h: e.tensor_scalar(
                    out=kbn_t[:, h, :],
                    in0=k_raw[:, c, h * 128:(h + 1) * 128],
                    scalar1=nbk[:, h:h + 1], scalar2=-1.0,
                    op0=ALU.mult, op1=ALU.mult),
                    reads=[b_kraw[c], b_nbk], writes=[b_kbnt], tag=f"kb{c}{h}")
                S.emit("dv", lambda e, c=c, h=h: e.tensor_scalar(
                    out=vb_t[:, h, :],
                    in0=v_t[pi][:, h * 256:(h + 1) * 256],
                    scalar1=beta_a[:, c, h:h + 1], scalar2=None,
                    op0=ALU.mult),
                    reads=[b_vt[pi], b_beta[c]], writes=[b_vbt],
                    tag=f"vb{c}{h}")
                if own:
                    S.emit("dv", lambda e, tt=tt, h=h: e.tensor_scalar(
                        out=qn_t[:, h, :],
                        in0=q_raw[:, tt, h * 128:(h + 1) * 128],
                        scalar1=inv_q[:, h:h + 1], scalar2=None,
                        op0=ALU.mult),
                        reads=[b_qraw[tt], b_invq], writes=[b_qnt],
                        tag=f"qn{c}{h}")
            S.emit("ac", lambda e: e.dma_start(
                out=kn_f[:], in_=kn_t[:].rearrange("p a n -> p (a n)"),
                transpose=True),
                reads=[b_knt], writes=list(b_knf), dma=True, xpose=True,
                tag=f"tk{c}")
            S.emit("sy", lambda e: e.dma_start(
                out=kbn_f[:], in_=kbn_t[:].rearrange("p a n -> p (a n)"),
                transpose=True),
                reads=[b_kbnt], writes=list(b_kbnf), dma=True, xpose=True,
                tag=f"tb{c}")
            if own:
                S.emit("ac", lambda e: e.dma_start(
                    out=qn_f[:], in_=qn_t[:].rearrange("p a n -> p (a n)"),
                    transpose=True),
                    reads=[b_qnt], writes=list(b_qnf), dma=True, xpose=True,
                    tag=f"tq{c}")
            for h in range(NH):
                prod_mm(0, h, kbn_f[:, h, :], kn_f[:, h, :],
                        [b_kbnf[h], b_knf[h]], f"X{c}{h}")
            S.emit("dv", lambda e: e.tensor_tensor(
                out=Xp[0][:].rearrange("p a n -> p (a n)"),
                in0=P_ps[0][:, 0:512], in1=maskSL[:], op=ALU.mult),
                reads=[b_Pps[0], b_maskSL], writes=[b_Xp[0]], tag=f"eX{c}")
            for h in range(NH):
                prod_mm(1, h, kn_f[:, h, :], kbn_f[:, h, :],
                        [b_knf[h], b_kbnf[h]], f"XT{c}{h}")
            S.emit("dv", lambda e: e.tensor_tensor(
                out=XpT[0][:].rearrange("p a n -> p (a n)"),
                in0=P_ps[1][:, 0:512], in1=maskSU[:], op=ALU.mult),
                reads=[b_Pps[1], b_maskSU], writes=[b_XpT[0]], tag=f"eXT{c}")
            for h in range(NH):
                S.emit("te", lambda e, h=h: e.matmul(
                    r_slot(h), i_sb[:], vb_t[:, h, :],
                    start=(h < 2), stop=False, skip_group_check=True),
                    reads=[b_isb, b_vbt], writes=[b_Rps[h % 2]],
                    tag=f"Ri{c}{h}")
            for h in range(NH):
                S.emit("te", lambda e, h=h: e.matmul(
                    r_slot(h), kbn_f[:, h, :], S_bf[:, hpos(h), :],
                    start=False, stop=(h >= 2), skip_group_check=True),
                    reads=[b_kbnf[h], b_Sbf[hpos(h)]], writes=[b_Rps[h % 2]],
                    tag=f"Rs{c}{h}")
            for lev in range(NLEV):
                cur = lev % 2
                S.emit("dv", lambda e: e.tensor_copy(
                    out=Rsn[:, 0:2, :].rearrange("p a n -> p (a n)"),
                    in_=R_ps[0][:]),
                    reads=[b_Rps[0]], writes=[b_Rsn[0]], tag=f"sn{c}_{lev}a")
                S.emit("dv", lambda e: e.tensor_copy(
                    out=Rsn[:, 2:4, :].rearrange("p a n -> p (a n)"),
                    in_=R_ps[1][:]),
                    reads=[b_Rps[1]], writes=[b_Rsn[1]], tag=f"sn{c}_{lev}b")
                if lev < NLEV - 1:
                    nxt = 1 - cur
                    last = (lev == NLEV - 2)
                    if not last:
                        for h in range(NH):
                            prod_mm(0, h, XpT[cur][:, h, :], Xp[cur][:, h, :],
                                    [b_XpT[cur], b_Xp[cur]], f"sq{c}_{lev}{h}")
                        S.emit("dv", lambda e, nxt=nxt: e.tensor_copy(
                            out=Xp[nxt][:].rearrange("p a n -> p (a n)"),
                            in_=P_ps[0][:, 0:512]),
                            reads=[b_Pps[0]], writes=[b_Xp[nxt]],
                            tag=f"esq{c}_{lev}")
                    for h in range(NH):
                        prod_mm(1, h, Xp[cur][:, h, :], XpT[cur][:, h, :],
                                [b_Xp[cur], b_XpT[cur]], f"sqT{c}_{lev}{h}")
                    S.emit("dv", lambda e, nxt=nxt: e.tensor_copy(
                        out=XpT[nxt][:].rearrange("p a n -> p (a n)"),
                        in_=P_ps[1][:, 0:512]),
                        reads=[b_Pps[1]], writes=[b_XpT[nxt]],
                        tag=f"esqT{c}_{lev}")
                for h in range(NH):
                    S.emit("te", lambda e, h=h, cur=cur: e.matmul(
                        r_slot(h), XpT[cur][:, h, :], Rsn[:, hpos(h), :],
                        start=False, stop=(h >= 2),
                        skip_group_check=True),
                        reads=[b_XpT[cur], b_Rsn[h % 2]],
                        writes=[b_Rps[h % 2]],
                        tag=f"Ra{c}_{lev}{h}")
            S.emit("dv", lambda e: e.tensor_copy(
                out=U_t[:, 0:2, :].rearrange("p a n -> p (a n)"),
                in_=R_ps[0][:]),
                reads=[b_Rps[0]], writes=[b_Ut[0]], tag=f"eU{c}a")
            S.emit("dv", lambda e: e.tensor_copy(
                out=U_t[:, 2:4, :].rearrange("p a n -> p (a n)"),
                in_=R_ps[1][:]),
                reads=[b_Rps[1]], writes=[b_Ut[1]], tag=f"eU{c}b")
            S.emit("dv", lambda e: e.tensor_copy(
                out=settle[:], in_=U_t[:, 0, 0:64]),
                reads=[b_Ut[0], b_Ut[1]], writes=[b_settle], tag=f"stl{c}")
            if own:
                for h in range(NH):
                    prod_mm(0, h, kn_f[:, h, :], qn_f[:, h, :],
                            [b_knf[h], b_qnf[h]], f"Pt{c}{h}")
                S.emit("dv", lambda e: e.tensor_tensor(
                    out=Pt_s[:].rearrange("p a n -> p (a n)"),
                    in0=P_ps[0][:, 0:512], in1=maskLE[:], op=ALU.mult),
                    reads=[b_Pps[0], b_maskLE], writes=[b_Pts], tag=f"eP{c}")
                for h in range(NH):
                    S.emit("te", lambda e, h=h: e.matmul(
                        o_slot(h), qn_f[:, h, :], S_bf[:, hpos(h), :],
                        start=True, stop=True, skip_group_check=True),
                        reads=[b_qnf[h], b_Sbf[hpos(h)]],
                        writes=[b_ops[h % 2]],
                        tag=f"o1{c}{h}")
                for h in range(NH):
                    S.emit("te", lambda e, h=h: e.matmul(
                        s_slot(h), Pt_s[:, h, :], U_t[:, hpos(h), :],
                        start=True, stop=True, skip_group_check=True),
                        reads=[b_Pts, b_Ut[h % 2], b_settle],
                        writes=[b_Sps[h % 2]],
                        tag=f"o2{c}{h}")
                for half in range(2):
                    S.emit("dv", lambda e, half=half: e.tensor_copy(
                        out=osb[:], in_=o_ps[half][:]),
                        reads=[b_ops[half]], writes=[b_osb],
                        tag=f"oe{c}{half}")
                    S.emit("dv", lambda e, half=half: e.scalar_tensor_tensor(
                        out=osb[:], in0=S_ps[half][:], scalar=1.0,
                        in1=osb[:], op0=ALU.mult, op1=ALU.add),
                        reads=[b_Sps[half], b_osb], writes=[b_osb],
                        tag=f"oc{c}{half}")
                    S.emit("ac", lambda e, half=half: e.activation(
                        out=osq[:], in_=osb[:], func=AF.Square),
                        reads=[b_osb], writes=[b_osq], tag=f"os{c}{half}")
                    S.emit("dv", lambda e, half=half: e.reduce_sum(
                        out=r2[:, half * 2:half * 2 + 2],
                        in_=osq[:].rearrange("p (h n) -> p h n", n=256),
                        axis=AX.X),
                        reads=[b_osq], writes=[b_r2], tag=f"or{c}{half}")
                    S.emit("dv", lambda e, half=half: e.tensor_copy(
                        out=gat[:].rearrange("p (a n) -> p a n", n=256)[
                            :, half * 2:half * 2 + 2, :].rearrange(
                            "p a n -> p (a n)"), in_=osb[:]),
                        reads=[b_osb],
                        writes=[b_gat[half], b_gat[half + 2]],
                        tag=f"ob{c}{half}")
                S.emit("dv", lambda e: e.tensor_scalar(
                    out=invr[:], in0=r2[:], scalar1=1.0 / 256.0, scalar2=1e-5,
                    op0=ALU.mult, op1=ALU.add),
                    reads=[b_r2], writes=[b_invr], tag=f"om{c}")
                S.emit("dv", lambda e: e.reciprocal(out=invr[:], in_=invr[:]),
                       reads=[b_invr], writes=[b_invr], tag=f"oc{c}")
                S.emit("ac", lambda e: e.activation(
                    out=invr[:], in_=invr[:], func=AF.Sqrt),
                    reads=[b_invr], writes=[b_invr], tag=f"oq{c}")
                for h in range(NH):
                    S.emit("dv", lambda e, h=h: e.tensor_scalar(
                        out=gat[:, hpos(h) * 256:hpos(h) * 256 + 256],
                        in0=gat[:, hpos(h) * 256:hpos(h) * 256 + 256],
                        scalar1=invr[:, hpos(h):hpos(h) + 1], scalar2=None,
                        op0=ALU.mult),
                        reads=[b_gat[hpos(h)], b_invr],
                        writes=[b_gat[hpos(h)]],
                        tag=f"og{c}{h}")
                for h in range(NH):
                    S.emit("dv", lambda e, h=h, tt=tt: e.tensor_tensor(
                        out=gat[:, hpos(h) * 256:hpos(h) * 256 + 256],
                        in0=gat[:, hpos(h) * 256:hpos(h) * 256 + 256],
                        in1=g_s[:, tt, h * 256:(h + 1) * 256], op=ALU.mult),
                        reads=[b_gat[hpos(h)], b_gs[tt]],
                        writes=[b_gat[hpos(h)]],
                        tag=f"gg{c}{h}")
                S.emit("sy", lambda e, tt=tt: e.dma_start(
                    out=g_h[tt], in_=gat[:]),
                    reads=list(b_gat), writes=[b_gh[tt]], dma=True,
                    tag=f"sg{c}")
            for h in range(NH):
                S.emit("te", lambda e, h=h: e.matmul(
                    s_slot(h), kn_t[:, h, :], U_t[:, hpos(h), :],
                    start=True, stop=True, skip_group_check=True),
                    reads=[b_knt, b_Ut[h % 2], b_settle],
                    writes=[b_Sps[h % 2]],
                    tag=f"Su{c}{h}")
            for half in range(2):
                S.emit("dv", lambda e, half=half: e.scalar_tensor_tensor(
                    out=S_ma[:, half * 2:half * 2 + 2, :].rearrange(
                        "p a n -> p (a n)"),
                    in0=S_ps[half][:], scalar=1.0,
                    in1=S_ma[:, half * 2:half * 2 + 2, :].rearrange(
                        "p a n -> p (a n)"),
                    op0=ALU.mult, op1=ALU.add),
                    reads=[b_Sps[half], b_Sma[half]],
                    writes=[b_Sma[half]],
                    tag=f"aS{c}{half}")
                S.emit("dv", lambda e, half=half: e.tensor_copy(
                    out=S_bf[:, half * 2:half * 2 + 2, :].rearrange(
                        "p a n -> p (a n)"),
                    in_=S_ma[:, half * 2:half * 2 + 2, :].rearrange(
                        "p a n -> p (a n)")),
                    reads=[b_Sma[half]],
                    writes=[b_Sbf[half * 2], b_Sbf[half * 2 + 1]],
                    tag=f"eS{c}{half}")
            if c == 0:
                dbg_dump(0, S_bf[:].rearrange("p a n -> p (a n)"),
                         list(b_Sbf))
            if c == 8:
                dbg_dump(1, S_bf[:].rearrange("p a n -> p (a n)"),
                         list(b_Sbf))

        for c in range(NCH):
            sc_chunk(c)
        st2.close()
        st12.close()
        S.fence()

        st3 = contextlib.ExitStack()
        wgqT = sbt(st3, "wgqT", [128, 8, IF], BF16)
        wuqT = sbt(st3, "wuqT", [128, 8, IF], BF16)
        wdqT = sbt(st3, "wdqT", [128, 22, D], BF16)
        Wo_b = sbt(st3, "Wo_b", [128, 8, D], BF16)
        b_wgqT = [Buf(f"wgqT{d}") for d in range(8)]
        b_wuqT = [Buf(f"wuqT{d}") for d in range(8)]
        b_wdqT = [Buf(f"wdqT{i}") for i in range(22)]
        b_wob = Buf("wob")
        qw_flat = qw_all.rearrange("a b -> (a b)")

        x3_t = [sbt(st3, f"x3_t{i}", [128, D], F32) for i in range(2)]
        b_x3 = [Buf(f"x3{i}") for i in range(2)]
        for dc in range(8):
            S.emit("sy", lambda e, dc=dc: e.dma_start(
                out=x3_t[dc % 2][:],
                in_=w_in["Wo"][dc * 128:(dc + 1) * 128, :].rearrange(
                    "(a p) n -> p (a n)", p=128)),
                writes=[b_x3[dc % 2]], dma=True, tag=f"lwo{dc}")
            S.emit("dv", lambda e, dc=dc: e.tensor_scalar(
                out=Wo_b[:, dc, :], in0=x3_t[dc % 2][:],
                scalar1=nwc[:, dc % 2:dc % 2 + 1], scalar2=None,
                op0=ALU.mult),
                reads=[b_x3[dc % 2], b_nwc], writes=[b_wob], tag=f"wos{dc}")
        S.emit("sy", lambda e: e.dma_start(
            out=wgqT[:], in_=qw_all[0:IF, 0:1024], transpose=True),
            reads=[b_qwall], writes=list(b_wgqT), dma=True, xpose=True,
            tag="lwg")
        S.emit("sy", lambda e: e.dma_start(
            out=wuqT[:], in_=qw_all[IF:2 * IF, 0:1024], transpose=True),
            reads=[b_qwall], writes=list(b_wuqT), dma=True, xpose=True,
            tag="lwu")
        wd_2d = qw_flat[2 * IF * 1024:].rearrange("(a b) -> a b", b=IF)
        S.emit("ac", lambda e: e.dma_start(
            out=wdqT[:], in_=wd_2d, transpose=True),
            reads=[b_qwall], writes=list(b_wdqT), dma=True, xpose=True,
            tag="lwd")

        xn3 = sbt(st3, "xn3", [128, D], F32)
        xq_t = sbt(st3, "xq_t", [128, D], BF16)
        xq_f = sbt(st3, "xq_f", [128, 8, 128], BF16)
        gg_t = [sbt(st3, "gg_t0", [128, D], BF16)] * 2
        gg_f = sbt(st3, "gg_f", [128, 8, 128], BF16)
        sg_t = sbt(st3, "sg_t", [128, 512], F32)
        h2 = sbt(st3, "h2", [128, 1024], F32)
        h2q = sbt(st3, "h2q", [128, 1024], BF16)
        h2q_f = sbt(st3, "h2q_f", [128, 22, 128], BF16)
        ot = [sbt(st3, "ot0", [128, 512], F32)] * 2
        b_xn3 = Buf("xn3")
        b_xqt = Buf("xqt")
        b_xqf = [Buf(f"xqf{d}") for d in range(8)]
        b_ggt = [Buf(f"ggt{i}") for i in range(2)]
        b_ggf = [Buf(f"ggf{d}") for d in range(8)]
        b_sgt = Buf("sgt")
        b_h2 = Buf("h2")
        b_h2q = Buf("h2q")
        b_h2qf = [Buf(f"h2qf{i}") for i in range(22)]
        _otb = Buf("ot_0")
        b_ot = [[_otb, _otb], [_otb, _otb]]
        b_outh = [Buf(f"oh{t}") for t in range(16)]

        gu_ps = [pst(st3, f"gu_ps{i}", [128, 512], F32) for i in range(4)]
        dn_ps = [pst(st3, f"dn_ps{i}", [128, 512], F32) for i in range(2)]
        b_gups = [Buf(f"gu{i}") for i in range(4)]
        b_dnps = [Buf(f"dn{i}") for i in range(2)]

        def p3_tile(tt):
            pi = tt % 2
            t = tt + OWN0
            S.emit("sy", lambda e, t=t: e.dma_start(
                out=x3_t[pi][:], in_=xseq[t * 128:(t + 1) * 128, :]),
                writes=[b_x3[pi]], dma=True, tag=f"3x{tt}")
            S.emit("ac", lambda e, t=t: e.activation(
                out=xn3[:], in_=x3_t[pi][:], func=AF.Identity,
                scale=stats[:, t, 1:2], bias=stats[:, t, 0:1]),
                reads=[b_x3[pi], b_stats[0], b_stats[1]], writes=[b_xn3],
                tag=f"3n{tt}")
            quant(xn3[:], b_xn3, 1024, xq_t[:], b_xqt)
            if tt == 0:
                dbg_dump(8, xq_t[:], [b_xqt])
                dbg_dump(9, wgqT[:, 0, 0:1024], [b_wgqT[0]])
                dbg_dump(10, wdqT[:, 0, :], [b_wdqT[0]])
                dbg_dump(11, gg_t[pi][:], [b_ggt[pi]])
            S.emit("sy", lambda e: e.dma_start(
                out=xq_f[:], in_=xq_t[:], transpose=True),
                reads=[b_xqt], writes=list(b_xqf), dma=True, xpose=True,
                tag=f"3t{tt}")
            S.emit("ac", lambda e, tt=tt: e.dma_start(
                out=gg_t[pi][:], in_=g_h[tt]),
                reads=[b_gh[tt]], writes=[b_ggt[pi]], dma=True, tag=f"3g{tt}")
            S.emit("ac", lambda e: e.dma_start(
                out=gg_f[:], in_=gg_t[pi][:], transpose=True),
                reads=[b_ggt[pi]], writes=list(b_ggf), dma=True, xpose=True,
                tag=f"3gt{tt}")
            for dc in range(8):
                for half in range(2):
                    S.emit("te", lambda e, dc=dc, half=half: e.matmul(
                        dn_ps[half][:], gg_f[:, dc, :],
                        Wo_b[:, dc, half * 512:(half + 1) * 512],
                        start=(dc == 0), stop=False, skip_group_check=True),
                        reads=[b_ggf[dc], b_wob], writes=[b_dnps[half]],
                        tag=f"wo{tt}_{dc}_{half}")
            pieces = [(i * 512, min(IF, (i + 1) * 512)) for i in range(6)]
            for pidx, (i0, i1) in enumerate(pieces):
                gb = (pidx % 2) * 2
                ub = gb + 1
                for dc in range(8):
                    S.emit("te", lambda e, dc=dc, i0=i0, i1=i1, gb=gb:
                           e.matmul(gu_ps[gb][:, 0:i1 - i0], xq_f[:, dc, :],
                                    wgqT[:, dc, i0:i1],
                                    start=(dc == 0), stop=(dc == 7)),
                           reads=[b_xqf[dc], b_wgqT[dc]],
                           writes=[b_gups[gb]],
                           tag=f"gm{tt}_{pidx}_{dc}")
                    S.emit("te", lambda e, dc=dc, i0=i0, i1=i1, ub=ub:
                           e.matmul(gu_ps[ub][:, 0:i1 - i0], xq_f[:, dc, :],
                                    wuqT[:, dc, i0:i1],
                                    start=(dc == 0), stop=(dc == 7)),
                           reads=[b_xqf[dc], b_wuqT[dc]],
                           writes=[b_gups[ub]],
                           tag=f"um{tt}_{pidx}_{dc}")
                S.emit("ac", lambda e, i0=i0, i1=i1, gb=gb: e.activation(
                    out=sg_t[:, 0:i1 - i0], in_=gu_ps[gb][:, 0:i1 - i0],
                    func=AF.Silu),
                    reads=[b_gups[gb]], writes=[b_sgt], tag=f"sg{tt}_{pidx}")
                S.emit("dv", lambda e, i0=i0, i1=i1, ub=ub, pidx=pidx:
                       e.tensor_tensor(
                    out=h2[:, (pidx % 2) * 512:(pidx % 2) * 512 + i1 - i0],
                    in0=gu_ps[ub][:, 0:i1 - i0],
                    in1=sg_t[:, 0:i1 - i0], op=ALU.mult),
                    reads=[b_gups[ub], b_sgt], writes=[b_h2],
                    tag=f"h2{tt}_{pidx}")
                if pidx % 2 == 1 or i1 == IF:
                    w = 512 + i1 - i0 if pidx % 2 == 1 else i1 - i0
                    lo = i0 - 512 if pidx % 2 == 1 else i0
                    quant_piece(h2[:, 0:w], b_h2, w,
                                h2q[:, 0:w], b_h2q)
                    nic = w // 128
                    S.emit("sy", lambda e, lo=lo, w=w, nic=nic: e.dma_start(
                        out=h2q_f[:, lo // 128:lo // 128 + nic, :],
                        in_=h2q[:, 0:w], transpose=True),
                        reads=[b_h2q],
                        writes=[b_h2qf[(lo // 128) % 22]], dma=True,
                        xpose=True, tag=f"ht{tt}_{lo}")
            for ic in range(22):
                for half in range(2):
                    S.emit("te", lambda e, ic=ic, half=half: e.matmul(
                        dn_ps[half][:], h2q_f[:, ic, :],
                        wdqT[:, ic, half * 512:(half + 1) * 512],
                        start=False, stop=(ic == 21), skip_group_check=True),
                        reads=[b_h2qf[(ic // 8) * 8 % 22], b_wdqT[ic]],
                        writes=[b_dnps[half]],
                        tag=f"dm{tt}_{ic}_{half}")
            if tt == 0:
                dbg_dump(14, gg_f[:].rearrange("p a n -> p (a n)"),
                         list(b_ggf))
            for half in range(2):
                S.emit("dv", lambda e, half=half: e.scalar_tensor_tensor(
                    out=ot[pi][:, 0:512],
                    in0=dn_ps[half][:], scalar=1.0,
                    in1=x3_t[pi][:, half * 512:(half + 1) * 512],
                    op0=ALU.mult, op1=ALU.add),
                    reads=[b_dnps[half], b_x3[pi]], writes=[b_ot[pi][half]],
                    tag=f"rs{tt}{half}")
                S.emit("sy", lambda e, tt=tt, half=half: e.dma_start(
                    out=out_h[tt * 128:(tt + 1) * 128,
                              half * 512:(half + 1) * 512],
                    in_=ot[pi][:, 0:512]),
                    reads=[b_ot[pi][half]], writes=[b_outh[tt]], dma=True,
                    tag=f"so{tt}{half}")

        for tt in range(16):
            p3_tile(tt)
        st3.close()
        stQ.close()
        stW.close()

        fin = S.barrier_dep()
        fbuf = Buf("fin")
        fbuf.w = fin
        S.emit("sy", lambda e: e.nop(), reads=[fbuf], tag="fin")

        S.flush(block, sems)
    outer.close()
    return nc


_NC_CACHE = {}


USE = None


def kernel(L0_attn_hidden_states, L0_attn_conv_w, L0_attn_Wq, L0_attn_Wk,
           L0_attn_Wv, L0_attn_Wb, L0_attn_Wg, L0_attn_norm_w, L0_attn_Wo,
           L0_ffn_w_gate, L0_ffn_w_up, L0_ffn_w_down):
    x = np.asarray(L0_attn_hidden_states, dtype=np.float32)
    w = {
        "conv_w": np.ascontiguousarray(L0_attn_conv_w, dtype=np.float32),
        "Wq": np.ascontiguousarray(L0_attn_Wq, dtype=np.float32),
        "Wk": np.ascontiguousarray(L0_attn_Wk, dtype=np.float32),
        "Wv": np.ascontiguousarray(L0_attn_Wv, dtype=np.float32),
        "Wb": np.ascontiguousarray(L0_attn_Wb, dtype=np.float32),
        "Wg": np.ascontiguousarray(L0_attn_Wg, dtype=np.float32),
        "norm_w": np.ascontiguousarray(L0_attn_norm_w, dtype=np.float32),
        "Wo": np.ascontiguousarray(np.concatenate(
            [np.asarray(L0_attn_Wo, np.float32)[0:256],
             np.asarray(L0_attn_Wo, np.float32)[512:768],
             np.asarray(L0_attn_Wo, np.float32)[256:512],
             np.asarray(L0_attn_Wo, np.float32)[768:1024]], axis=0)),
    }
    flat = np.concatenate([
        np.asarray(L0_ffn_w_gate, np.float32).reshape(-1, 1024),
        np.asarray(L0_ffn_w_up, np.float32).reshape(-1, 1024),
        np.asarray(L0_ffn_w_down, np.float32).reshape(-1, 1024)], axis=0)
    if "nc" not in _NC_CACHE:
        _NC_CACHE["nc"] = build_program()
    nc = _NC_CACHE["nc"]
    in_maps = []
    for core in range(8):
        b, hf = core // 2, core % 2
        if hf == 0:
            seq = np.concatenate(
                [np.zeros((2048, D), np.float32), x[b, 0:2048]], 0)
        else:
            seq = np.ascontiguousarray(x[b])
        tri = np.tril(np.ones((128, 128), np.float32), -1)
        cns = np.stack([np.tile(tri, (1, 4)),
                        np.tile(tri.T, (1, 4)),
                        np.tile(np.triu(np.ones((128, 128), np.float32)),
                                (1, 4)),
                        np.concatenate([np.eye(128, dtype=np.float32),
                                        np.zeros((128, 384), np.float32)], 1)],
                       axis=1)
        if USE_COLLECTIVE:
            qsh = np.ascontiguousarray(flat[core * QSH:(core + 1) * QSH])
        else:
            qsh = flat
        im = {"xseq": seq, "qw_in": qsh, "consts": cns}
        im.update(w)
        in_maps.append(im)
    res = run_bass_kernel_spmd(nc, in_maps, core_ids=list(range(8)))
    if DBG:
        sys.modules[__name__].dbg_results = [r.get("dbg") for r in res.results]
    out = np.empty((B, L, D), np.float32)
    for core in range(8):
        b, hf = core // 2, core % 2
        out[b, hf * 2048:(hf + 1) * 2048] = res.results[core]["out"]
    return out
